# version 08
# speedup vs baseline: 1.4520x; 1.0743x over previous
import contextlib
import numpy as np
import ml_dtypes
import concourse.bass as bass
import concourse.mybir as mybir
from concourse.bass_utils import run_bass_kernel_spmd

F32 = mybir.dt.float32
BF16 = mybir.dt.bfloat16
I32 = mybir.dt.int32
AF = mybir.ActivationFunctionType
ALU = mybir.AluOpType
AX = mybir.AxisListType

PI = float(np.pi)
GELU_C = float(np.sqrt(2.0 / np.pi))
NT = 2048
NTILE = 16
D = 1024
BF = ml_dtypes.bfloat16


class P:
    def __init__(self):
        self.nc = bass.Bass("TRN2", target_bir_lowering=False, debug=False)
        self.st = contextlib.ExitStack()
        self.sems = {}
        self.cnt = {}

    def sem(self, name):
        if name not in self.sems:
            self.sems[name] = self.st.enter_context(self.nc.semaphore(name))
            self.cnt[name] = 0
        return self.sems[name]

    def sb(self, name, shape, dt):
        return self.st.enter_context(self.nc.sbuf_tensor(name, shape, dt))

    def ps(self, name, shape, dt=F32):
        return self.st.enter_context(self.nc.psum_tensor(name, shape, dt))

    def inc(self, name, instr, n=None):
        if n is None:
            n = 16 if name.startswith("d") else 1
        if name.startswith("d") and self.cnt[name] > 0:
            instr._wait_ge(self.sems[name], self.cnt[name])
        self.cnt[name] += n
        instr.then_inc(self.sems[name], n)
        return instr

    def add_consts(self, block, values):
        self.cpool = self.sb("cpool", [128, max(1, len(values))], F32)
        self.consts = {}
        pool = self.cpool
        pp = self

        @block.vector
        def _(v):
            for i, val in enumerate(values):
                pp.consts[float(val)] = pool[:, i:i + 1]
                v.memset(pool[:, i:i + 1], float(val))
            v.drain()
            pp.inc("eV", v.nop())

    def const(self, val):
        return self.consts[float(val)]

    def w(self, eng, name, val=None):
        v = self.cnt[name] if val is None else val
        if v > 0:
            eng.wait_ge(self.sems[name], v)


def fence(p, block, engines="vstgy"):
    for e in engines:
        p.sem({"v": "eV", "s": "eS", "t": "eT", "g": "eG", "y": "eY"}[e])
    tgt = {e: p.cnt[{"v": "eV", "s": "eS", "t": "eT", "g": "eG", "y": "eY"}[e]] + 1
           for e in engines}
    sn = {"v": "eV", "s": "eS", "t": "eT", "g": "eG", "y": "eY"}

    def body(e, eng):
        if e == "t":
            eng.wait_ge(p.sems["eT"], p.cnt["eT"])
        for dn in ("dY", "dG"):
            if dn in p.cnt and p.cnt[dn] > 0:
                eng.wait_ge(p.sems[dn], p.cnt[dn])
        ins = eng.drain() if e in "vsg" else eng.nop()
        p.inc(sn[e], ins)
        for o in engines:
            if o != e:
                eng.wait_ge(p.sems[sn[o]], tgt[o])

    if "v" in engines:
        @block.vector
        def _(v):
            body("v", v)
    if "s" in engines:
        @block.scalar
        def _(s):
            body("s", s)
    if "t" in engines:
        @block.tensor
        def _(t):
            body("t", t)
    if "g" in engines:
        @block.gpsimd
        def _(g):
            body("g", g)
    if "y" in engines:
        @block.sync
        def _(y):
            body("y", y)


def ln_sched(p, block, ntile, get_x, get_out, scr, eps, sub_mean=True, gain_bcast=None,
             wait_fn=None, post_fn=None):
    sV0 = p.cnt["eV"]
    sS0 = p.cnt["eS"]

    @block.scalar
    def _(s):
        for t in range(ntile):
            par = t % 2
            if wait_fn is not None:
                wait_fn(s, t)
            if t >= 2:
                s.wait_ge(p.sems["eV"], sV0 + 2 * (t - 1))
            s.activation(scr["sq"][par], get_x(t), AF.Square, accum_out=scr["vv"][par])
            s.drain()
            s.nop().then_inc(p.sems["eS"], 1)
            s.wait_ge(p.sems["eV"], sV0 + 2 * t + 1)
            s.activation(scr["vv"][par], scr["vv"][par], AF.Ln, bias=eps)
            s.drain()
            s.activation(scr["r"][par], scr["vv"][par], AF.Exp, scale=-0.5)
            s.drain()
            s.nop().then_inc(p.sems["eS"], 1)

    @block.vector
    def _(v):
        for t in range(ntile):
            par = t % 2
            if wait_fn is not None:
                wait_fn(v, t)
            if sub_mean:
                v.reduce_sum(scr["m"][par], get_x(t), AX.X)
                v.drain()
                v.tensor_scalar(scr["m"][par], scr["m"][par], 1.0 / D, None, ALU.mult)
                v.drain()
            v.wait_ge(p.sems["eS"], sS0 + 2 * t + 1)
            v.tensor_scalar(scr["vv"][par], scr["vv"][par], 1.0 / D, None, ALU.mult)
            v.drain()
            if sub_mean:
                v.tensor_mul(scr["r"][par], scr["m"][par], scr["m"][par])
                v.drain()
                v.tensor_sub(scr["vv"][par], scr["vv"][par], scr["r"][par])
                v.drain()
            v.nop().then_inc(p.sems["eV"], 1)
            v.wait_ge(p.sems["eS"], sS0 + 2 * t + 2)
            if sub_mean:
                v.tensor_scalar(get_out(t), get_x(t), scr["m"][par], scr["r"][par],
                                ALU.subtract, ALU.mult)
            else:
                v.tensor_scalar(get_out(t), get_x(t), scr["r"][par], None, ALU.mult)
            v.drain()
            if gain_bcast is not None:
                v.tensor_mul(get_out(t), get_out(t), gain_bcast)
                v.drain()
            v.nop().then_inc(p.sems["eV"], 1)
            if post_fn is not None:
                post_fn(v, t)

    p.cnt["eS"] = sS0 + 2 * ntile
    p.cnt["eV"] = sV0 + 2 * ntile


def tp_sched(p, block, jobs, ps_bf, id_b):
    sT0 = p.cnt["eT"]
    sV0 = p.cnt["eV"]
    groups = [jobs[i:i + 8] for i in range(0, len(jobs), 8)]

    @block.tensor
    def _(t):
        t.wait_ge(p.sems["eV"], sV0)
        for gi, g in enumerate(groups):
            par = gi % 2
            if gi >= 2:
                t.wait_ge(p.sems["eV"], sV0 + gi - 1)
            for j, (src, dst) in enumerate(g):
                if j == len(g) - 1:
                    t.wait_ge(p.sems["eT"], sT0 + gi)
                ins = t.transpose(ps_bf[:, par * 1024 + 128 * j:par * 1024 + 128 * (j + 1)],
                                  src, id_b)
            ins.then_inc(p.sems["eT"], 1)

    @block.vector
    def _(v):
        for gi, g in enumerate(groups):
            par = gi % 2
            v.wait_ge(p.sems["eT"], sT0 + gi + 1)
            for j, (src, dst) in enumerate(g):
                v.tensor_copy(dst, ps_bf[:, par * 1024 + 128 * j:par * 1024 + 128 * (j + 1)])
            v.drain()
            v.nop().then_inc(p.sems["eV"], 1)

    p.cnt["eT"] = sT0 + len(groups)
    p.cnt["eV"] = sV0 + len(groups)


def build_A():
    p = P()
    nc = p.nc
    xs = nc.dram_tensor("xs", [NT, D], F32, kind="ExternalInput").ap()
    wqk = nc.dram_tensor("wqk", [D, 32], F32, kind="ExternalInput").ap()
    wv = nc.dram_tensor("wv", [D, D], F32, kind="ExternalInput").ap()
    iden = nc.dram_tensor("iden", [128, 128], F32, kind="ExternalInput").ap()
    trig_o = nc.dram_tensor("trig", [NT, 64], F32, kind="ExternalOutput").ap()
    vtok_o = nc.dram_tensor("vtok", [NT, D], BF16, kind="ExternalOutput").ap()
    tot_o = nc.dram_tensor("tot", [1, 2080], F32, kind="ExternalOutput").ap()

    for s_ in ("dY", "dG", "eV", "eS", "eT", "eG", "eY"):
        p.sem(s_)

    x_sb = p.sb("x_sb", [128, NTILE * D], F32)
    ln_bf = p.sb("ln_bf", [128, NTILE * D], BF16)
    lfm = p.sb("lfm", [128, 8 * NT], BF16)
    v_sb = p.sb("v_sb", [128, NTILE * D], BF16)
    trig_sb = p.sb("trig_sb", [128, NTILE * 64], F32)
    qk_sb = p.sb("qk_sb", [128, 32], F32)
    cqa = p.sb("cqa", [128, 32], F32)
    pch = p.sb("pch", [128, 2080], F32)
    id_f = p.sb("id_f", [128, 128], F32)
    id_b = p.sb("id_b", [128, 128], BF16)
    ones_col = p.sb("ones_col", [128, 1], F32)
    tot_sb = p.sb("tot_sb", [1, 2080], F32)
    scr = {"sq": [p.sb("ssq0", [128, D], F32)[:], p.sb("ssq1", [128, D], F32)[:]],
           "m": [p.sb("sm0", [128, 1], F32)[:], p.sb("sm1", [128, 1], F32)[:]],
           "vv": [p.sb("sv0", [128, 1], F32)[:], p.sb("sv1", [128, 1], F32)[:]],
           "r": [p.sb("sr0", [128, 1], F32)[:], p.sb("sr1", [128, 1], F32)[:]]}
    wqk_sb = p.sb("wqk_sb", [128, 8 * 32], BF16)
    wv_sb = p.sb("wv_sb", [128, 8 * D], BF16)

    ps_tr = p.ps("ps_tr", [128, 2048], BF16)

    with p.st:
        with nc.Block() as block:
            p.add_consts(block, [1e-5])

            @block.sync
            def _(y):
                p.inc("dY", y.dma_start(out=x_sb[:].rearrange("p (t d) -> p t d", t=NTILE),
                                        in_=xs.rearrange("(t p) d -> p t d", p=128)))
                p.inc("dY", y.dma_start(out=id_f[:], in_=iden))

            @block.gpsimd
            def _(g):
                p.inc("dG", g.dma_start(
                    out=wqk_sb[:].rearrange("p (c n) -> p c n", c=8),
                    in_=wqk.rearrange("(c p) n -> p c n", p=128)))
                p.inc("dG", g.dma_start(
                    out=wv_sb[:].rearrange("p (c n) -> p c n", c=8),
                    in_=wv.rearrange("(c p) n -> p c n", p=128)))

            @block.vector
            def _(v):
                p.w(v, "dY")
                v.tensor_copy(id_b[:], id_f[:])
                v.memset(ones_col[:], 1.0)
                v.drain()
                p.inc("eV", v.nop())

            def getx(t):
                return x_sb[:, t * D:(t + 1) * D]

            def getln(t):
                return ln_bf[:, t * D:(t + 1) * D]

            def wait_in(eng, t):
                if t == 0:
                    p.w(eng, "dY")
                    p.w(eng, "dG")

            ln_sched(p, block, NTILE, getx, getln, scr, p.const(1e-5), wait_fn=wait_in)
            fence(p, block, "vst")

            jobs = []
            for t in range(NTILE):
                for c in range(8):
                    jobs.append((ln_bf[:, t * D + 128 * c: t * D + 128 * (c + 1)],
                                 lfm[:, c * NT + 128 * t: c * NT + 128 * (t + 1)]))
            tp_sched(p, block, jobs, ps_tr, id_b[:])
            fence(p, block, "vt")

            ph3 = contextlib.ExitStack()
            ps_qk = ph3.enter_context(nc.psum_tensor("ps_qk", [128, 32], F32))
            ps_v = ph3.enter_context(nc.psum_tensor("ps_v", [128, 1024], F32))
            sT0 = p.cnt["eT"]
            sV0 = p.cnt["eV"]
            sS0 = p.cnt["eS"]

            @block.tensor
            def _(t_):
                for t in range(NTILE):
                    if t >= 1:
                        t_.wait_ge(p.sems["eS"], sS0 + 2 * t - 1)
                        t_.wait_ge(p.sems["eV"], sV0 + 2 * t)
                    for c in range(8):
                        if c == 7:
                            t_.wait_ge(p.sems["eT"], sT0 + 2 * t)
                        ins = t_.matmul(ps_qk[:],
                                        lfm[:, c * NT + 128 * t: c * NT + 128 * (t + 1)],
                                        wqk_sb[:, c * 32:(c + 1) * 32],
                                        start=(c == 0), stop=(c == 7))
                    ins.then_inc(p.sems["eT"], 1)
                    for n in range(2):
                        for c in range(8):
                            if n == 1 and c == 7:
                                t_.wait_ge(p.sems["eT"], sT0 + 2 * t + 1)
                            ins = t_.matmul(ps_v[:, 512 * n:512 * (n + 1)],
                                            lfm[:, c * NT + 128 * t: c * NT + 128 * (t + 1)],
                                            wv_sb[:, c * D + 512 * n: c * D + 512 * n + 512],
                                            start=(c == 0), stop=(c == 7))
                    ins.then_inc(p.sems["eT"], 1)

            @block.scalar
            def _(s):
                for t in range(NTILE):
                    s.wait_ge(p.sems["eT"], sT0 + 2 * t + 1)
                    if t >= 1:
                        s.wait_ge(p.sems["eV"], sV0 + 2 * t - 1)
                    s.activation(qk_sb[:], ps_qk[:], AF.Tanh)
                    s.drain()
                    s.nop().then_inc(p.sems["eS"], 1)
                    s.wait_ge(p.sems["eV"], sV0 + 2 * t + 1)
                    tr = trig_sb[:, 64 * t:64 * (t + 1)]
                    s.activation(tr[:, 0:16], cqa[:, 0:16], AF.Sin)
                    s.activation(tr[:, 16:32], qk_sb[:, 0:16], AF.Sin, scale=PI / 4)
                    s.activation(tr[:, 32:48], cqa[:, 16:32], AF.Sin)
                    s.activation(tr[:, 48:64], qk_sb[:, 16:32], AF.Sin, scale=PI / 4)
                    s.drain()
                    s.nop().then_inc(p.sems["eS"], 1)

            @block.vector
            def _(v):
                for t in range(NTILE):
                    v.wait_ge(p.sems["eS"], sS0 + 2 * t + 1)
                    v.tensor_scalar(cqa[:], qk_sb[:], PI / 4, PI / 2, ALU.mult, ALU.add)
                    v.drain()
                    v.nop().then_inc(p.sems["eV"], 1)
                    v.wait_ge(p.sems["eT"], sT0 + 2 * t + 2)
                    v.tensor_copy(v_sb[:, t * D:(t + 1) * D], ps_v[:])
                    v.drain()
                    v.nop().then_inc(p.sems["eV"], 1)

            p.cnt["eT"] = sT0 + 2 * NTILE
            p.cnt["eS"] = sS0 + 2 * NTILE
            p.cnt["eV"] = sV0 + 2 * NTILE
            fence(p, block, "vst")
            ph3.close()

            ps_tot = p.ps("ps_tot", [1, 512 * 4])
            ps_tot2 = p.ps("ps_tot2", [1, 32])
            sV0 = p.cnt["eV"]
            sT0 = p.cnt["eT"]

            @block.vector
            def _(v):
                for t in range(NTILE):
                    if t >= 1:
                        v.wait_ge(p.sems["eT"], sT0 + t)
                    tr = trig_sb[:, 64 * t:64 * (t + 1)]
                    vt = v_sb[:, t * D:(t + 1) * D]
                    for h in range(16):
                        v.tensor_scalar(pch[:, 64 * h:64 * (h + 1)],
                                        vt[:, 64 * h:64 * (h + 1)],
                                        tr[:, 32 + h:33 + h], None, ALU.mult)
                        v.tensor_scalar(pch[:, 1024 + 64 * h:1024 + 64 * (h + 1)],
                                        vt[:, 64 * h:64 * (h + 1)],
                                        tr[:, 48 + h:49 + h], None, ALU.mult)
                    v.tensor_copy(pch[:, 2048:2064], tr[:, 32:48])
                    v.tensor_copy(pch[:, 2064:2080], tr[:, 48:64])
                    v.drain()
                    v.nop().then_inc(p.sems["eV"], 1)

            @block.tensor
            def _(t_):
                for t in range(NTILE):
                    t_.wait_ge(p.sems["eV"], sV0 + t + 1)
                    for j in range(4):
                        t_.matmul(ps_tot[0:1, 512 * j:512 * (j + 1)], ones_col[:],
                                  pch[:, 512 * j:512 * (j + 1)],
                                  start=(t == 0), stop=(t == NTILE - 1))
                    t_.wait_ge(p.sems["eT"], sT0 + t)
                    t_.matmul(ps_tot2[0:1, :], ones_col[:], pch[:, 2048:2080],
                              start=(t == 0), stop=(t == NTILE - 1)).then_inc(p.sems["eT"], 1)

            p.cnt["eV"] = sV0 + NTILE
            p.cnt["eT"] = sT0 + NTILE
            fence(p, block, "vt")

            @block.vector
            def _(v):
                v.tensor_copy(tot_sb[0:1, 0:2048], ps_tot[0:1, :])
                v.tensor_copy(tot_sb[0:1, 2048:2080], ps_tot2[0:1, :])
                v.drain()
                p.inc("eV", v.nop())

            @block.gpsimd
            def _(g):
                p.w(g, "eV")
                p.w(g, "eS")
                p.inc("dG", g.dma_start(
                    out=trig_o.rearrange("(t p) n -> p t n", p=128),
                    in_=trig_sb[:].rearrange("p (t n) -> p t n", t=NTILE)))
                p.inc("dG", g.dma_start(
                    out=vtok_o.rearrange("(t p) n -> p t n", p=128),
                    in_=v_sb[:].rearrange("p (t n) -> p t n", t=NTILE)))
                p.inc("dG", g.dma_start(out=tot_o, in_=tot_sb[:]))
                p.w(g, "dG")
    return p.nc


def kernel(**inputs):
    raise NotImplementedError("under construction")

def seq_ln(p, block, x_ap, out_ap, scr, eps_ap, sub_mean=True, gain_bcast=None):
    cV = p.cnt["eV"]
    cY = p.cnt.get("dY", 0)
    cG = p.cnt.get("dG", 0)

    def wf(eng, t):
        eng.wait_ge(p.sems["eV"], cV)
        if cY > 0:
            eng.wait_ge(p.sems["dY"], cY)
        if cG > 0:
            eng.wait_ge(p.sems["dG"], cG)

    ln_sched(p, block, 1, lambda t: x_ap, lambda t: out_ap, scr, eps_ap,
             sub_mean=sub_mean, gain_bcast=gain_bcast, wait_fn=wf)


def build_B():
    p = P()
    nc = p.nc
    xs = nc.dram_tensor("xs", [NT, D], F32, kind="ExternalInput").ap()
    trig_i = nc.dram_tensor("trig", [NT, 64], F32, kind="ExternalInput").ap()
    vtok_i = nc.dram_tensor("vtok", [NT, D], BF16, kind="ExternalInput").ap()
    totp = nc.dram_tensor("totp", [1, 2080], F32, kind="ExternalInput").ap()
    tric = nc.dram_tensor("tric", [128, 128], F32, kind="ExternalInput").ap()
    iden = nc.dram_tensor("iden", [128, 128], F32, kind="ExternalInput").ap()
    mo = nc.dram_tensor("mo", [D, D], F32, kind="ExternalInput").ap()
    u1 = nc.dram_tensor("u1", [D, 256], F32, kind="ExternalInput").ap()
    v1 = nc.dram_tensor("v1", [256, 4096], F32, kind="ExternalInput").ap()
    b1 = nc.dram_tensor("b1", [128, 32], F32, kind="ExternalInput").ap()
    u2 = nc.dram_tensor("u2", [4096, 256], F32, kind="ExternalInput").ap()
    v2 = nc.dram_tensor("v2", [256, D], F32, kind="ExternalInput").ap()
    b2 = nc.dram_tensor("b2", [1, D], F32, kind="ExternalInput").ap()
    x2_o = nc.dram_tensor("x2", [NT, D], F32, kind="ExternalOutput").ap()
    outg_o = nc.dram_tensor("outg", [NT, D], BF16, kind="ExternalOutput").ap()
    onfm_o = nc.dram_tensor("onfm", [D, NT], BF16, kind="ExternalOutput").ap()

    for s_ in ("dY", "dG", "eV", "eS", "eT", "eG", "eY"):
        p.sem(s_)

    x_sb = p.sb("x_sb", [128, NTILE * D], F32)
    id_f = p.sb("id_f", [128, 128], F32)
    id_b = p.sb("id_b", [128, 128], BF16)
    tri_f = p.sb("tri_f", [128, 128], F32)
    ones_col = p.sb("ones_col", [128, 1], F32)
    ones_row = p.sb("ones_row", [1, 128], F32)
    scr = {"sq": [p.sb("ssq0", [128, D], F32)[:], p.sb("ssq1", [128, D], F32)[:]],
           "m": [p.sb("sm0", [128, 1], F32)[:], p.sb("sm1", [128, 1], F32)[:]],
           "vv": [p.sb("sv0", [128, 1], F32)[:], p.sb("sv1", [128, 1], F32)[:]],
           "r": [p.sb("sr0", [128, 1], F32)[:], p.sb("sr1", [128, 1], F32)[:]]}

    with p.st:
        with nc.Block() as block:
            p.add_consts(block, [1e-5, 1e-6])

            al = contextlib.ExitStack()
            vtok_sb = al.enter_context(nc.sbuf_tensor("vtok_sb", [128, NTILE * D], BF16))
            trig_sb = al.enter_context(nc.sbuf_tensor("trig_sb", [128, NTILE * 64], F32))
            pch = al.enter_context(nc.sbuf_tensor("pch", [128, 2080], F32))
            cum_sb = al.enter_context(nc.sbuf_tensor("cum_sb", [128, 2080], F32))
            carry = al.enter_context(nc.sbuf_tensor("carry", [1, 2080], F32))
            cnew = al.enter_context(nc.sbuf_tensor("cnew", [1, 2080], F32))
            heads = al.enter_context(nc.sbuf_tensor("heads", [128, D], F32))
            heads_b = al.enter_context(nc.sbuf_tensor("heads_b", [128, D], BF16))
            hfm = al.enter_context(nc.sbuf_tensor("hfm", [128, D], BF16))
            dnm = al.enter_context(nc.sbuf_tensor("dnm", [128, 16], F32))
            dn2 = al.enter_context(nc.sbuf_tensor("dn2", [128, 16], F32))
            attn = al.enter_context(nc.sbuf_tensor("attn", [128, D], F32))
            yt = al.enter_context(nc.sbuf_tensor("yt", [128, D], F32))
            mo_sb = al.enter_context(nc.sbuf_tensor("mo_sb", [128, 8 * D], BF16))
            ps_cum = al.enter_context(nc.psum_tensor("ps_cum", [128, 1024], F32))
            ps_car = al.enter_context(nc.psum_tensor("ps_car", [1, 1024], F32))
            ps_tr = al.enter_context(nc.psum_tensor("ps_tr", [128, 1024], BF16))
            ps_at = al.enter_context(nc.psum_tensor("ps_at", [128, 1024], F32))

            @block.sync
            def _(y):
                p.inc("dY", y.dma_start(out=x_sb[:].rearrange("p (t d) -> p t d", t=NTILE),
                                        in_=xs.rearrange("(t p) d -> p t d", p=128)))
                p.inc("dY", y.dma_start(out=trig_sb[:].rearrange("p (t n) -> p t n", t=NTILE),
                                        in_=trig_i.rearrange("(t p) n -> p t n", p=128)))
                p.inc("dY", y.dma_start(out=vtok_sb[:].rearrange("p (t n) -> p t n", t=NTILE),
                                        in_=vtok_i.rearrange("(t p) n -> p t n", p=128)))
                p.inc("dY", y.dma_start(out=id_f[:], in_=iden))
                p.inc("dY", y.dma_start(out=tri_f[:], in_=tric))
                p.inc("dY", y.dma_start(out=carry[:], in_=totp))

            @block.gpsimd
            def _(g):
                p.inc("dG", g.dma_start(
                    out=mo_sb[:].rearrange("p (c n) -> p c n", c=8),
                    in_=mo.rearrange("(c p) n -> p c n", p=128)))

            @block.vector
            def _(v):
                p.w(v, "dY")
                p.w(v, "dG")
                v.tensor_copy(id_b[:], id_f[:])
                v.memset(ones_col[:], 1.0)
                v.memset(ones_row[:], 1.0)
                v.drain()
                p.inc("eV", v.nop())
            fence(p, block)

            ROUNDS = [(0, 1024), (1024, 1024), (2048, 32)]
            for t in range(NTILE):
                tr = trig_sb[:, 64 * t:64 * (t + 1)]
                vt = vtok_sb[:, t * D:(t + 1) * D]

                @block.vector
                def _(v, tr=tr, vt=vt):
                    p.w(v, "eT")
                    for h in range(16):
                        v.tensor_scalar(pch[:, 64 * h:64 * (h + 1)], vt[:, 64 * h:64 * (h + 1)],
                                        tr[:, 32 + h:33 + h], None, ALU.mult)
                        v.tensor_scalar(pch[:, 1024 + 64 * h:1024 + 64 * (h + 1)],
                                        vt[:, 64 * h:64 * (h + 1)],
                                        tr[:, 48 + h:49 + h], None, ALU.mult)
                    v.tensor_copy(pch[:, 2048:2064], tr[:, 32:48])
                    v.tensor_copy(pch[:, 2064:2080], tr[:, 48:64])
                    v.drain()
                    p.inc("eV", v.nop())

                for (lo, wdt) in ROUNDS:
                    @block.tensor
                    def _(t_, lo=lo, wdt=wdt):
                        p.w(t_, "eV")
                        nch = (wdt + 511) // 512
                        for j in range(nch):
                            w_ = min(512, wdt - 512 * j)
                            t_.matmul(ps_cum[:, 512 * j:512 * j + w_], tri_f[:],
                                      pch[:, lo + 512 * j: lo + 512 * j + w_],
                                      start=True, stop=False)
                            t_.matmul(ps_cum[:, 512 * j:512 * j + w_], ones_row[:],
                                      carry[0:1, lo + 512 * j: lo + 512 * j + w_],
                                      start=False, stop=True)
                        for j in range(nch):
                            w_ = min(512, wdt - 512 * j)
                            if j == nch - 1:
                                t_.wait_ge(p.sems["eT"], p.cnt["eT"])
                            ins = t_.matmul(ps_car[0:1, 512 * j:512 * j + w_], ones_col[:],
                                            pch[:, lo + 512 * j: lo + 512 * j + w_],
                                            start=True, stop=True)
                        p.inc("eT", ins)

                    @block.vector
                    def _(v, lo=lo, wdt=wdt):
                        p.w(v, "eT")
                        v.tensor_copy(cum_sb[:, lo:lo + wdt], ps_cum[:, 0:wdt])
                        v.tensor_add(cnew[0:1, lo:lo + wdt], carry[0:1, lo:lo + wdt],
                                     ps_car[0:1, 0:wdt])
                        v.drain()
                        p.inc("eV", v.nop())

                @block.vector
                def _(v, tr=tr, t=t):
                    v.tensor_copy(carry[:], cnew[:])
                    v.drain()
                    v.tensor_mul(dnm[:], tr[:, 0:16], cum_sb[:, 2048:2064])
                    v.tensor_mul(dn2[:], tr[:, 16:32], cum_sb[:, 2064:2080])
                    v.drain()
                    v.tensor_add(dnm[:], dnm[:], dn2[:])
                    v.drain()
                    v.tensor_scalar(dnm[:], dnm[:], 1e-6, None, ALU.add)
                    v.drain()
                    v.reciprocal(dn2[:], dnm[:])
                    v.drain()
                    for h in range(16):
                        hc = heads[:, 64 * h:64 * (h + 1)]
                        v.tensor_scalar(hc, cum_sb[:, 64 * h:64 * (h + 1)],
                                        tr[:, 0 + h:1 + h], None, ALU.mult)
                        v.drain()
                        v.scalar_tensor_tensor(hc, cum_sb[:, 1024 + 64 * h:1024 + 64 * (h + 1)],
                                               tr[:, 16 + h:17 + h], hc, ALU.mult, ALU.add)
                        v.drain()
                        v.tensor_scalar(heads_b[:, 64 * h:64 * (h + 1)], hc,
                                        dn2[:, h:h + 1], None, ALU.mult)
                    v.drain()
                    p.inc("eV", v.nop())

                jobs = [(heads_b[:, 128 * c:128 * (c + 1)], hfm[:, 128 * c:128 * (c + 1)])
                        for c in range(8)]
                tp_sched(p, block, jobs, ps_tr, id_b[:])

                @block.tensor
                def _(t_):
                    p.w(t_, "eV")
                    for n in range(2):
                        for c in range(8):
                            if n == 1 and c == 7:
                                t_.wait_ge(p.sems["eT"], p.cnt["eT"])
                            ins = t_.matmul(ps_at[:, 512 * n:512 * (n + 1)],
                                            hfm[:, 128 * c:128 * (c + 1)],
                                            mo_sb[:, c * D + 512 * n:c * D + 512 * n + 512],
                                            start=(c == 0), stop=(c == 7))
                    p.inc("eT", ins)

                @block.vector
                def _(v):
                    p.w(v, "eT")
                    v.tensor_copy(attn[:], ps_at[:])
                    v.drain()
                    p.inc("eV", v.nop())

                seq_ln(p, block, attn[:], yt[:], scr, p.const(1e-5))

                @block.vector
                def _(v, t=t):
                    p.w(v, "eS")
                    xa = x_sb[:, t * D:(t + 1) * D]
                    v.tensor_add(xa, xa, yt[:])
                    v.drain()
                    p.inc("eV", v.nop())

            fence(p, block)
            al.close()

            be = contextlib.ExitStack()
            u1_sb = be.enter_context(nc.sbuf_tensor("u1_sb", [128, 8 * 256], BF16))
            v1_sb = be.enter_context(nc.sbuf_tensor("v1_sb", [128, 2 * 4096], BF16))
            u2_sb = be.enter_context(nc.sbuf_tensor("u2_sb", [128, 32 * 256], BF16))
            v2_sb = be.enter_context(nc.sbuf_tensor("v2_sb", [128, 2 * D], BF16))
            b1_sb = be.enter_context(nc.sbuf_tensor("b1_sb", [128, 32], F32))
            b2b = be.enter_context(nc.sbuf_tensor("b2b", [128, D], F32))
            b2r = be.enter_context(nc.sbuf_tensor("b2r", [1, D], F32))
            lnt = be.enter_context(nc.sbuf_tensor("lnt", [128, 512 * 8], BF16))
            hfm2 = be.enter_context(nc.sbuf_tensor("hfm2", [128, 32 * 512], BF16))
            pfm = be.enter_context(nc.sbuf_tensor("pfm", [128, 2 * 512], BF16))
            qfm = be.enter_context(nc.sbuf_tensor("qfm", [128, 2 * 512], BF16))
            g0 = be.enter_context(nc.sbuf_tensor("g0", [128, 1024], F32))
            g1s = be.enter_context(nc.sbuf_tensor("g1s", [128, 1024], F32))
            g2s = be.enter_context(nc.sbuf_tensor("g2s", [128, 1024], F32))
            lnb = be.enter_context(nc.sbuf_tensor("lnb", [128, 4 * D], BF16))
            ot = be.enter_context(nc.sbuf_tensor("ot", [128, 4 * D], F32))
            ps_b = be.enter_context(nc.psum_tensor("ps_b", [128, 1024], F32))
            ps_h = be.enter_context(nc.psum_tensor("ps_h", [128, 1024], F32))
            ps_t2 = be.enter_context(nc.psum_tensor("ps_t2", [128, 2048], BF16))
            yt2 = be.enter_context(nc.sbuf_tensor("yt2", [128, D], F32))

            @block.gpsimd
            def _(g):
                p.inc("dG", g.dma_start(out=u1_sb[:].rearrange("p (c n) -> p c n", c=8),
                                        in_=u1.rearrange("(c p) n -> p c n", p=128)))
                p.inc("dG", g.dma_start(out=v1_sb[:].rearrange("p (c n) -> p c n", c=2),
                                        in_=v1.rearrange("(c p) n -> p c n", p=128)))
                p.inc("dG", g.dma_start(out=u2_sb[:].rearrange("p (c n) -> p c n", c=32),
                                        in_=u2.rearrange("(c p) n -> p c n", p=128)))
                p.inc("dG", g.dma_start(out=v2_sb[:].rearrange("p (c n) -> p c n", c=2),
                                        in_=v2.rearrange("(c p) n -> p c n", p=128)))

            @block.sync
            def _(y):
                p.inc("dY", y.dma_start(out=b1_sb[:], in_=b1))
                p.inc("dY", y.dma_start(out=b2r[:], in_=b2))

            @block.tensor
            def _(t_):
                p.w(t_, "dY")
                t_.wait_ge(p.sems["eT"], p.cnt["eT"])
                p.inc("eT", t_.matmul(ps_b[:, 0:512], ones_row[:], b2r[0:1, 0:512],
                                      start=True, stop=True))
                t_.wait_ge(p.sems["eT"], p.cnt["eT"])
                p.inc("eT", t_.matmul(ps_b[:, 512:1024], ones_row[:], b2r[0:1, 512:1024],
                                      start=True, stop=True))

            @block.vector
            def _(v):
                p.w(v, "eT")
                v.tensor_copy(b2b[:], ps_b[:])
                v.drain()
                p.inc("eV", v.nop())
            fence(p, block)

            for gtile in range(4):
                for u in range(4):
                    t128 = 4 * gtile + u
                    seq_ln(p, block, x_sb[:, t128 * D:(t128 + 1) * D],
                           lnb[:, u * D:(u + 1) * D], scr, p.const(1e-5))
                jobs = []
                for u in range(4):
                    for c in range(8):
                        jobs.append((lnb[:, u * D + 128 * c:u * D + 128 * (c + 1)],
                                     lnt[:, c * 512 + 128 * u:c * 512 + 128 * (u + 1)]))
                tp_sched(p, block, jobs, ps_t2, id_b[:])

                @block.tensor
                def _(t_):
                    p.w(t_, "eV")
                    for rc in range(2):
                        for c in range(8):
                            if rc == 1 and c == 7:
                                t_.wait_ge(p.sems["eT"], p.cnt["eT"])
                            ins = t_.matmul(ps_b[:, 512 * rc:512 * (rc + 1)],
                                            u1_sb[:, c * 256 + 128 * rc:c * 256 + 128 * (rc + 1)],
                                            lnt[:, c * 512:(c + 1) * 512],
                                            start=(c == 0), stop=(c == 7))
                    p.inc("eT", ins)

                @block.vector
                def _(v):
                    p.w(v, "eT")
                    v.tensor_copy(pfm[:], ps_b[:])
                    v.drain()
                    p.inc("eV", v.nop())

                h_t = {}
                h_v1 = {}
                for j in range(32):
                    par = j % 2

                    @block.tensor
                    def _(t_, j=j, par=par):
                        if j == 0:
                            p.w(t_, "eV")
                        if j >= 2:
                            t_.wait_ge(p.sems["eV"], h_v1[j - 2])
                        t_.wait_ge(p.sems["eT"], p.cnt["eT"])
                        for rc in range(2):
                            ins = t_.matmul(ps_h[:, 512 * par:512 * (par + 1)],
                                            v1_sb[:, rc * 4096 + 128 * j:rc * 4096 + 128 * (j + 1)],
                                            pfm[:, 512 * rc:512 * (rc + 1)],
                                            start=(rc == 0), stop=(rc == 1))
                        p.inc("eT", ins)
                        h_t[j] = p.cnt["eT"]

                    @block.vector
                    def _(v, j=j, par=par):
                        v.wait_ge(p.sems["eT"], h_t[j])
                        gp = g0[:, 512 * par:512 * (par + 1)]
                        g1p = g1s[:, 512 * par:512 * (par + 1)]
                        v.tensor_scalar(gp, ps_h[:, 512 * par:512 * (par + 1)],
                                        b1_sb[:, j:j + 1], None, ALU.add)
                        v.drain()
                        v.tensor_mul(g1p, gp, gp)
                        v.drain()
                        v.tensor_scalar(g1p, g1p, 0.044715, 1.0, ALU.mult, ALU.add)
                        v.drain()
                        v.tensor_mul(g1p, g1p, gp)
                        v.drain()
                        p.inc("eV", v.nop())
                        h_v1[j] = p.cnt["eV"]

                    @block.scalar
                    def _(s, j=j, par=par):
                        p.w(s, "eV")
                        s.activation(g2s[:, 512 * par:512 * (par + 1)],
                                     g1s[:, 512 * par:512 * (par + 1)], AF.Tanh, scale=GELU_C)
                        s.drain()
                        p.inc("eS", s.nop())

                    @block.vector
                    def _(v, j=j, par=par):
                        p.w(v, "eS")
                        gp = g0[:, 512 * par:512 * (par + 1)]
                        g2p = g2s[:, 512 * par:512 * (par + 1)]
                        v.tensor_scalar(g2p, g2p, 0.5, 0.5, ALU.mult, ALU.add)
                        v.drain()
                        v.tensor_mul(hfm2[:, 512 * j:512 * (j + 1)], g2p, gp)
                        v.drain()
                        p.inc("eV", v.nop())

                @block.tensor
                def _(t_):
                    p.w(t_, "eV")
                    for rc in range(2):
                        for j in range(32):
                            if rc == 1 and j == 31:
                                t_.wait_ge(p.sems["eT"], p.cnt["eT"])
                            ins = t_.matmul(ps_b[:, 512 * rc:512 * (rc + 1)],
                                            u2_sb[:, j * 256 + 128 * rc:j * 256 + 128 * (rc + 1)],
                                            hfm2[:, 512 * j:512 * (j + 1)],
                                            start=(j == 0), stop=(j == 31))
                    p.inc("eT", ins)

                @block.vector
                def _(v):
                    p.w(v, "eT")
                    v.tensor_copy(qfm[:], ps_b[:])
                    v.drain()
                    p.inc("eV", v.nop())

                for u in range(4):
                    @block.tensor
                    def _(t_, u=u):
                        p.w(t_, "eV")
                        for n in range(2):
                            for rc in range(2):
                                if n == 1 and rc == 1:
                                    t_.wait_ge(p.sems["eT"], p.cnt["eT"])
                                ins = t_.matmul(ps_h[:, 512 * n:512 * (n + 1)],
                                                qfm[:, 512 * rc + 128 * u:512 * rc + 128 * (u + 1)],
                                                v2_sb[:, rc * D + 512 * n:rc * D + 512 * n + 512],
                                                start=(rc == 0), stop=(rc == 1))
                        p.inc("eT", ins)

                    @block.vector
                    def _(v, u=u):
                        p.w(v, "eT")
                        v.tensor_add(ot[:, u * D:(u + 1) * D], ps_h[:], b2b[:])
                        v.drain()
                        p.inc("eV", v.nop())

                for u in range(4):
                    t128 = 4 * gtile + u
                    seq_ln(p, block, ot[:, u * D:(u + 1) * D], yt2[:], scr, p.const(1e-5))

                    @block.vector
                    def _(v, t128=t128):
                        p.w(v, "eS")
                        xa = x_sb[:, t128 * D:(t128 + 1) * D]
                        v.tensor_add(xa, xa, yt2[:])
                        v.drain()
                        p.inc("eV", v.nop())

            fence(p, block)
            be.close()

            ga = contextlib.ExitStack()
            lng = ga.enter_context(nc.sbuf_tensor("lng", [128, D], F32))
            gg0 = ga.enter_context(nc.sbuf_tensor("gg0", [128, D], F32))
            gg1 = ga.enter_context(nc.sbuf_tensor("gg1", [128, D], F32))
            outg_sb = ga.enter_context(nc.sbuf_tensor("outg_sb", [128, NTILE * D], BF16))
            onfm_sb = ga.enter_context(nc.sbuf_tensor("onfm_sb", [128, 8 * NT], BF16))
            ps_tg = ga.enter_context(nc.psum_tensor("ps_tg", [128, 2048], BF16))
            hb2 = ga.enter_context(nc.sbuf_tensor("hb2", [128, D], BF16))

            for t in range(NTILE):
                seq_ln(p, block, x_sb[:, t * D:(t + 1) * D], lng[:], scr, p.const(1e-5))

                @block.vector
                def _(v, t=t):
                    p.w(v, "eS")
                    v.tensor_mul(gg0[:], lng[:], lng[:])
                    v.drain()
                    v.tensor_scalar(gg0[:], gg0[:], 0.044715, 1.0, ALU.mult, ALU.add)
                    v.drain()
                    v.tensor_mul(gg0[:], gg0[:], lng[:])
                    v.drain()
                    p.inc("eV", v.nop())

                @block.scalar
                def _(s, t=t):
                    p.w(s, "eV")
                    s.activation(gg1[:], gg0[:], AF.Tanh, scale=GELU_C)
                    s.drain()
                    p.inc("eS", s.nop())

                @block.vector
                def _(v, t=t):
                    p.w(v, "eS")
                    v.tensor_scalar(gg1[:], gg1[:], 0.5, 0.5, ALU.mult, ALU.add)
                    v.drain()
                    v.tensor_mul(gg0[:], gg1[:], lng[:])
                    v.drain()
                    v.tensor_copy(outg_sb[:, t * D:(t + 1) * D], gg0[:])
                    v.drain()
                    p.inc("eV", v.nop())

                @block.scalar
                def _(s, t=t):
                    p.w(s, "eV")
                    s.activation(scr["sq"][0], gg0[:], AF.Square, accum_out=scr["vv"][0])
                    s.drain()
                    s.activation(scr["vv"][0], scr["vv"][0], AF.Ln)
                    s.drain()
                    s.activation(scr["r"][0], scr["vv"][0], AF.Exp, scale=-0.5)
                    s.drain()
                    p.inc("eS", s.nop())

                @block.vector
                def _(v, t=t):
                    p.w(v, "eS")
                    v.tensor_scalar(scr["r"][0], scr["r"][0], 1e12, None, ALU.min)
                    v.drain()
                    v.tensor_scalar(hb2[:], gg0[:], scr["r"][0], None, ALU.mult)
                    v.drain()
                    p.inc("eV", v.nop())

                jobs = [(hb2[:, 128 * c:128 * (c + 1)],
                         onfm_sb[:, c * NT + 128 * t:c * NT + 128 * (t + 1)])
                        for c in range(8)]
                tp_sched(p, block, jobs, ps_tg, id_b[:])

            fence(p, block)

            @block.gpsimd
            def _(g):
                p.inc("dG", g.dma_start(out=x2_o.rearrange("(t p) d -> p t d", p=128),
                                        in_=x_sb[:].rearrange("p (t d) -> p t d", t=NTILE)))
                p.inc("dG", g.dma_start(out=outg_o.rearrange("(t p) d -> p t d", p=128),
                                        in_=outg_sb[:].rearrange("p (t d) -> p t d", t=NTILE)))
                p.inc("dG", g.dma_start(out=onfm_o.rearrange("(c p) n -> p c n", p=128),
                                        in_=onfm_sb[:].rearrange("p (c n) -> p c n", c=8)))
                p.w(g, "dG")
            ga.close()
    return p.nc


def emit_quant_tile(p, v, wt, wq_out, q_scr, fence_wait=None):
    mx, mn, rng, isc, base = q_scr["mx"], q_scr["mn"], q_scr["rng"], q_scr["isc"], q_scr["base"]
    i1, c1, d1, m1 = q_scr["i1"], q_scr["c1"], q_scr["d1"], q_scr["m1"]
    it, ct, dt, mt = q_scr["it"], q_scr["ct"], q_scr["dt"], q_scr["mt"]
    v.reduce_max(mx, wt, AX.X)
    v.tensor_reduce(mn, wt, AX.X, ALU.min)
    v.drain()
    v.tensor_sub(rng, mx, mn)
    v.drain()
    v.tensor_scalar(rng, rng, 1e-5, 1.0 / 15.0, ALU.max, ALU.mult)
    v.drain()
    v.reciprocal(isc, rng)
    v.drain()
    v.tensor_scalar(c1, mn, isc, -1.0, ALU.mult, ALU.mult)
    v.drain()
    v.tensor_scalar_max(d1, c1, 0.0)
    v.drain()
    v.tensor_copy(i1, d1)
    v.drain()
    v.tensor_copy(c1, i1)
    v.drain()
    v.tensor_sub(m1, d1, c1)
    v.drain()
    v.tensor_tensor(m1, m1, q_scr["half1"], ALU.is_ge)
    v.drain()
    v.tensor_add(base, c1, m1)
    v.drain()
    v.tensor_scalar_min(base, base, 15.0)
    v.drain()
    v.tensor_scalar(ct, wt, isc, base, ALU.mult, ALU.add)
    v.drain()
    v.tensor_scalar_max(ct, ct, 0.0)
    v.drain()
    v.tensor_copy(it, ct)
    v.drain()
    v.tensor_copy(dt, it)
    v.drain()
    v.tensor_sub(mt, ct, dt)
    v.drain()
    v.tensor_tensor(mt, mt, q_scr["halft"], ALU.is_ge)
    v.drain()
    v.tensor_add(dt, dt, mt)
    v.drain()
    v.tensor_scalar_min(dt, dt, 15.0)
    v.drain()
    v.tensor_scalar(wq_out, dt, base, rng, ALU.subtract, ALU.mult)
    v.drain()


def quant_stream(p, block, src_dram, nrows, ncols, dst, dst_is_dram, id_b, q_scr,
                 wld, wq_t, ps_q, f32=False):
    RB = nrows // 128
    CB4 = ncols // 512
    batches = [(rb, c4) for rb in range(RB) for c4 in range(CB4)]
    ld_cnt = {}
    v_cnt = {}
    t_cnt = {}
    g_cnt = {}

    for bi, (rb, c4) in enumerate(batches):
        par = bi % 2
        lda = wld[:, 512 * par:512 * (par + 1)]

        @block.sync
        def _(y, rb=rb, c4=c4, lda=lda, bi=bi):
            if bi >= 2:
                p.w(y, "eV", v_cnt[bi - 2])
            else:
                p.w(y, "eV")
            p.inc("dY", y.dma_start(out=lda,
                                    in_=src_dram[128 * rb:128 * (rb + 1),
                                                 512 * c4:512 * (c4 + 1)]))
            ld_cnt[bi] = p.cnt["dY"]

        @block.vector
        def _(v, lda=lda, bi=bi, par=par):
            v.wait_ge(p.sems["dY"], ld_cnt[bi])
            if bi >= 2:
                p.w(v, "eT", t_cnt[bi - 2])
            for cc in range(4):
                emit_quant_tile(p, v, lda[:, 128 * cc:128 * (cc + 1)],
                                wq_t[:, 512 * par + 128 * cc:512 * par + 128 * (cc + 1)],
                                q_scr)
            v.nop().then_inc(p.sems["eV"], 1)
            p.cnt["eV"] += 1

        @block.tensor
        def _(t_, bi=bi, par=par):
            p.w(t_, "eV")
            t_.wait_ge(p.sems["eT"], p.cnt["eT"])
            for cc in range(4):
                ins = t_.transpose(ps_q[:, 128 * cc:128 * (cc + 1)],
                                   wq_t[:, 512 * par + 128 * cc:512 * par + 128 * (cc + 1)],
                                   id_b)
            p.inc("eT", ins)
            t_cnt[bi] = p.cnt["eT"]

        @block.vector
        def _(v, rb=rb, c4=c4, bi=bi, par=par):
            p.w(v, "eT")
            if dst_is_dram:
                if bi >= 2:
                    p.w(v, "dG", g_cnt[bi - 2])
                else:
                    p.w(v, "dG")
            for cc in range(4):
                c = 4 * c4 + cc
                out_ap = (q_scr["obuf"][:, 512 * par + 128 * cc:512 * par + 128 * (cc + 1)]
                          if dst_is_dram
                          else dst[:, c * nrows + 128 * rb: c * nrows + 128 * (rb + 1)])
                v.tensor_copy(out_ap, ps_q[:, 128 * cc:128 * (cc + 1)])
            v.drain()
            v.nop().then_inc(p.sems["eV"], 1)
            p.cnt["eV"] += 1
            v_cnt[bi] = p.cnt["eV"]

        if dst_is_dram:
            @block.gpsimd
            def _(g, rb=rb, c4=c4, bi=bi, par=par):
                p.w(g, "eV", v_cnt[bi])
                for cc in range(4):
                    c = 4 * c4 + cc
                    p.inc("dG", g.dma_start(
                        out=dst[128 * c:128 * (c + 1), 128 * rb:128 * (rb + 1)],
                        in_=q_scr["obuf"][:, 512 * par + 128 * cc:512 * par + 128 * (cc + 1)]))
                g_cnt[bi] = p.cnt["dG"]

def build_C(half):
    p = P()
    nc = p.nc
    x2_i = nc.dram_tensor("x2", [NT, D], F32, kind="ExternalInput").ap()
    outg_i = nc.dram_tensor("outg", [NT, D], BF16, kind="ExternalInput").ap()
    onfm_i = nc.dram_tensor("onfm", [D, NT], BF16, kind="ExternalInput").ap()
    onfmp_i = nc.dram_tensor("onfmp", [D, NT], BF16, kind="ExternalInput").ap()
    ema8 = nc.dram_tensor("ema8", [128, 8], F32, kind="ExternalInput").ap()
    sc3 = nc.dram_tensor("sc3", [1, 3], F32, kind="ExternalInput").ap()
    tris = nc.dram_tensor("tris", [128, 128], F32, kind="ExternalInput").ap()
    m0 = nc.dram_tensor("m0", [128, 2], F32, kind="ExternalInput").ap()
    poscol = nc.dram_tensor("poscol", [128, 16], F32, kind="ExternalInput").ap()
    g1r = nc.dram_tensor("g1r", [1, D], F32, kind="ExternalInput").ap()
    iden = nc.dram_tensor("iden", [128, 128], F32, kind="ExternalInput").ap()
    wq_i = nc.dram_tensor("wq", [D, D], F32, kind="ExternalInput").ap()
    wk_i = nc.dram_tensor("wk", [256, D], F32, kind="ExternalInput").ap()
    wv_i = nc.dram_tensor("wv", [256, D], F32, kind="ExternalInput").ap()
    wo_i = nc.dram_tensor("wo", [D, D], F32, kind="ExternalInput").ap()
    w1e_i = nc.dram_tensor("w1e", [3584, D], F32, kind="ExternalInput").ap()
    w3e_i = nc.dram_tensor("w3e", [3584, D], F32, kind="ExternalInput").ap()
    w2e_i = nc.dram_tensor("w2e", [D, 3584], F32, kind="ExternalInput").ap()
    x3_o = nc.dram_tensor("x3", [NT, D], F32, kind="ExternalOutput").ap()
    qfm_o = nc.dram_tensor("qfm", [D, NT], BF16, kind="ExternalOutput").ap()
    kv_o = nc.dram_tensor("kv", [NT, 512], BF16, kind="ExternalOutput").ap()
    woqt_o = nc.dram_tensor("woqt", [D, D], BF16, kind="ExternalOutput").ap()
    w1t_o = nc.dram_tensor("w1t", [D, 3584], BF16, kind="ExternalOutput").ap()
    w3t_o = nc.dram_tensor("w3t", [D, 3584], BF16, kind="ExternalOutput").ap()
    w2t_o = nc.dram_tensor("w2t", [3584, D], BF16, kind="ExternalOutput").ap()

    for s_ in ("dY", "dG", "eV", "eS", "eT", "eG", "eY"):
        p.sem(s_)

    x_sb = p.sb("x_sb", [128, NTILE * D], F32)
    id_f = p.sb("id_f", [128, 128], F32)
    id_b = p.sb("id_b", [128, 128], BF16)
    ones_row = p.sb("ones_row", [1, 128], F32)
    ones_col = p.sb("ones_col", [128, 1], F32)
    scr = {"sq": [p.sb("ssq0", [128, D], F32)[:], p.sb("ssq1", [128, D], F32)[:]],
           "m": [p.sb("sm0", [128, 1], F32)[:], p.sb("sm1", [128, 1], F32)[:]],
           "vv": [p.sb("sv0", [128, 1], F32)[:], p.sb("sv1", [128, 1], F32)[:]],
           "r": [p.sb("sr0", [128, 1], F32)[:], p.sb("sr1", [128, 1], F32)[:]]}

    with p.st:
        with nc.Block() as block:
            p.add_consts(block, [1e-5])

            s1 = contextlib.ExitStack()
            onfm_sb = s1.enter_context(nc.sbuf_tensor("onfm_sb", [128, 8 * NT], BF16))
            outg_sb = s1.enter_context(nc.sbuf_tensor("outg_sb", [128, NTILE * D], BF16))
            kst = s1.enter_context(nc.sbuf_tensor("kst", [128, 2 * 8 * 512], BF16))
            runm = s1.enter_context(nc.sbuf_tensor("runm", [128, 16], F32))
            esim = s1.enter_context(nc.sbuf_tensor("esim", [128, 16], F32))
            eman = s1.enter_context(nc.sbuf_tensor("eman", [128, 8], BF16))
            ema_sb = s1.enter_context(nc.sbuf_tensor("ema_sb", [128, 8], F32))
            tris_sb = s1.enter_context(nc.sbuf_tensor("tris_sb", [128, 128], F32))
            m0_sb = s1.enter_context(nc.sbuf_tensor("m0_sb", [128, 2], F32))
            mng2 = s1.enter_context(nc.sbuf_tensor("mng2", [128, 128], F32))
            seldg = s1.enter_context(nc.sbuf_tensor("seldg", [128, 128], F32))
            chm = s1.enter_context(nc.sbuf_tensor("chm", [128, 1], F32))
            chm2 = s1.enter_context(nc.sbuf_tensor("chm2", [128, 1], F32))
            scc = s1.enter_context(nc.sbuf_tensor("scc", [128, 8], F32))
            gcol = s1.enter_context(nc.sbuf_tensor("gcol", [128, 4], F32))
            rt = s1.enter_context(nc.sbuf_tensor("rt", [128, D], F32))
            yt3 = s1.enter_context(nc.sbuf_tensor("yt3", [128, D], F32))
            ps_s = s1.enter_context(nc.psum_tensor("ps_s", [128, 1024], F32))
            ps_e = s1.enter_context(nc.psum_tensor("ps_e", [128, 128], F32))

            @block.sync
            def _(y):
                p.inc("dY", y.dma_start(out=x_sb[:].rearrange("p (t d) -> p t d", t=NTILE),
                                        in_=x2_i.rearrange("(t p) d -> p t d", p=128)))
                p.inc("dY", y.dma_start(out=onfm_sb[:].rearrange("p (c n) -> p c n", c=8),
                                        in_=onfm_i.rearrange("(c p) n -> p c n", p=128)))
                p.inc("dY", y.dma_start(out=outg_sb[:].rearrange("p (t d) -> p t d", t=NTILE),
                                        in_=outg_i.rearrange("(t p) d -> p t d", p=128)))
                p.inc("dY", y.dma_start(out=id_f[:], in_=iden))
                p.inc("dY", y.dma_start(out=tris_sb[:], in_=tris))
                p.inc("dY", y.dma_start(out=m0_sb[:], in_=m0))
                p.inc("dY", y.dma_start(out=ema_sb[:], in_=ema8))
                p.inc("dY", y.dma_start(out=scc[0:1, 0:3], in_=sc3))

            @block.vector
            def _(v):
                p.w(v, "dY")
                v.tensor_copy(id_b[:], id_f[:])
                v.memset(ones_row[:], 1.0)
                v.memset(ones_col[:], 1.0)
                v.memset(runm[:], -2.0)
                v.drain()
                v.tensor_scalar(mng2[:], tris_sb[:], -1.0, 2.0, ALU.add, ALU.mult)
                v.drain()
                p.inc("eV", v.nop())
            fence(p, block)

            @block.tensor
            def _(t_):
                t_.wait_ge(p.sems["eT"], p.cnt["eT"])
                p.inc("eT", t_.matmul(ps_e[:, 0:3], ones_row[:], scc[0:1, 0:3],
                                      start=True, stop=True))

            @block.scalar
            def _(s):
                p.w(s, "eT")
                s.activation(scc[:, 3:4], ps_e[:, 0:1], AF.Exp)
                s.activation(scc[:, 4:5], ps_e[:, 1:2], AF.Sigmoid)
                s.activation(scc[:, 5:6], ps_e[:, 2:3], AF.Sigmoid)
                s.drain()
                s.activation(scr["sq"][0][:, 0:8], ema_sb[:], AF.Square,
                             accum_out=scr["vv"][0])
                s.drain()
                p.inc("eS", s.nop())

            @block.vector
            def _(v):
                p.w(v, "eS")
                v.tensor_scalar(scc[:, 6:7], scc[:, 4:5], -1.0, 1.0, ALU.mult, ALU.add)
                v.drain()
                p.inc("eV", v.nop())

            @block.tensor
            def _(t_):
                p.w(t_, "eS")
                t_.wait_ge(p.sems["eT"], p.cnt["eT"])
                p.inc("eT", t_.matmul(ps_e[0:1, 4:5], ones_col[:], scr["vv"][0],
                                      start=True, stop=True))

            @block.scalar
            def _(s):
                p.w(s, "eT")
                s.activation(scr["m"][0][0:1, :], ps_e[0:1, 4:5], AF.Ln)
                s.drain()
                s.activation(scr["m"][0][0:1, :], scr["m"][0][0:1, :], AF.Exp, scale=-0.5)
                s.drain()
                p.inc("eS", s.nop())

            @block.tensor
            def _(t_):
                p.w(t_, "eS")
                t_.wait_ge(p.sems["eT"], p.cnt["eT"])
                p.inc("eT", t_.matmul(ps_e[:, 5:6], ones_row[:], scr["m"][0][0:1, :],
                                      start=True, stop=True))

            @block.vector
            def _(v):
                p.w(v, "eT")
                v.tensor_scalar(scc[:, 7:8], ps_e[:, 5:6], 1e12, None, ALU.min)
                v.drain()
                v.tensor_scalar(eman[:], ema_sb[:], scc[:, 7:8], None, ALU.mult)
                v.drain()
                p.inc("eV", v.nop())
            fence(p, block)

            NKS = 4 * (half + 1)
            for ks in range(NKS):
                peer = (half == 1 and ks < 4)
                if peer:
                    par = ks % 2
                    @block.sync
                    def _(y, ks=ks, par=par):
                        if ks >= 2:
                            p.w(y, "eT")
                        p.inc("dY", y.dma_start(
                            out=kst[:, par * 4096:(par + 1) * 4096].rearrange("p (c n) -> p c n", c=8),
                            in_=onfmp_i.rearrange("(c p) n -> p c n", p=128)[:, :, 512 * ks:512 * (ks + 1)]))
                qt0 = max(0, 4 * ks - 16 * half)
                for qt in range(qt0, NTILE):
                    qbase = 2048 * half + 128 * qt
                    dj = (qbase - 512 * ks) // 128 if (qbase - 512 * ks) < 512 else None
                    wfull = 512 if dj is None else 128 * dj

                    @block.tensor
                    def _(t_, ks=ks, qt=qt, peer=peer):
                        p.w(t_, "eV")
                        if peer:
                            p.w(t_, "dY")
                        for c in range(8):
                            if c == 7:
                                t_.wait_ge(p.sems["eT"], p.cnt["eT"])
                            rhs = (kst[:, (ks % 2) * 4096 + 512 * c:(ks % 2) * 4096 + 512 * (c + 1)]
                                   if peer else
                                   onfm_sb[:, c * NT + 512 * (ks - 4 * half):c * NT + 512 * (ks - 4 * half) + 512])
                            ins = t_.matmul(ps_s[:, 0:512],
                                            onfm_sb[:, c * NT + 128 * qt:c * NT + 128 * (qt + 1)],
                                            rhs, start=(c == 0), stop=(c == 7))
                        p.inc("eT", ins)

                    @block.vector
                    def _(v, qt=qt, dj=dj, wfull=wfull):
                        p.w(v, "eT")
                        if dj is not None:
                            v.tensor_mul(seldg[:], ps_s[:, 128 * dj:128 * (dj + 1)], tris_sb[:])
                            v.drain()
                            v.tensor_add(seldg[:], seldg[:], mng2[:])
                            v.drain()
                            v.reduce_max(chm[:], seldg[:], AX.X)
                            v.drain()
                            if wfull > 0:
                                v.reduce_max(chm2[:], ps_s[:, 0:wfull], AX.X)
                                v.drain()
                                v.tensor_max(chm[:], chm[:], chm2[:])
                                v.drain()
                        else:
                            v.reduce_max(chm[:], ps_s[:, 0:512], AX.X)
                            v.drain()
                        v.tensor_max(runm[:, qt:qt + 1], runm[:, qt:qt + 1], chm[:])
                        v.drain()
                        p.inc("eV", v.nop())

            fence(p, block)

            for qt in range(NTILE):
                @block.tensor
                def _(t_, qt=qt):
                    p.w(t_, "eV")
                    for c in range(8):
                        if c == 7:
                            t_.wait_ge(p.sems["eT"], p.cnt["eT"])
                        ins = t_.matmul(ps_e[:, 8 + (qt % 4):9 + (qt % 4)],
                                        onfm_sb[:, c * NT + 128 * qt:c * NT + 128 * (qt + 1)],
                                        eman[:, c:c + 1], start=(c == 0), stop=(c == 7))
                    p.inc("eT", ins)

                @block.vector
                def _(v, qt=qt):
                    p.w(v, "eT")
                    v.tensor_copy(esim[:, qt:qt + 1], ps_e[:, 8 + (qt % 4):9 + (qt % 4)])
                    v.drain()
                    p.inc("eV", v.nop())
            fence(p, block)

            for qt in range(NTILE):
                @block.vector
                def _(v, qt=qt):
                    p.w(v, "eS")
                    sq_ = gcol[:, 0:1]
                    if qt == 0:
                        v.tensor_scalar(sq_, esim[:, 0:1], m0_sb[:, 0:1], None, ALU.mult)
                        v.drain()
                        v.scalar_tensor_tensor(sq_, runm[:, 0:1], m0_sb[:, 1:2], sq_,
                                               ALU.mult, ALU.add)
                        v.drain()
                    else:
                        v.tensor_copy(sq_, runm[:, qt:qt + 1])
                        v.drain()
                    v.tensor_sub(gcol[:, 1:2], sq_, esim[:, qt:qt + 1])
                    v.drain()
                    v.tensor_mul(gcol[:, 1:2], gcol[:, 1:2], scc[:, 5:6])
                    v.drain()
                    v.tensor_add(gcol[:, 1:2], gcol[:, 1:2], esim[:, qt:qt + 1])
                    v.drain()
                    v.tensor_mul(gcol[:, 2:3], gcol[:, 1:2], scc[:, 3:4])
                    v.drain()
                    p.inc("eV", v.nop())

                @block.scalar
                def _(s, qt=qt):
                    p.w(s, "eV")
                    s.activation(gcol[:, 3:4], gcol[:, 2:3], AF.Exp, scale=-1.0)
                    s.drain()
                    p.inc("eS", s.nop())

                @block.vector
                def _(v, qt=qt):
                    p.w(v, "eS")
                    v.tensor_mul(gcol[:, 3:4], gcol[:, 3:4], scc[:, 4:5])
                    v.drain()
                    v.tensor_add(gcol[:, 3:4], gcol[:, 3:4], scc[:, 6:7])
                    v.drain()
                    v.tensor_scalar(rt[:], outg_sb[:, qt * D:(qt + 1) * D], gcol[:, 3:4],
                                    None, ALU.mult)
                    v.drain()
                    p.inc("eV", v.nop())

                seq_ln(p, block, rt[:], yt3[:], scr, p.const(1e-5))

                @block.vector
                def _(v, qt=qt):
                    p.w(v, "eS")
                    xa = x_sb[:, qt * D:(qt + 1) * D]
                    v.tensor_add(xa, xa, yt3[:])
                    v.drain()
                    p.inc("eV", v.nop())

            fence(p, block)
            s1.close()

            s3 = contextlib.ExitStack()
            wqT = s3.enter_context(nc.sbuf_tensor("wqT", [128, 8 * D], BF16))
            wkT = s3.enter_context(nc.sbuf_tensor("wkT", [128, 8 * 256], F32))
            wvT = s3.enter_context(nc.sbuf_tensor("wvT", [128, 8 * 256], F32))
            wld = s3.enter_context(nc.sbuf_tensor("wld", [128, 1024], F32))
            wq_t = s3.enter_context(nc.sbuf_tensor("wq_t", [128, 1024], BF16))
            wq_t32 = s3.enter_context(nc.sbuf_tensor("wq_t32", [128, 1024], F32))
            q_scr = {k: s3.enter_context(nc.sbuf_tensor("q_" + k, [128, 1],
                      I32 if k == "i1" else F32))[:] for k in
                     ("mx", "mn", "rng", "isc", "base", "i1", "c1", "d1", "m1", "half1")}
            q_scr["it"] = s3.enter_context(nc.sbuf_tensor("q_it", [128, 128], I32))[:]
            q_scr["ct"] = s3.enter_context(nc.sbuf_tensor("q_ct", [128, 128], F32))[:]
            q_scr["dt"] = s3.enter_context(nc.sbuf_tensor("q_dt", [128, 128], F32))[:]
            q_scr["mt"] = s3.enter_context(nc.sbuf_tensor("q_mt", [128, 128], F32))[:]
            q_scr["halft"] = s3.enter_context(nc.sbuf_tensor("q_halft", [128, 128], F32))[:]
            q_scr["obuf"] = s3.enter_context(nc.sbuf_tensor("q_obuf", [128, 1024], BF16))[:]
            ps3 = contextlib.ExitStack()
            ps_q = ps3.enter_context(nc.psum_tensor("ps_q", [128, 512], BF16))
            ps_q32 = ps3.enter_context(nc.psum_tensor("ps_q32", [128, 512], F32))

            @block.vector
            def _(v):
                v.memset(q_scr["half1"], 0.5)
                v.memset(q_scr["halft"], 0.5)
                v.drain()
                p.inc("eV", v.nop())
            fence(p, block, "vy")

            quant_stream(p, block, wq_i, D, D, wqT, False, id_b[:], q_scr, wld[:], wq_t[:], ps_q)
            quant_stream(p, block, wk_i, 256, D, wkT, False, id_f[:], q_scr, wld[:],
                         wq_t32[:], ps_q32, f32=True)
            quant_stream(p, block, wv_i, 256, D, wvT, False, id_f[:], q_scr, wld[:],
                         wq_t32[:], ps_q32, f32=True)
            quant_stream(p, block, wo_i, D, D, woqt_o, True, id_b[:], q_scr, wld[:], wq_t[:], ps_q)
            quant_stream(p, block, w1e_i, 3584, D, w1t_o, True, id_b[:], q_scr, wld[:], wq_t[:], ps_q)
            quant_stream(p, block, w3e_i, 3584, D, w3t_o, True, id_b[:], q_scr, wld[:], wq_t[:], ps_q)
            quant_stream(p, block, w2e_i, D, 3584, w2t_o, True, id_b[:], q_scr, wld[:], wq_t[:], ps_q)
            fence(p, block)
            ps3.close()

            s2 = contextlib.ExitStack()
            xnfm = s2.enter_context(nc.sbuf_tensor("xnfm", [128, 8 * NT], BF16))
            g1b = s2.enter_context(nc.sbuf_tensor("g1b", [128, D], F32))
            g1r_sb = s2.enter_context(nc.sbuf_tensor("g1r_sb", [1, D], F32))
            ln3 = s2.enter_context(nc.sbuf_tensor("ln3", [128, D], F32))
            xn32 = s2.enter_context(nc.sbuf_tensor("xn32", [128, D], F32))
            xnb = s2.enter_context(nc.sbuf_tensor("xnb", [128, D], BF16))
            xnfm32 = s2.enter_context(nc.sbuf_tensor("xnfm32", [128, D], F32))
            qstg = s2.enter_context(nc.sbuf_tensor("qstg", [128, 2 * D], BF16))
            kvstg = s2.enter_context(nc.sbuf_tensor("kvstg", [128, 2 * 512], BF16))
            invb = s2.enter_context(nc.sbuf_tensor("invb", [128, 128], F32))
            invr = s2.enter_context(nc.sbuf_tensor("invr", [1, 128], F32))
            pos_sb = s2.enter_context(nc.sbuf_tensor("pos_sb", [128, 16], F32))
            frq = s2.enter_context(nc.sbuf_tensor("frq", [128, 128], F32))
            frq2 = s2.enter_context(nc.sbuf_tensor("frq2", [128, 128], F32))
            zz = s2.enter_context(nc.sbuf_tensor("zz", [128, 128], F32))
            fli = s2.enter_context(nc.sbuf_tensor("fli", [128, 128], I32))
            flc = s2.enter_context(nc.sbuf_tensor("flc", [128, 128], F32))
            flm = s2.enter_context(nc.sbuf_tensor("flm", [128, 128], F32))
            tinyp = s2.enter_context(nc.sbuf_tensor("tinyp", [128, 128], F32))
            ptin = s2.enter_context(nc.sbuf_tensor("ptin", [128, 256], F32))
            cost = s2.enter_context(nc.sbuf_tensor("cost", [128, 128], F32))
            sint = s2.enter_context(nc.sbuf_tensor("sint", [128, 128], F32))
            qf = s2.enter_context(nc.sbuf_tensor("qf", [128, D], F32))
            rq = s2.enter_context(nc.sbuf_tensor("rq", [128, D], F32))
            qb = s2.enter_context(nc.sbuf_tensor("qb", [128, D], BF16))
            kvf = s2.enter_context(nc.sbuf_tensor("kvf", [128, 512], F32))
            kqf = s2.enter_context(nc.sbuf_tensor("kqf", [128, 256], F32))
            krt = s2.enter_context(nc.sbuf_tensor("krt", [128, 256], F32))
            ps_g = s2.enter_context(nc.psum_tensor("ps_g", [128, 1024], F32))
            ps_kv = s2.enter_context(nc.psum_tensor("ps_kv", [128, 512], F32))
            ps_t4 = s2.enter_context(nc.psum_tensor("ps_t4", [128, 2048], BF16))
            ps_t32 = s2.enter_context(nc.psum_tensor("ps_t32", [128, 512], F32))

            @block.sync
            def _(y):
                p.inc("dY", y.dma_start(out=g1r_sb[:], in_=g1r))
                p.inc("dY", y.dma_start(out=pos_sb[:], in_=poscol))

            @block.tensor
            def _(t_):
                p.w(t_, "dY")
                t_.wait_ge(p.sems["eT"], p.cnt["eT"])
                t_.matmul(ps_g[:, 0:512], ones_row[:], g1r_sb[0:1, 0:512],
                          start=True, stop=True)
                p.inc("eT", t_.matmul(ps_g[:, 512:1024], ones_row[:], g1r_sb[0:1, 512:1024],
                                      start=True, stop=True))

            @block.vector
            def _(v):
                p.w(v, "eT")
                v.tensor_copy(g1b[:], ps_g[:])
                v.memset(tinyp[:], 1e-30)
                v.drain()
                p.inc("eV", v.nop())

            @block.gpsimd
            def _(g):
                p.w(g, "dY")
                g.iota(invr[0:1, 0:64], [[1, 64]], base=0, channel_multiplier=0,
                       allow_small_or_imprecise_dtypes=True)
                g.drain()
                g.tensor_copy(invr[0:1, 64:128], invr[0:1, 0:64])
                g.drain()
                p.inc("eG", g.engine_nop())

            @block.scalar
            def _(s):
                p.w(s, "eG")
                s.activation(invr[:], invr[:], AF.Exp, scale=-float(np.log(1e6)) / 64.0)
                s.drain()
                p.inc("eS", s.nop())

            @block.tensor
            def _(t_):
                p.w(t_, "eS")
                p.w(t_, "eV")
                t_.wait_ge(p.sems["eT"], p.cnt["eT"])
                p.inc("eT", t_.matmul(ps_g[:, 0:128], ones_row[:], invr[:],
                                      start=True, stop=True))

            @block.vector
            def _(v):
                p.w(v, "eT")
                v.tensor_copy(invb[:], ps_g[:, 0:128])
                v.drain()
                p.inc("eV", v.nop())
            fence(p, block)

            TWO_PI = 2.0 * PI
            for t in range(NTILE):
                seq_ln(p, block, x_sb[:, t * D:(t + 1) * D], ln3[:], scr, p.const(1e-5))
                seq_ln(p, block, ln3[:], xn32[:], scr, p.const(1e-5), sub_mean=False,
                       gain_bcast=g1b[:])

                @block.vector
                def _(v, t=t):
                    p.w(v, "eS")
                    v.tensor_copy(xnb[:], xn32[:])
                    v.drain()
                    p.inc("eV", v.nop())

                jobs = [(xnb[:, 128 * c:128 * (c + 1)],
                         xnfm[:, c * NT + 128 * t:c * NT + 128 * (t + 1)]) for c in range(8)]
                tp_sched(p, block, jobs, ps_t4, id_b[:])
                for c in range(8):
                    @block.tensor
                    def _(t_, c=c):
                        p.w(t_, "eV")
                        if c >= 1:
                            p.w(t_, "eV")
                        t_.wait_ge(p.sems["eT"], p.cnt["eT"])
                        p.inc("eT", t_.transpose(ps_t32[:, 0:128] if c % 2 == 0 else ps_t32[:, 128:256],
                                                 xn32[:, 128 * c:128 * (c + 1)], id_f[:]))

                    @block.vector
                    def _(v, c=c):
                        p.w(v, "eT")
                        v.tensor_copy(xnfm32[:, 128 * c:128 * (c + 1)],
                                      ps_t32[:, 0:128] if c % 2 == 0 else ps_t32[:, 128:256])
                        v.drain()
                        p.inc("eV", v.nop())

                @block.vector
                def _(v, t=t):
                    p.w(v, "eT")
                    v.tensor_scalar(frq[:], invb[:], pos_sb[:, t:t + 1], None, ALU.mult)
                    v.drain()
                    v.tensor_scalar(frq2[:], frq[:], 1.0, PI / 2.0, ALU.mult, ALU.add)
                    v.drain()
                    for k2, fr in enumerate((frq, frq2)):
                        v.tensor_scalar(zz[:], fr[:], 1.0 / TWO_PI, 0.5, ALU.mult, ALU.add)
                        v.drain()
                        v.tensor_copy(fli[:], zz[:])
                        v.drain()
                        v.tensor_copy(flc[:], fli[:])
                        v.drain()
                        v.tensor_sub(flm[:], flc[:], zz[:])
                        v.drain()
                        v.tensor_tensor(flm[:], flm[:], tinyp[:], ALU.is_ge)
                        v.drain()
                        v.tensor_sub(flc[:], flc[:], flm[:])
                        v.drain()
                        v.scalar_tensor_tensor(ptin[:, 128 * k2:128 * (k2 + 1)], flc[:],
                                               -TWO_PI, fr[:], ALU.mult, ALU.add)
                        v.drain()
                    p.inc("eV", v.nop())

                @block.scalar
                def _(s, t=t):
                    p.w(s, "eV")
                    s.activation(sint[:], ptin[:, 0:128], AF.Sin)
                    s.activation(cost[:], ptin[:, 128:256], AF.Sin)
                    s.drain()
                    p.inc("eS", s.nop())

                @block.tensor
                def _(t_, t=t):
                    p.w(t_, "eV")
                    for n in range(2):
                        for c in range(8):
                            t_.matmul(ps_g[:, 512 * n:512 * (n + 1)],
                                      xnfm[:, c * NT + 128 * t:c * NT + 128 * (t + 1)],
                                      wqT[:, c * D + 512 * n:c * D + 512 * n + 512],
                                      start=(c == 0), stop=(c == 7))
                    for c in range(8):
                        t_.matmul(ps_kv[:, 0:256],
                                  xnfm32[:, 128 * c:128 * (c + 1)],
                                  wkT[:, c * 256:(c + 1) * 256],
                                  start=(c == 0), stop=(c == 7))
                    for c in range(8):
                        if c == 7:
                            t_.wait_ge(p.sems["eT"], p.cnt["eT"])
                        ins = t_.matmul(ps_kv[:, 256:512],
                                        xnfm32[:, 128 * c:128 * (c + 1)],
                                        wvT[:, c * 256:(c + 1) * 256],
                                        start=(c == 0), stop=(c == 7))
                    p.inc("eT", ins)

                @block.vector
                def _(v, t=t):
                    p.w(v, "eT")
                    p.w(v, "eS")
                    if t >= 2:
                        p.w(v, "dG")
                    v.tensor_copy(qf[:], ps_g[:])
                    v.drain()
                    for h in range(8):
                        v.tensor_scalar(rq[:, 128 * h:128 * h + 64],
                                        qf[:, 128 * h + 64:128 * (h + 1)], -1.0, None, ALU.mult)
                        v.tensor_copy(rq[:, 128 * h + 64:128 * (h + 1)],
                                      qf[:, 128 * h:128 * h + 64])
                    v.drain()
                    for h in range(8):
                        v.tensor_mul(qf[:, 128 * h:128 * (h + 1)],
                                     qf[:, 128 * h:128 * (h + 1)], cost[:])
                        v.tensor_mul(rq[:, 128 * h:128 * (h + 1)],
                                     rq[:, 128 * h:128 * (h + 1)], sint[:])
                    v.drain()
                    v.tensor_add(qb[:], qf[:], rq[:])
                    v.drain()
                    v.tensor_copy(kvf[:], ps_kv[:])
                    v.drain()
                    emit_quant_tile(p, v, kvf[:, 0:128], kqf[:, 0:128], q_scr)
                    emit_quant_tile(p, v, kvf[:, 128:256], kqf[:, 128:256], q_scr)
                    par_t = t % 2
                    emit_quant_tile(p, v, kvf[:, 256:384],
                                    kvstg[:, 512 * par_t + 256:512 * par_t + 384], q_scr)
                    emit_quant_tile(p, v, kvf[:, 384:512],
                                    kvstg[:, 512 * par_t + 384:512 * par_t + 512], q_scr)
                    for kvh in range(2):
                        o = 128 * kvh
                        v.tensor_scalar(krt[:, o:o + 64], kqf[:, o + 64:o + 128],
                                        -1.0, None, ALU.mult)
                        v.tensor_copy(krt[:, o + 64:o + 128], kqf[:, o:o + 64])
                        v.drain()
                        v.tensor_mul(kqf[:, o:o + 128], kqf[:, o:o + 128], cost[:])
                        v.drain()
                        v.tensor_mul(krt[:, o:o + 128], krt[:, o:o + 128], sint[:])
                        v.drain()
                        v.tensor_add(kvstg[:, 512 * par_t + o:512 * par_t + o + 128],
                                     kqf[:, o:o + 128], krt[:, o:o + 128])
                        v.drain()
                    p.inc("eV", v.nop())

                @block.gpsimd
                def _(g, t=t):
                    p.w(g, "eV")
                    p.inc("dG", g.dma_start(out=kv_o[128 * t:128 * (t + 1), :],
                                            in_=kvstg[:, 512 * (t % 2):512 * (t % 2) + 512]))

                jobs = [(qb[:, 128 * h:128 * (h + 1)],
                         qstg[:, (t % 2) * D + 128 * h:(t % 2) * D + 128 * (h + 1)])
                        for h in range(8)]
                tp_sched(p, block, jobs, ps_t4, id_b[:])

                @block.gpsimd
                def _(g, t=t):
                    p.w(g, "eV")
                    p.inc("dG", g.dma_start(
                        out=qfm_o.rearrange("(c p) n -> p c n", p=128)[:, :, 128 * t:128 * (t + 1)],
                        in_=qstg[:, (t % 2) * D:(t % 2) * D + D].rearrange("p (c n) -> p c n", c=8)))
            fence(p, block)

            @block.gpsimd
            def _(g):
                p.w(g, "eV")
                p.inc("dG", g.dma_start(out=x3_o.rearrange("(t p) d -> p t d", p=128),
                                        in_=x_sb[:].rearrange("p (t d) -> p t d", t=NTILE)))
                p.w(g, "dG")
            s2.close()
            s3.close()
    return p.nc


def build_E(half, dbg=None, ne=8, noscale=False, upconst=False, now2=False):
    p = P()
    nc = p.nc
    Lk = 2048 * (half + 1)
    KC = Lk // 128
    x3_i = nc.dram_tensor("x3", [NT, D], F32, kind="ExternalInput").ap()
    qfm_i = nc.dram_tensor("qfm", [D, NT], BF16, kind="ExternalInput").ap()
    kv_i = nc.dram_tensor("kv", [Lk, 512], BF16, kind="ExternalInput").ap()
    woqt_i = nc.dram_tensor("woqt", [D, D], BF16, kind="ExternalInput").ap()
    g2r = nc.dram_tensor("g2r", [1, D], F32, kind="ExternalInput").ap()
    wgT_i = nc.dram_tensor("wgT", [D, 8], F32, kind="ExternalInput").ap()
    tri_inc = nc.dram_tensor("tri_inc", [128, 128], F32, kind="ExternalInput").ap()
    mneg_i = nc.dram_tensor("mneg", [128, 128], F32, kind="ExternalInput").ap()
    iden = nc.dram_tensor("iden", [128, 128], F32, kind="ExternalInput").ap()
    oneh_i = nc.dram_tensor("oneh", [8, 1024], F32, kind="ExternalInput").ap()
    w1t_a = nc.dram_tensor("w1t_a", [8, D, 3584], BF16, kind="ExternalInput").ap()
    w3t_a = nc.dram_tensor("w3t_a", [8, D, 3584], BF16, kind="ExternalInput").ap()
    w2t_a = nc.dram_tensor("w2t_a", [8, 3584, D], BF16, kind="ExternalInput").ap()
    out_o = nc.dram_tensor("out", [NT, D], F32, kind="ExternalOutput").ap()

    for s_ in ("dY", "dG", "eV", "eS", "eT", "eG", "eY"):
        p.sem(s_)

    h_sb = p.sb("h_sb", [128, NTILE * D], F32)
    id_f = p.sb("id_f", [128, 128], F32)
    id_b = p.sb("id_b", [128, 128], BF16)
    ones_row = p.sb("ones_row", [1, 128], F32)
    ones8 = p.sb("ones8", [128, 8], F32)
    tri_f = p.sb("tri_f", [128, 128], F32)
    tri_b = p.sb("tri_b", [128, 128], BF16)
    x3st = p.sb("x3st", [128, D], F32)
    scr = {"sq": [p.sb("ssq0", [128, D], F32)[:], p.sb("ssq1", [128, D], F32)[:]],
           "m": [p.sb("sm0", [128, 1], F32)[:], p.sb("sm1", [128, 1], F32)[:]],
           "vv": [p.sb("sv0", [128, 1], F32)[:], p.sb("sv1", [128, 1], F32)[:]],
           "r": [p.sb("sr0", [128, 1], F32)[:], p.sb("sr1", [128, 1], F32)[:]]}

    ISQ = 1.0 / float(np.sqrt(128.0))

    with p.st:
        with nc.Block() as block:
            p.add_consts(block, [1e-5])

            e2 = contextlib.ExitStack()
            qfm_sb = e2.enter_context(nc.sbuf_tensor("qfm_sb", [128, 8 * NT], BF16))
            kfm_sb = e2.enter_context(nc.sbuf_tensor("kfm_sb", [128, 2 * Lk], BF16))
            v_sb = e2.enter_context(nc.sbuf_tensor("v_sb", [128, KC * 256], BF16))
            wo_sb = e2.enter_context(nc.sbuf_tensor("wo_sb", [128, 8 * D], BF16))
            psb = e2.enter_context(nc.sbuf_tensor("psb", [128, 2048], BF16))
            rmx = e2.enter_context(nc.sbuf_tensor("rmx", [128, 4], F32))
            rsm = e2.enter_context(nc.sbuf_tensor("rsm", [128, 4], F32))
            seld = e2.enter_context(nc.sbuf_tensor("seld", [128, 128], F32))
            ofh = e2.enter_context(nc.sbuf_tensor("ofh", [128, 128], F32))
            ofb = e2.enter_context(nc.sbuf_tensor("ofb", [128, 128], BF16))
            ofm = e2.enter_context(nc.sbuf_tensor("ofm", [128, 8 * 128], BF16))
            ptrc = e2.enter_context(nc.sbuf_tensor("ptrc", [128, 256], BF16))
            mneg = e2.enter_context(nc.sbuf_tensor("mneg_sb", [128, 128], F32))
            at_sb = e2.enter_context(nc.sbuf_tensor("at_sb", [128, D], F32))
            ln3t = e2.enter_context(nc.sbuf_tensor("ln3t", [128, D], F32))
            ps_big = e2.enter_context(nc.psum_tensor("ps_big", [128, 2048], F32))
            ps_ptr = e2.enter_context(nc.psum_tensor("ps_ptr", [128, 2048], BF16))
            ps_o = e2.enter_context(nc.psum_tensor("ps_o", [128, 384], F32))

            @block.sync
            def _(y):
                p.inc("dY", y.dma_start(out=id_f[:], in_=iden))
                p.inc("dY", y.dma_start(out=tri_f[:], in_=tri_inc))
                p.inc("dY", y.dma_start(out=mneg[:], in_=mneg_i))
                p.inc("dY", y.dma_start(out=qfm_sb[:].rearrange("p (c n) -> p c n", c=8),
                                        in_=qfm_i.rearrange("(c p) n -> p c n", p=128)))
                p.inc("dY", y.dma_start(out=v_sb[:].rearrange("p (t n) -> p t n", t=KC),
                                        in_=kv_i.rearrange("(t p) n -> p t n", p=128)[:, :, 256:512]))
                p.inc("dY", y.dma_start_transpose(kfm_sb[:, 0:Lk], kv_i[:, 0:128]))
                p.inc("dY", y.dma_start_transpose(kfm_sb[:, Lk:2 * Lk], kv_i[:, 128:256]))

            @block.gpsimd
            def _(g):
                p.inc("dG", g.dma_start(out=wo_sb[:].rearrange("p (c n) -> p c n", c=8),
                                        in_=woqt_i.rearrange("(c p) n -> p c n", p=128)))

            @block.vector
            def _(v):
                p.w(v, "dY")
                p.w(v, "dG")
                v.tensor_copy(id_b[:], id_f[:])
                v.tensor_copy(tri_b[:], tri_f[:])
                v.memset(ones_row[:], 1.0)
                v.memset(ones8[:], 1.0)
                v.drain()
                p.inc("eV", v.nop())
            fence(p, block)

            for qt in range(NTILE):
                kmax = 2048 * half + 128 * (qt + 1)
                nseg = (kmax + 2047) // 2048

                @block.sync
                def _(y, qt=qt):
                    p.w(y, "eS")
                    p.w(y, "eV")
                    p.inc("dY", y.dma_start(out=x3st[:],
                                            in_=x3_i[128 * qt:128 * (qt + 1), :]))

                for h_ in range(8):
                    kvh = h_ // 4
                    lq = qfm_sb[:, h_ * NT + 128 * qt:h_ * NT + 128 * (qt + 1)]
                    for pass2 in (False, True):
                        for seg in range(nseg):
                            k0 = 2048 * seg
                            w_ = min(2048, kmax - k0)
                            nmm = (w_ + 511) // 512

                            @block.tensor
                            def _(t_, lq=lq, k0=k0, w_=w_, kvh=kvh, nmm=nmm):
                                p.w(t_, "eV")
                                p.w(t_, "eS")
                                for j in range(nmm):
                                    ww = min(512, w_ - 512 * j)
                                    if j == nmm - 1:
                                        t_.wait_ge(p.sems["eT"], p.cnt["eT"])
                                    ins = t_.matmul(ps_big[:, 512 * j:512 * j + ww], lq,
                                                    kfm_sb[:, kvh * Lk + k0 + 512 * j:
                                                            kvh * Lk + k0 + 512 * j + ww],
                                                    start=True, stop=True)
                                p.inc("eT", ins)

                            diag = (k0 + w_ == kmax)
                            if not pass2:
                                @block.vector
                                def _(v, seg=seg, w_=w_, diag=diag):
                                    p.w(v, "eT")
                                    if diag:
                                        v.tensor_mul(seld[:], ps_big[:, w_ - 128:w_], tri_f[:])
                                        v.drain()
                                        v.tensor_add(seld[:], seld[:], mneg[:])
                                        v.drain()
                                        v.reduce_max(rmx[:, 1:2], seld[:], AX.X)
                                        v.drain()
                                        if w_ > 128:
                                            v.reduce_max(rmx[:, 2:3], ps_big[:, 0:w_ - 128], AX.X)
                                            v.drain()
                                            v.tensor_max(rmx[:, 1:2], rmx[:, 1:2], rmx[:, 2:3])
                                            v.drain()
                                    else:
                                        v.reduce_max(rmx[:, 1:2], ps_big[:, 0:w_], AX.X)
                                        v.drain()
                                    if seg == 0:
                                        v.tensor_copy(rmx[:, 0:1], rmx[:, 1:2])
                                    else:
                                        v.tensor_max(rmx[:, 0:1], rmx[:, 0:1], rmx[:, 1:2])
                                    v.drain()
                                    p.inc("eV", v.nop())
                            else:
                                if seg == 0:
                                    @block.vector
                                    def _(v):
                                        p.w(v, "eT")
                                        v.tensor_scalar(rmx[:, 3:4], rmx[:, 0:1], -ISQ,
                                                        None, ALU.mult)
                                        v.drain()
                                        p.inc("eV", v.nop())

                                @block.scalar
                                def _(s, w_=w_):
                                    p.w(s, "eV")
                                    p.w(s, "eT")
                                    s.activation(psb[:, 0:w_], ps_big[:, 0:w_], AF.Exp,
                                                 scale=ISQ, bias=rmx[:, 3:4],
                                                 accum_out=rsm[:, 1:2])
                                    s.drain()
                                    p.inc("eS", s.nop())

                                @block.vector
                                def _(v, seg=seg, w_=w_, diag=diag):
                                    p.w(v, "eS")
                                    if diag:
                                        v.reduce_sum(rsm[:, 2:3], psb[:, w_ - 128:w_], AX.X)
                                        v.drain()
                                        v.tensor_mul(psb[:, w_ - 128:w_], psb[:, w_ - 128:w_],
                                                     tri_b[:])
                                        v.drain()
                                        v.reduce_sum(rsm[:, 3:4], psb[:, w_ - 128:w_], AX.X)
                                        v.drain()
                                        v.tensor_sub(rsm[:, 1:2], rsm[:, 1:2], rsm[:, 2:3])
                                        v.drain()
                                        v.tensor_add(rsm[:, 1:2], rsm[:, 1:2], rsm[:, 3:4])
                                        v.drain()
                                    if seg == 0:
                                        v.tensor_copy(rsm[:, 0:1], rsm[:, 1:2])
                                    else:
                                        v.tensor_add(rsm[:, 0:1], rsm[:, 0:1], rsm[:, 1:2])
                                    v.drain()
                                    p.inc("eV", v.nop())

                                nch = w_ // 128
                                for kc in range(nch):
                                    pslot = kc % 8
                                    first = (seg == 0 and kc == 0)
                                    last = (seg == nseg - 1 and kc == nch - 1)

                                    @block.tensor
                                    def _(t_, kc=kc, pslot=pslot):
                                        p.w(t_, "eV")
                                        t_.wait_ge(p.sems["eT"], p.cnt["eT"])
                                        p.inc("eT", t_.transpose(
                                            ps_ptr[:, 128 * pslot:128 * (pslot + 1)],
                                            psb[:, 128 * kc:128 * (kc + 1)], id_b[:]))

                                    @block.vector
                                    def _(v, pslot=pslot):
                                        p.w(v, "eT")
                                        v.tensor_copy(ptrc[:, 128 * (pslot % 2):128 * (pslot % 2 + 1)],
                                                      ps_ptr[:, 128 * pslot:128 * (pslot + 1)])
                                        v.drain()
                                        p.inc("eV", v.nop())

                                    @block.tensor
                                    def _(t_, k0=k0, kc=kc, kvh=kvh, first=first,
                                          last=last, pslot=pslot):
                                        p.w(t_, "eV")
                                        kcg = (k0 + 128 * kc) // 128
                                        if last:
                                            t_.wait_ge(p.sems["eT"], p.cnt["eT"])
                                        ins = t_.matmul(ps_o[:, 0:128],
                                                        ptrc[:, 128 * (pslot % 2):128 * (pslot % 2 + 1)],
                                                        v_sb[:, kcg * 256 + 128 * kvh:
                                                             kcg * 256 + 128 * (kvh + 1)],
                                                        start=first, stop=last)
                                        if last:
                                            p.inc("eT", ins)

                    @block.vector
                    def _(v, h_=h_):
                        p.w(v, "eT")
                        v.reciprocal(rsm[:, 1:2], rsm[:, 0:1])
                        v.drain()
                        v.tensor_scalar(ofb[:], ps_o[:, 0:128], rsm[:, 1:2], None, ALU.mult)
                        v.drain()
                        p.inc("eV", v.nop())

                    @block.tensor
                    def _(t_, h_=h_):
                        p.w(t_, "eV")
                        t_.wait_ge(p.sems["eT"], p.cnt["eT"])
                        p.inc("eT", t_.transpose(ps_ptr[:, 0:128], ofb[:], id_b[:]))

                    @block.vector
                    def _(v, h_=h_):
                        p.w(v, "eT")
                        v.tensor_copy(ofm[:, 128 * h_:128 * (h_ + 1)], ps_ptr[:, 0:128])
                        v.drain()
                        p.inc("eV", v.nop())

                @block.tensor
                def _(t_, qt=qt):
                    p.w(t_, "eV")
                    for n in range(2):
                        for h2 in range(8):
                            if n == 1 and h2 == 7:
                                t_.wait_ge(p.sems["eT"], p.cnt["eT"])
                            ins = t_.matmul(ps_big[:, 512 * n:512 * (n + 1)],
                                            ofm[:, 128 * h2:128 * (h2 + 1)],
                                            wo_sb[:, h2 * D + 512 * n:h2 * D + 512 * n + 512],
                                            start=(h2 == 0), stop=(h2 == 7))
                    p.inc("eT", ins)

                @block.vector
                def _(v, qt=qt):
                    p.w(v, "eT")
                    v.tensor_copy(at_sb[:], ps_big[:, 0:1024])
                    v.drain()
                    p.inc("eV", v.nop())

                seq_ln(p, block, x3st[:], ln3t[:], scr, p.const(1e-5))

                @block.vector
                def _(v, qt=qt):
                    p.w(v, "eS")
                    v.tensor_add(h_sb[:, qt * D:(qt + 1) * D], ln3t[:], at_sb[:])
                    v.drain()
                    p.inc("eV", v.nop())
            fence(p, block)
            e2.close()

            e3 = contextlib.ExitStack()
            hnfm = e3.enter_context(nc.sbuf_tensor("hnfm", [128, 8 * NT], BF16))
            g2b = e3.enter_context(nc.sbuf_tensor("g2b", [128, D], F32))
            g2r_sb = e3.enter_context(nc.sbuf_tensor("g2r_sb", [1, D], F32))
            wg_sb = e3.enter_context(nc.sbuf_tensor("wg_sb", [128, 64], F32))
            wgb = e3.enter_context(nc.sbuf_tensor("wgb", [128, 64], BF16))
            hnb = e3.enter_context(nc.sbuf_tensor("hnb", [128, D], BF16))
            rwT = e3.enter_context(nc.sbuf_tensor("rwT", [8, NT], BF16))
            rwbb = e3.enter_context(nc.sbuf_tensor("rwbb", [128, 8], BF16))
            lg = e3.enter_context(nc.sbuf_tensor("lg", [128, 8], F32))
            exl = e3.enter_context(nc.sbuf_tensor("exl", [128, 8], F32))
            exm = e3.enter_context(nc.sbuf_tensor("exm", [128, 8], F32))
            mk2 = e3.enter_context(nc.sbuf_tensor("mk2", [128, 8], F32))
            tcol = e3.enter_context(nc.sbuf_tensor("tcol", [128, 6], F32))
            neg1 = e3.enter_context(nc.sbuf_tensor("neg1", [128, 8], F32))
            rw_sb = e3.enter_context(nc.sbuf_tensor("rw_sb", [128, 8], F32))
            ps_l = e3.enter_context(nc.psum_tensor("ps_l", [128, 1024], F32))
            ps_t5 = e3.enter_context(nc.psum_tensor("ps_t5", [128, 2048], BF16))

            @block.sync
            def _(y):
                p.inc("dY", y.dma_start(out=g2r_sb[:], in_=g2r))
                p.inc("dY", y.dma_start(out=wg_sb[:].rearrange("p (c n) -> p c n", c=8),
                                        in_=wgT_i.rearrange("(c p) n -> p c n", p=128)))

            @block.tensor
            def _(t_):
                p.w(t_, "dY")
                t_.wait_ge(p.sems["eT"], p.cnt["eT"])
                t_.matmul(ps_l[:, 0:512], ones_row[:], g2r_sb[0:1, 0:512],
                          start=True, stop=True)
                p.inc("eT", t_.matmul(ps_l[:, 512:1024], ones_row[:], g2r_sb[0:1, 512:1024],
                                      start=True, stop=True))

            @block.vector
            def _(v):
                p.w(v, "eT")
                p.w(v, "dY")
                v.tensor_copy(g2b[:], ps_l[:])
                v.tensor_copy(wgb[:], wg_sb[:])
                v.memset(neg1[:], -1.0)
                v.drain()
                p.inc("eV", v.nop())
            fence(p, block)

            for t in range(NTILE):
                seq_ln(p, block, h_sb[:, t * D:(t + 1) * D], hnb[:], scr, p.const(1e-5),
                       sub_mean=False, gain_bcast=g2b[:])
                if dbg == "hn":
                    @block.gpsimd
                    def _(g, t=t):
                        p.w(g, "eV")
                        p.inc("dG", g.dma_start(out=out_o[128 * t:128 * (t + 1), :],
                                                in_=hnb[:]))
                jobs = [(hnb[:, 128 * c:128 * (c + 1)],
                         hnfm[:, c * NT + 128 * t:c * NT + 128 * (t + 1)]) for c in range(8)]
                tp_sched(p, block, jobs, ps_t5, id_b[:])

                @block.tensor
                def _(t_, t=t):
                    p.w(t_, "eV")
                    for c in range(8):
                        if c == 7:
                            t_.wait_ge(p.sems["eT"], p.cnt["eT"])
                        ins = t_.matmul(ps_l[:, 0:8],
                                        hnfm[:, c * NT + 128 * t:c * NT + 128 * (t + 1)],
                                        wgb[:, 8 * c:8 * (c + 1)],
                                        start=(c == 0), stop=(c == 7))
                    p.inc("eT", ins)

                @block.vector
                def _(v, t=t):
                    p.w(v, "eT")
                    v.tensor_copy(lg[:], ps_l[:, 0:8])
                    v.drain()
                    v.reduce_max(tcol[:, 0:1], lg[:], AX.X)
                    v.drain()
                    v.tensor_scalar(tcol[:, 1:2], tcol[:, 0:1], -1.0, None, ALU.mult)
                    v.drain()
                    p.inc("eV", v.nop())

                @block.scalar
                def _(s, t=t):
                    p.w(s, "eV")
                    s.activation(exl[:], lg[:], AF.Exp, bias=tcol[:, 1:2])
                    s.drain()
                    p.inc("eS", s.nop())

                @block.vector
                def _(v, t=t):
                    p.w(v, "eS")
                    v.reduce_max(tcol[:, 2:3], exl[:], AX.X)
                    v.drain()
                    v.tensor_scalar(exm[:], ones8[:], tcol[:, 2:3], None, ALU.mult)
                    v.drain()
                    v.tensor_tensor(mk2[:], exl[:], exm[:], ALU.is_ge)
                    v.drain()
                    v.tensor_scalar(exm[:], exl[:], 1.0, None, ALU.add)
                    v.drain()
                    v.tensor_mul(exm[:], exm[:], mk2[:])
                    v.drain()
                    v.tensor_sub(exm[:], exl[:], exm[:])
                    v.drain()
                    v.reduce_max(tcol[:, 3:4], exm[:], AX.X)
                    v.drain()
                    v.tensor_scalar(exm[:], ones8[:], tcol[:, 3:4], None, ALU.mult)
                    v.drain()
                    v.tensor_tensor(mk2[:], exl[:], exm[:], ALU.is_ge)
                    v.drain()
                    v.tensor_add(tcol[:, 4:5], tcol[:, 2:3], tcol[:, 3:4])
                    v.drain()
                    v.reciprocal(tcol[:, 5:6], tcol[:, 4:5])
                    v.drain()
                    v.tensor_mul(rw_sb[:], exl[:], mk2[:])
                    v.drain()
                    v.tensor_scalar(rwbb[:], rw_sb[:], tcol[:, 5:6], None, ALU.mult)
                    v.drain()
                    p.inc("eV", v.nop())

                @block.tensor
                def _(t_, t=t):
                    p.w(t_, "eV")
                    t_.wait_ge(p.sems["eT"], p.cnt["eT"])
                    p.inc("eT", t_.transpose(ps_t5[0:8, 0:128], rwbb[:], id_b[:]))

                @block.vector
                def _(v, t=t):
                    p.w(v, "eT")
                    v.tensor_copy(rwT[0:8, 128 * t:128 * (t + 1)], ps_t5[0:8, 0:128])
                    v.drain()
                    p.inc("eV", v.nop())
            fence(p, block)

            moe_acc = e3.enter_context(nc.sbuf_tensor("moe_acc", [128, 8 * NT], BF16))
            e4 = contextlib.ExitStack()
            up_sb = e4.enter_context(nc.sbuf_tensor("up_sb", [128, 28 * 512], BF16))
            pan1 = e4.enter_context(nc.sbuf_tensor("pan1", [128, 2 * 1024], BF16))
            pan3 = e4.enter_context(nc.sbuf_tensor("pan3", [128, 2 * 1024], BF16))
            pan2 = e4.enter_context(nc.sbuf_tensor("pan2", [128, 3584], BF16))
            rwb = e4.enter_context(nc.sbuf_tensor("rwb", [128, 512], BF16))
            oneh_sb = e4.enter_context(nc.sbuf_tensor("oneh_sb", [8, 1024], BF16))
            u1f = e4.enter_context(nc.sbuf_tensor("u1f", [128, 512], F32))
            ps_up = e4.enter_context(nc.psum_tensor("ps_up", [128, 1024], F32))
            ps_m = e4.enter_context(nc.psum_tensor("ps_m", [128, 1024], F32))

            @block.gpsimd
            def _(g):
                p.inc("dG", g.dma_start(out=oneh_sb[:], in_=oneh_i))

            @block.vector
            def _(v):
                p.w(v, "dG")
                v.memset(moe_acc[:], 0.0)
                v.drain()
                p.inc("eV", v.nop())
            fence(p, block, "vty")

            for e in range(ne):
                for mt in range(4):
                    tok0 = 512 * mt
                    @block.tensor
                    def _(t_, e=e, mt=mt, tok0=tok0):
                        p.w(t_, "eV")
                        t_.wait_ge(p.sems["eT"], p.cnt["eT"])
                        p.inc("eT", t_.matmul(ps_up[:, 512:1024],
                                              oneh_sb[0:8, 128 * e:128 * (e + 1)],
                                              rwT[0:8, tok0:tok0 + 512],
                                              start=True, stop=True))

                    @block.vector
                    def _(v, e=e, mt=mt):
                        p.w(v, "eT")
                        v.tensor_copy(rwb[:], ps_up[:, 512:1024])
                        v.drain()
                        p.inc("eV", v.nop())

                    if upconst:
                        @block.vector
                        def _(v):
                            p.w(v, "eT")
                            v.memset(up_sb[:], 0.01)
                            v.drain()
                            p.inc("eV", v.nop())
                    for ffc in range(0 if upconst else 28):
                        par = ffc % 2

                        @block.sync
                        def _(y, e=e, ffc=ffc, par=par):
                            p.w(y, "eT")
                            p.inc("dY", y.dma_start(
                                out=pan1[:, 1024 * par:1024 * (par + 1)].rearrange("p (c n) -> p c n", c=8),
                                in_=w1t_a[e].rearrange("(c p) n -> p c n", p=128)[:, :, 128 * ffc:128 * (ffc + 1)]))
                            p.inc("dY", y.dma_start(
                                out=pan3[:, 1024 * par:1024 * (par + 1)].rearrange("p (c n) -> p c n", c=8),
                                in_=w3t_a[e].rearrange("(c p) n -> p c n", p=128)[:, :, 128 * ffc:128 * (ffc + 1)]))

                        @block.tensor
                        def _(t_, e=e, mt=mt, ffc=ffc, par=par, tok0=tok0):
                            p.w(t_, "dY")
                            p.w(t_, "eS")
                            p.w(t_, "eV")
                            for c in range(8):
                                t_.matmul(ps_up[:, 512 * 0:512 * 0 + 512] if False else ps_up[:, 0:512],
                                          pan1[:, 1024 * par + 128 * c:1024 * par + 128 * (c + 1)],
                                          hnfm[:, c * NT + tok0:c * NT + tok0 + 512],
                                          start=(c == 0), stop=(c == 7))
                            for c in range(8):
                                if c == 7:
                                    t_.wait_ge(p.sems["eT"], p.cnt["eT"])
                                ins = t_.matmul(ps_up[:, 512:1024],
                                                pan3[:, 1024 * par + 128 * c:1024 * par + 128 * (c + 1)],
                                                hnfm[:, c * NT + tok0:c * NT + tok0 + 512],
                                                start=(c == 0), stop=(c == 7))
                            p.inc("eT", ins)

                        @block.scalar
                        def _(s, ffc=ffc):
                            p.w(s, "eT")
                            s.activation(u1f[:], ps_up[:, 0:512], AF.Silu)
                            s.drain()
                            p.inc("eS", s.nop())

                        @block.vector
                        def _(v, ffc=ffc):
                            p.w(v, "eS")
                            v.tensor_mul(u1f[:], u1f[:], ps_up[:, 512:1024])
                            v.drain()
                            if noscale:
                                v.tensor_copy(up_sb[:, 512 * ffc:512 * (ffc + 1)], u1f[:])
                            else:
                                v.tensor_mul(up_sb[:, 512 * ffc:512 * (ffc + 1)], u1f[:], rwb[:])
                            v.drain()
                            p.inc("eV", v.nop())

                    for dc in range(0 if now2 else 8):
                        par = dc % 2

                        @block.sync
                        def _(y, e=e, dc=dc, par=par):
                            p.w(y, "eV")
                            p.w(y, "eT")
                            p.inc("dY", y.dma_start(
                                out=pan2[:].rearrange("p (c n) -> p c n", c=28),
                                in_=w2t_a[e].rearrange("(c p) n -> p c n", p=128)[:, :, 128 * dc:128 * (dc + 1)]))

                        @block.tensor
                        def _(t_, e=e, mt=mt, dc=dc, par=par, tok0=tok0):
                            p.w(t_, "dY")
                            p.w(t_, "eV")
                            for ffc in range(28):
                                if ffc == 27:
                                    t_.wait_ge(p.sems["eT"], p.cnt["eT"])
                                ins = t_.matmul(ps_m[:, 512 * (dc % 2):512 * (dc % 2) + 512],
                                                pan2[:, 128 * ffc:128 * (ffc + 1)],
                                                up_sb[:, 512 * ffc:512 * (ffc + 1)],
                                                start=(ffc == 0), stop=(ffc == 27))
                            p.inc("eT", ins)

                        @block.vector
                        def _(v, e=e, dc=dc, tok0=tok0):
                            p.w(v, "eT")
                            acc = moe_acc[:, dc * NT + tok0:dc * NT + tok0 + 512]
                            v.tensor_add(acc, acc, ps_m[:, 512 * (dc % 2):512 * (dc % 2) + 512])
                            v.drain()
                            p.inc("eV", v.nop())
            fence(p, block)
            e4.close()

            fin = contextlib.ExitStack()
            moet = fin.enter_context(nc.sbuf_tensor("moet", [128, D], BF16))
            hmo = fin.enter_context(nc.sbuf_tensor("hmo", [128, D], F32))
            yt4 = fin.enter_context(nc.sbuf_tensor("yt4", [128, D], F32))
            outt = fin.enter_context(nc.sbuf_tensor("outt", [128, 2 * D], F32))
            ps_t7 = fin.enter_context(nc.psum_tensor("ps_t7", [128, 2048], BF16))

            if dbg == "h":
                @block.gpsimd
                def _(g):
                    p.w(g, "eV")
                    p.inc("dG", g.dma_start(out=out_o.rearrange("(t p) d -> p t d", p=128),
                                            in_=h_sb[:].rearrange("p (t d) -> p t d", t=NTILE)))
                    p.w(g, "dG")
            if dbg == "macc":
                @block.gpsimd
                def _(g):
                    p.w(g, "eV")
                    for c in range(8):
                        for j in range(2):
                            p.inc("dG", g.dma_start(
                                out=out_o[128 * (2 * c + j):128 * (2 * c + j + 1), :],
                                in_=moe_acc[:, c * NT + 1024 * j:c * NT + 1024 * (j + 1)]))
                    p.w(g, "dG")
            if dbg == "rw":
                @block.gpsimd
                def _(g):
                    p.w(g, "eV")
                    p.inc("dG", g.dma_start(out=out_o[0:8, :], in_=rwT[0:8, 0:1024]))
                    p.inc("dG", g.dma_start(out=out_o[8:16, :], in_=rwT[0:8, 1024:2048]))
                    p.w(g, "dG")
            for t in range(0 if dbg in ("h", "hn", "rw", "macc") else NTILE):
                @block.sync
                def _(y, t=t):
                    p.w(y, "eG")
                    p.w(y, "eV")
                    p.inc("dY", y.dma_start(out=x3st[:],
                                            in_=x3_i[128 * t:128 * (t + 1), :]))

                jobs = [(moe_acc[:, c * NT + 128 * t:c * NT + 128 * (t + 1)],
                         moet[:, 128 * c:128 * (c + 1)]) for c in range(8)]
                tp_sched(p, block, jobs, ps_t7, id_b[:])

                @block.vector
                def _(v, t=t):
                    p.w(v, "eV")
                    v.tensor_add(hmo[:], h_sb[:, t * D:(t + 1) * D], moet[:])
                    v.drain()
                    p.inc("eV", v.nop())

                seq_ln(p, block, hmo[:], yt4[:], scr, p.const(1e-5))

                @block.vector
                def _(v, t=t):
                    p.w(v, "eS")
                    p.w(v, "dY")
                    if dbg == "moe":
                        v.tensor_copy(outt[:, (t % 2) * D:(t % 2) * D + D], moet[:])
                    else:
                        v.tensor_add(outt[:, (t % 2) * D:(t % 2) * D + D],
                                     x3st[:], yt4[:])
                    v.drain()
                    p.inc("eV", v.nop())

                @block.gpsimd
                def _(g, t=t):
                    p.w(g, "eV")
                    p.inc("dG", g.dma_start(out=out_o[128 * t:128 * (t + 1), :],
                                            in_=outt[:, (t % 2) * D:(t % 2) * D + D]))

            @block.gpsimd
            def _(g):
                p.w(g, "dG")
            fin.close()
            e3.close()
    return p.nc


def kernel_multi(L0_attn_query, L0_attn_W_Q, L0_attn_W_K, L0_attn_W_V, L0_attn_W_O,
           L0_ffn_U1, L0_ffn_V1, L0_ffn_b1, L0_ffn_U2, L0_ffn_V2, L0_ffn_b2,
           L1_attn_ema_out, L1_attn_logit_decay, L1_attn_log_tau, L1_attn_log_blend,
           L1_attn_logit_blend_seq, L1_ffn_w_q, L1_ffn_w_k, L1_ffn_w_v, L1_ffn_w_o,
           L1_ffn_g1, L1_ffn_g2, L1_ffn_w_gate, L1_ffn_w1, L1_ffn_w3, L1_ffn_w2):
    f32 = np.float32
    X = np.asarray(L0_attn_query, f32)
    iden = np.eye(128, dtype=f32)
    tric = np.triu(np.ones((128, 128), dtype=f32))
    tris = np.tril(np.ones((128, 128), dtype=f32), -1)
    tri_inc = np.tril(np.ones((128, 128), dtype=f32))
    mneg = ((1.0 - tri_inc) * np.float32(-1e30)).astype(f32)
    oneh = np.zeros((8, 1024), dtype=f32)
    for e in range(8):
        oneh[e, 128 * e:128 * (e + 1)] = 1.0

    def shard(c):
        b, h = c // 2, c % 2
        return np.ascontiguousarray(X[b, h * NT:(h + 1) * NT])

    wqk = np.ascontiguousarray(np.concatenate(
        [np.asarray(L0_attn_W_Q, f32).reshape(16, D).T,
         np.asarray(L0_attn_W_K, f32).reshape(16, D).T], axis=1))
    wv_l0 = np.ascontiguousarray(np.asarray(L0_attn_W_V, f32).transpose(1, 0, 2).reshape(D, D))
    ncA = build_A()
    resA = run_bass_kernel_spmd(
        ncA, [{"xs": shard(c), "wqk": wqk, "wv": wv_l0, "iden": iden} for c in range(8)],
        core_ids=list(range(8)))
    rA = resA.results

    mo = np.ascontiguousarray(np.asarray(L0_attn_W_O, f32).transpose(2, 1, 0).reshape(D, D))
    b1h = np.ascontiguousarray(np.asarray(L0_ffn_b1, f32).reshape(32, 128).T)
    zeros_tot = np.zeros((1, 2080), dtype=f32)
    ncB = build_B()
    inB = []
    for c in range(8):
        b, h = c // 2, c % 2
        totp = np.asarray(rA[2 * b]["tot"], f32) if h == 1 else zeros_tot
        inB.append({"xs": shard(c), "trig": np.asarray(rA[c]["trig"], f32),
                    "vtok": rA[c]["vtok"], "totp": totp, "tric": tric, "iden": iden,
                    "mo": mo, "u1": np.asarray(L0_ffn_U1, f32),
                    "v1": np.asarray(L0_ffn_V1, f32), "b1": b1h,
                    "u2": np.asarray(L0_ffn_U2, f32), "v2": np.asarray(L0_ffn_V2, f32),
                    "b2": np.asarray(L0_ffn_b2, f32)[None]})
    resB = run_bass_kernel_spmd(ncB, inB, core_ids=list(range(8)))
    rB = resB.results

    ema8 = np.ascontiguousarray(np.asarray(L1_attn_ema_out, f32).reshape(8, 128).T)
    sc3 = np.array([[float(np.asarray(L1_attn_log_tau)), float(np.asarray(L1_attn_log_blend)),
                     float(np.asarray(L1_attn_logit_blend_seq))]], dtype=f32)
    g1r = np.asarray(L1_ffn_g1, f32)[None]
    w1 = np.asarray(L1_ffn_w1, f32)
    w3 = np.asarray(L1_ffn_w3, f32)
    w2 = np.asarray(L1_ffn_w2, f32)
    rC = [None] * 8
    for h in (0, 1):
        cores = [c for c in range(8) if c % 2 == h]
        ncC = build_C(h)
        inC = []
        for c in cores:
            b = c // 2
            m0c = ((np.arange(128) == 0).astype(f32)[:, None]
                   if h == 0 else np.zeros((128, 1), dtype=f32))
            m0 = np.ascontiguousarray(np.concatenate([m0c, 1.0 - m0c], axis=1))
            poscol = np.ascontiguousarray(
                (h * NT + np.arange(NT).reshape(16, 128).T).astype(f32))
            inC.append({"x2": np.asarray(rB[c]["x2"], f32), "outg": rB[c]["outg"],
                        "onfm": rB[c]["onfm"], "onfmp": rB[2 * b]["onfm"],
                        "ema8": ema8, "sc3": sc3, "tris": tris, "m0": m0,
                        "poscol": poscol, "g1r": g1r, "iden": iden,
                        "wq": np.asarray(L1_ffn_w_q, f32), "wk": np.asarray(L1_ffn_w_k, f32),
                        "wv": np.asarray(L1_ffn_w_v, f32), "wo": np.asarray(L1_ffn_w_o, f32),
                        "w1e": w1[c], "w3e": w3[c], "w2e": w2[c]})
        res = run_bass_kernel_spmd(ncC, inC, core_ids=list(range(len(cores))))
        for i, c in enumerate(cores):
            rC[c] = res.results[i]

    w1t_a = np.ascontiguousarray(np.stack([np.asarray(rC[e]["w1t"]) for e in range(8)]))
    w3t_a = np.ascontiguousarray(np.stack([np.asarray(rC[e]["w3t"]) for e in range(8)]))
    w2t_a = np.ascontiguousarray(np.stack([np.asarray(rC[e]["w2t"]) for e in range(8)]))
    g2r = np.asarray(L1_ffn_g2, f32)[None]
    wgT = np.ascontiguousarray(np.asarray(L1_ffn_w_gate, f32).T)
    rE = [None] * 8
    for h in (0, 1):
        cores = [c for c in range(8) if c % 2 == h]
        ncE = build_E(h)
        inE = []
        for c in cores:
            b = c // 2
            kv = (np.asarray(rC[c]["kv"]) if h == 0 else
                  np.ascontiguousarray(np.concatenate(
                      [np.asarray(rC[2 * b]["kv"]), np.asarray(rC[c]["kv"])], axis=0)))
            inE.append({"x3": np.asarray(rC[c]["x3"], f32), "qfm": rC[c]["qfm"], "kv": kv,
                        "woqt": rC[c]["woqt"], "g2r": g2r, "wgT": wgT,
                        "tri_inc": tri_inc, "mneg": mneg, "iden": iden, "oneh": oneh,
                        "w1t_a": w1t_a, "w3t_a": w3t_a, "w2t_a": w2t_a})
        res = run_bass_kernel_spmd(ncE, inE, core_ids=list(range(len(cores))))
        for i, c in enumerate(cores):
            rE[c] = res.results[i]

    out = np.empty((4, 4096, D), dtype=f32)
    for c in range(8):
        b, h = c // 2, c % 2
        out[b, h * NT:(h + 1) * NT] = np.asarray(rE[c]["out"], f32)
    return out


def build_full():
    p = P()
    nc = p.nc
    f = nc.dram_tensor
    xs = f("xs", [NT, D], F32, kind="ExternalInput").ap()
    wqk = f("wqk", [D, 32], F32, kind="ExternalInput").ap()
    wvl0 = f("wvl0", [D, D], F32, kind="ExternalInput").ap()
    iden = f("iden", [128, 128], F32, kind="ExternalInput").ap()
    tric = f("tric", [128, 128], F32, kind="ExternalInput").ap()
    tris = f("tris", [128, 128], F32, kind="ExternalInput").ap()
    tri_inc = f("tri_inc", [128, 128], F32, kind="ExternalInput").ap()
    mneg_i = f("mneg", [128, 128], F32, kind="ExternalInput").ap()
    mo = f("mo", [D, D], F32, kind="ExternalInput").ap()
    u1 = f("u1", [D, 256], F32, kind="ExternalInput").ap()
    v1 = f("v1", [256, 4096], F32, kind="ExternalInput").ap()
    b1 = f("b1", [128, 32], F32, kind="ExternalInput").ap()
    u2 = f("u2", [4096, 256], F32, kind="ExternalInput").ap()
    v2 = f("v2", [256, D], F32, kind="ExternalInput").ap()
    b2 = f("b2", [1, D], F32, kind="ExternalInput").ap()
    ema8 = f("ema8", [128, 8], F32, kind="ExternalInput").ap()
    sc3 = f("sc3", [1, 3], F32, kind="ExternalInput").ap()
    m0 = f("m0", [128, 2], F32, kind="ExternalInput").ap()
    poscol = f("poscol", [128, 16], F32, kind="ExternalInput").ap()
    g1r = f("g1r", [1, D], F32, kind="ExternalInput").ap()
    g2r = f("g2r", [1, D], F32, kind="ExternalInput").ap()
    wgT_i = f("wgT", [D, 8], F32, kind="ExternalInput").ap()
    oneh_i = f("oneh", [8, 1024], F32, kind="ExternalInput").ap()
    wq_i = f("wq", [D, D], F32, kind="ExternalInput").ap()
    wk_i = f("wk", [256, D], F32, kind="ExternalInput").ap()
    wv_i = f("wv", [256, D], F32, kind="ExternalInput").ap()
    wo_i = f("wo", [D, D], F32, kind="ExternalInput").ap()
    w1e_i = f("w1e", [3584, D], F32, kind="ExternalInput").ap()
    w3e_i = f("w3e", [3584, D], F32, kind="ExternalInput").ap()
    w2e_i = f("w2e", [D, 3584], F32, kind="ExternalInput").ap()
    mh_i = f("mh", [1, 2080], F32, kind="ExternalInput").ap()
    flagc_i = f("flagc", [128, 2], F32, kind="ExternalInput").ap()
    flagr_i = f("flagr", [1, 512], F32, kind="ExternalInput").ap()
    out_o = f("out", [NT, D], F32, kind="ExternalOutput").ap()

    cc_tot_in = f("cc_tot_in", [1, 2080], F32).ap()
    cc_tot_out = f("cc_tot_out", [2, 2080], F32).ap()
    cc_on_in = f("cc_on_in", [D, NT], BF16).ap()
    cc_on_out = f("cc_on_out", [2 * D, NT], BF16).ap()
    cc_kv_in = f("cc_kv_in", [NT, 512], BF16).ap()
    cc_kv_out = f("cc_kv_out", [2 * NT, 512], BF16).ap()
    MWR = (D * 3584 * 2 + 3584 * D) // 128
    cc_mw_in = f("cc_mw_in", [MWR, 128], BF16).ap()
    cc_mw_out = f("cc_mw_out", [8 * MWR, 128], BF16, addr_space="Shared").ap()
    x3_dram = f("x3_dram", [NT, D], F32).ap()
    qfm_dram = f("qfm_dram", [D, NT], BF16).ap()
    wo_dram = f("wo_dram", [D, D], BF16).ap()
    RG2 = [[0, 1], [2, 3], [4, 5], [6, 7]]
    RG8 = [list(range(8))]

    for s_ in ("dY", "dG", "dCC", "eV", "eS", "eT", "eG", "eY"):
        p.sem(s_)

    id_f = p.sb("id_f", [128, 128], F32)
    id_b = p.sb("id_b", [128, 128], BF16)
    ones_row = p.sb("ones_row", [1, 128], F32)
    ones_col = p.sb("ones_col", [128, 1], F32)
    ones8 = p.sb("ones8", [128, 8], F32)
    _sq = p.sb("ssq0", [128, D], F32)[:]
    scr = {"sq": [_sq, _sq],
           "m": [p.sb("sm0", [128, 1], F32)[:], p.sb("sm1", [128, 1], F32)[:]],
           "vv": [p.sb("sv0", [128, 1], F32)[:], p.sb("sv1", [128, 1], F32)[:]],
           "r": [p.sb("sr0", [128, 1], F32)[:], p.sb("sr1", [128, 1], F32)[:]]}
    ISQ = 1.0 / float(np.sqrt(128.0))
    TWO_PI = 2.0 * PI

    with p.st:
        with nc.Block() as block:
            p.add_consts(block, [1e-5])

            sx = contextlib.ExitStack()
            x_sb = sx.enter_context(nc.sbuf_tensor("x_sb", [128, NTILE * D], F32))
            tot_sb = sx.enter_context(nc.sbuf_tensor("tot_sb", [1, 2080], F32))
            sVT = contextlib.ExitStack()
            v_sb = sVT.enter_context(nc.sbuf_tensor("v_sb", [128, NTILE * D], BF16))
            trig_sb = sVT.enter_context(nc.sbuf_tensor("trig_sb", [128, NTILE * 64], F32))

            sA = contextlib.ExitStack()
            ln_bf = sA.enter_context(nc.sbuf_tensor("ln_bf", [128, NTILE * D], BF16))
            lfm = sA.enter_context(nc.sbuf_tensor("lfm", [128, 8 * NT], BF16))
            qk_sb = sA.enter_context(nc.sbuf_tensor("qk_sb", [128, 32], F32))
            cqa = sA.enter_context(nc.sbuf_tensor("cqa", [128, 32], F32))
            pch = sA.enter_context(nc.sbuf_tensor("pch", [128, 2080], F32))
            wqk_sb = sA.enter_context(nc.sbuf_tensor("wqk_sb", [128, 8 * 32], BF16))
            wv_sb = sA.enter_context(nc.sbuf_tensor("wv_sb", [128, 8 * D], BF16))
            psA = contextlib.ExitStack()
            ps_tr = psA.enter_context(nc.psum_tensor("ps_tr", [128, 2048], BF16))
            psA2 = contextlib.ExitStack()
            ps_qk = psA2.enter_context(nc.psum_tensor("ps_qk", [128, 32], F32))
            ps_v = psA2.enter_context(nc.psum_tensor("ps_v", [128, 1024], F32))

            @block.sync
            def _(y):
                p.inc("dY", y.dma_start(out=x_sb[:].rearrange("p (t d) -> p t d", t=NTILE),
                                        in_=xs.rearrange("(t p) d -> p t d", p=128)))
                p.inc("dY", y.dma_start(out=id_f[:], in_=iden))

            @block.gpsimd
            def _(g):
                p.inc("dG", g.dma_start(
                    out=wqk_sb[:].rearrange("p (c n) -> p c n", c=8),
                    in_=wqk.rearrange("(c p) n -> p c n", p=128)))
                p.inc("dG", g.dma_start(
                    out=wv_sb[:].rearrange("p (c n) -> p c n", c=8),
                    in_=wvl0.rearrange("(c p) n -> p c n", p=128)))

            @block.vector
            def _(v):
                p.w(v, "dY")
                v.tensor_copy(id_b[:], id_f[:])
                v.memset(ones_col[:], 1.0)
                v.memset(ones_row[:], 1.0)
                v.memset(ones8[:], 1.0)
                v.drain()
                p.inc("eV", v.nop())

            def getx(t):
                return x_sb[:, t * D:(t + 1) * D]

            def getln(t):
                return ln_bf[:, t * D:(t + 1) * D]

            def wait_in(eng, t):
                if t == 0:
                    p.w(eng, "dY")
                    p.w(eng, "dG")

            ln_sched(p, block, NTILE, getx, getln, scr, p.const(1e-5), wait_fn=wait_in)
            fence(p, block, "vst")

            jobs = []
            for t in range(NTILE):
                for c in range(8):
                    jobs.append((ln_bf[:, t * D + 128 * c: t * D + 128 * (c + 1)],
                                 lfm[:, c * NT + 128 * t: c * NT + 128 * (t + 1)]))
            tp_sched(p, block, jobs, ps_tr, id_b[:])
            fence(p, block, "vt")

            sT0 = p.cnt["eT"]
            sV0 = p.cnt["eV"]
            sS0 = p.cnt["eS"]

            @block.tensor
            def _(t_):
                for t in range(NTILE):
                    if t >= 1:
                        t_.wait_ge(p.sems["eS"], sS0 + 2 * t - 1)
                        t_.wait_ge(p.sems["eV"], sV0 + 2 * t)
                    for c in range(8):
                        if c == 7:
                            t_.wait_ge(p.sems["eT"], p.cnt["eT"])
                        ins = t_.matmul(ps_qk[:],
                                        lfm[:, c * NT + 128 * t: c * NT + 128 * (t + 1)],
                                        wqk_sb[:, c * 32:(c + 1) * 32],
                                        start=(c == 0), stop=(c == 7))
                    ins.then_inc(p.sems["eT"], 1)
                    for n in range(2):
                        for c in range(8):
                            if n == 1 and c == 7:
                                t_.wait_ge(p.sems["eT"], sT0 + 2 * t + 1)
                            ins = t_.matmul(ps_v[:, 512 * n:512 * (n + 1)],
                                            lfm[:, c * NT + 128 * t: c * NT + 128 * (t + 1)],
                                            wv_sb[:, c * D + 512 * n: c * D + 512 * n + 512],
                                            start=(c == 0), stop=(c == 7))
                    ins.then_inc(p.sems["eT"], 1)

            @block.scalar
            def _(s):
                for t in range(NTILE):
                    s.wait_ge(p.sems["eT"], sT0 + 2 * t + 1)
                    if t >= 1:
                        s.wait_ge(p.sems["eV"], sV0 + 2 * t - 1)
                    s.activation(qk_sb[:], ps_qk[:], AF.Tanh)
                    s.drain()
                    s.nop().then_inc(p.sems["eS"], 1)
                    s.wait_ge(p.sems["eV"], sV0 + 2 * t + 1)
                    tr = trig_sb[:, 64 * t:64 * (t + 1)]
                    s.activation(tr[:, 0:16], cqa[:, 0:16], AF.Sin)
                    s.activation(tr[:, 16:32], qk_sb[:, 0:16], AF.Sin, scale=PI / 4)
                    s.activation(tr[:, 32:48], cqa[:, 16:32], AF.Sin)
                    s.activation(tr[:, 48:64], qk_sb[:, 16:32], AF.Sin, scale=PI / 4)
                    s.drain()
                    s.nop().then_inc(p.sems["eS"], 1)

            @block.vector
            def _(v):
                for t in range(NTILE):
                    v.wait_ge(p.sems["eS"], sS0 + 2 * t + 1)
                    v.tensor_scalar(cqa[:], qk_sb[:], PI / 4, PI / 2, ALU.mult, ALU.add)
                    v.drain()
                    v.nop().then_inc(p.sems["eV"], 1)
                    v.wait_ge(p.sems["eT"], sT0 + 2 * t + 2)
                    v.tensor_copy(v_sb[:, t * D:(t + 1) * D], ps_v[:])
                    v.drain()
                    v.nop().then_inc(p.sems["eV"], 1)

            p.cnt["eT"] = sT0 + 2 * NTILE
            p.cnt["eS"] = sS0 + 2 * NTILE
            p.cnt["eV"] = sV0 + 2 * NTILE
            fence(p, block, "vst")
            psA2.close()
            ps_tt = psA.enter_context(nc.psum_tensor("ps_tt", [1, 512 * 4], F32))
            ps_t2x = psA.enter_context(nc.psum_tensor("ps_t2x", [1, 32], F32))

            sV0 = p.cnt["eV"]
            sT0 = p.cnt["eT"]

            @block.vector
            def _(v):
                for t in range(NTILE):
                    if t >= 1:
                        v.wait_ge(p.sems["eT"], sT0 + t)
                    tr = trig_sb[:, 64 * t:64 * (t + 1)]
                    vt = v_sb[:, t * D:(t + 1) * D]
                    for h in range(16):
                        v.tensor_scalar(pch[:, 64 * h:64 * (h + 1)],
                                        vt[:, 64 * h:64 * (h + 1)],
                                        tr[:, 32 + h:33 + h], None, ALU.mult)
                        v.tensor_scalar(pch[:, 1024 + 64 * h:1024 + 64 * (h + 1)],
                                        vt[:, 64 * h:64 * (h + 1)],
                                        tr[:, 48 + h:49 + h], None, ALU.mult)
                    v.tensor_copy(pch[:, 2048:2064], tr[:, 32:48])
                    v.tensor_copy(pch[:, 2064:2080], tr[:, 48:64])
                    v.drain()
                    v.nop().then_inc(p.sems["eV"], 1)

            @block.tensor
            def _(t_):
                for t in range(NTILE):
                    t_.wait_ge(p.sems["eV"], sV0 + t + 1)
                    for j in range(4):
                        t_.matmul(ps_tt[0:1, 512 * j:512 * (j + 1)], ones_col[:],
                                  pch[:, 512 * j:512 * (j + 1)],
                                  start=(t == 0), stop=(t == NTILE - 1))
                    t_.wait_ge(p.sems["eT"], p.cnt["eT"])
                    t_.matmul(ps_t2x[0:1, :], ones_col[:], pch[:, 2048:2080],
                              start=(t == 0), stop=(t == NTILE - 1)).then_inc(p.sems["eT"], 1)

            p.cnt["eV"] = sV0 + NTILE
            p.cnt["eT"] = sT0 + NTILE
            fence(p, block, "vt")

            @block.vector
            def _(v):
                v.tensor_copy(tot_sb[0:1, 0:2048], ps_tt[0:1, :])
                v.tensor_copy(tot_sb[0:1, 2048:2080], ps_t2x[0:1, :])
                v.drain()
                p.inc("eV", v.nop())
            fence(p, block, "vtg")
            psA.close()
            sA.close()

            sB = contextlib.ExitStack()
            carry = sB.enter_context(nc.sbuf_tensor("carry", [1, 2080], F32))
            mh_sb = sB.enter_context(nc.sbuf_tensor("mh_sb", [1, 2080], F32))

            @block.gpsimd
            def _(g):
                p.w(g, "eV")
                p.inc("dG", g.dma_start(out=cc_tot_in, in_=tot_sb[:]))
                p.w(g, "dG")
                p.inc("dCC", g.collective_compute(
                    "AllGather", ALU.bypass, replica_groups=RG2,
                    ins=[cc_tot_in.opt()], outs=[cc_tot_out.opt()]), 16)
                p.w(g, "dCC")
                p.inc("dG", g.dma_start(out=carry[:], in_=cc_tot_out[0:1, :]))
                p.inc("dG", g.dma_start(out=mh_sb[:], in_=mh_i))

            @block.vector
            def _(v):
                p.w(v, "dG")
                v.tensor_mul(carry[:], carry[:], mh_sb[:])
                v.drain()
                p.inc("eV", v.nop())
            fence(p, block)
            tri_f = sB.enter_context(nc.sbuf_tensor("tri_f", [128, 128], F32))
            pch2 = sB.enter_context(nc.sbuf_tensor("pch2", [128, 2080], F32))
            cum_sb = sB.enter_context(nc.sbuf_tensor("cum_sb", [128, 2080], F32))
            cnew = sB.enter_context(nc.sbuf_tensor("cnew", [1, 2080], F32))
            heads = sB.enter_context(nc.sbuf_tensor("heads", [128, D], F32))
            heads_b = sB.enter_context(nc.sbuf_tensor("heads_b", [128, D], BF16))
            hfm = sB.enter_context(nc.sbuf_tensor("hfm", [128, D], BF16))
            dnm = sB.enter_context(nc.sbuf_tensor("dnm", [128, 16], F32))
            dn2 = sB.enter_context(nc.sbuf_tensor("dn2", [128, 16], F32))
            attn = sB.enter_context(nc.sbuf_tensor("attn", [128, D], F32))
            yt = sB.enter_context(nc.sbuf_tensor("yt", [128, D], F32))
            mo_sb = sB.enter_context(nc.sbuf_tensor("mo_sb", [128, 8 * D], BF16))
            psB = contextlib.ExitStack()
            ps_cum = psB.enter_context(nc.psum_tensor("ps_cum", [128, 1024], F32))
            ps_car = psB.enter_context(nc.psum_tensor("ps_car", [1, 1024], F32))
            ps_trB = psB.enter_context(nc.psum_tensor("ps_trB", [128, 1024], BF16))
            ps_at = psB.enter_context(nc.psum_tensor("ps_at", [128, 1024], F32))

            @block.sync
            def _(y):
                p.inc("dY", y.dma_start(out=tri_f[:], in_=tric))

            @block.gpsimd
            def _(g):
                p.inc("dG", g.dma_start(
                    out=mo_sb[:].rearrange("p (c n) -> p c n", c=8),
                    in_=mo.rearrange("(c p) n -> p c n", p=128)))
            fence(p, block, "vtgy")

            ROUNDS = [(0, 1024), (1024, 1024), (2048, 32)]
            for t in range(NTILE):
                tr = trig_sb[:, 64 * t:64 * (t + 1)]
                vt = v_sb[:, t * D:(t + 1) * D]

                @block.vector
                def _(v, tr=tr, vt=vt):
                    p.w(v, "eT")
                    for h in range(16):
                        v.tensor_scalar(pch2[:, 64 * h:64 * (h + 1)], vt[:, 64 * h:64 * (h + 1)],
                                        tr[:, 32 + h:33 + h], None, ALU.mult)
                        v.tensor_scalar(pch2[:, 1024 + 64 * h:1024 + 64 * (h + 1)],
                                        vt[:, 64 * h:64 * (h + 1)],
                                        tr[:, 48 + h:49 + h], None, ALU.mult)
                    v.tensor_copy(pch2[:, 2048:2064], tr[:, 32:48])
                    v.tensor_copy(pch2[:, 2064:2080], tr[:, 48:64])
                    v.drain()
                    v.nop().then_inc(p.sems["eV"], 1)
                    p.cnt["eV"] += 1

                for (lo, wdt) in ROUNDS:
                    @block.tensor
                    def _(t_, lo=lo, wdt=wdt):
                        p.w(t_, "eV")
                        nch = (wdt + 511) // 512
                        for j in range(nch):
                            w_ = min(512, wdt - 512 * j)
                            t_.matmul(ps_cum[:, 512 * j:512 * j + w_], tri_f[:],
                                      pch2[:, lo + 512 * j: lo + 512 * j + w_],
                                      start=True, stop=False)
                            t_.matmul(ps_cum[:, 512 * j:512 * j + w_], ones_row[:],
                                      carry[0:1, lo + 512 * j: lo + 512 * j + w_],
                                      start=False, stop=True)
                        for j in range(nch):
                            w_ = min(512, wdt - 512 * j)
                            if j == nch - 1:
                                t_.wait_ge(p.sems["eT"], p.cnt["eT"])
                            ins = t_.matmul(ps_car[0:1, 512 * j:512 * j + w_], ones_col[:],
                                            pch2[:, lo + 512 * j: lo + 512 * j + w_],
                                            start=True, stop=True)
                        p.inc("eT", ins)

                    @block.vector
                    def _(v, lo=lo, wdt=wdt):
                        p.w(v, "eT")
                        v.tensor_copy(cum_sb[:, lo:lo + wdt], ps_cum[:, 0:wdt])
                        v.tensor_add(cnew[0:1, lo:lo + wdt], carry[0:1, lo:lo + wdt],
                                     ps_car[0:1, 0:wdt])
                        v.drain()
                        v.nop().then_inc(p.sems["eV"], 1)
                        p.cnt["eV"] += 1

                @block.vector
                def _(v, tr=tr, t=t):
                    v.tensor_copy(carry[:], cnew[:])
                    v.drain()
                    v.tensor_mul(dnm[:], tr[:, 0:16], cum_sb[:, 2048:2064])
                    v.tensor_mul(dn2[:], tr[:, 16:32], cum_sb[:, 2064:2080])
                    v.drain()
                    v.tensor_add(dnm[:], dnm[:], dn2[:])
                    v.drain()
                    v.tensor_scalar(dnm[:], dnm[:], 1e-6, None, ALU.add)
                    v.drain()
                    v.reciprocal(dn2[:], dnm[:])
                    v.drain()
                    for h in range(16):
                        hc = heads[:, 64 * h:64 * (h + 1)]
                        v.tensor_scalar(hc, cum_sb[:, 64 * h:64 * (h + 1)],
                                        tr[:, 0 + h:1 + h], None, ALU.mult)
                        v.drain()
                        v.scalar_tensor_tensor(hc, cum_sb[:, 1024 + 64 * h:1024 + 64 * (h + 1)],
                                               tr[:, 16 + h:17 + h], hc, ALU.mult, ALU.add)
                        v.drain()
                        v.tensor_scalar(heads_b[:, 64 * h:64 * (h + 1)], hc,
                                        dn2[:, h:h + 1], None, ALU.mult)
                    v.drain()
                    v.nop().then_inc(p.sems["eV"], 1)
                    p.cnt["eV"] += 1

                jobs = [(heads_b[:, 128 * c:128 * (c + 1)], hfm[:, 128 * c:128 * (c + 1)])
                        for c in range(8)]
                tp_sched(p, block, jobs, ps_trB, id_b[:])

                @block.tensor
                def _(t_):
                    p.w(t_, "eV")
                    for n in range(2):
                        for c in range(8):
                            if n == 1 and c == 7:
                                t_.wait_ge(p.sems["eT"], p.cnt["eT"])
                            ins = t_.matmul(ps_at[:, 512 * n:512 * (n + 1)],
                                            hfm[:, 128 * c:128 * (c + 1)],
                                            mo_sb[:, c * D + 512 * n:c * D + 512 * n + 512],
                                            start=(c == 0), stop=(c == 7))
                    p.inc("eT", ins)

                @block.vector
                def _(v):
                    p.w(v, "eT")
                    v.tensor_copy(attn[:], ps_at[:])
                    v.drain()
                    p.inc("eV", v.nop())

                seq_ln(p, block, attn[:], yt[:], scr, p.const(1e-5))

                @block.vector
                def _(v, t=t):
                    p.w(v, "eS")
                    xa = x_sb[:, t * D:(t + 1) * D]
                    v.tensor_add(xa, xa, yt[:])
                    v.drain()
                    p.inc("eV", v.nop())

            fence(p, block)
            psB.close()
            sB.close()
            sVT.close()

            be = contextlib.ExitStack()
            u1_sb = be.enter_context(nc.sbuf_tensor("u1_sb", [128, 8 * 256], BF16))
            v1_sb = be.enter_context(nc.sbuf_tensor("v1_sb", [128, 2 * 4096], BF16))
            u2_sb = be.enter_context(nc.sbuf_tensor("u2_sb", [128, 32 * 256], BF16))
            v2_sb = be.enter_context(nc.sbuf_tensor("v2_sb", [128, 2 * D], BF16))
            b1_sb = be.enter_context(nc.sbuf_tensor("b1_sb", [128, 32], F32))
            b2b = be.enter_context(nc.sbuf_tensor("b2b", [128, D], F32))
            b2r = be.enter_context(nc.sbuf_tensor("b2r", [1, D], F32))
            lnt = be.enter_context(nc.sbuf_tensor("lnt", [128, 512 * 8], BF16))
            hfm2 = be.enter_context(nc.sbuf_tensor("hfm2", [128, 32 * 512], BF16))
            pfm = be.enter_context(nc.sbuf_tensor("pfm", [128, 2 * 512], BF16))
            qfm2 = be.enter_context(nc.sbuf_tensor("qfm2", [128, 2 * 512], BF16))
            g0 = be.enter_context(nc.sbuf_tensor("g0", [128, 512], F32))
            g1s = be.enter_context(nc.sbuf_tensor("g1s", [128, 512], F32))
            g2s = be.enter_context(nc.sbuf_tensor("g2s", [128, 512], F32))
            lnb = be.enter_context(nc.sbuf_tensor("lnb", [128, 4 * D], BF16))
            ot = be.enter_context(nc.sbuf_tensor("ot", [128, 4 * D], F32))
            yt2 = be.enter_context(nc.sbuf_tensor("yt2", [128, D], F32))
            psBE = contextlib.ExitStack()
            ps_b = psBE.enter_context(nc.psum_tensor("ps_b", [128, 1024], F32))
            ps_h = psBE.enter_context(nc.psum_tensor("ps_h", [128, 1024], F32))
            ps_t2 = psBE.enter_context(nc.psum_tensor("ps_t2", [128, 2048], BF16))

            @block.gpsimd
            def _(g):
                p.inc("dG", g.dma_start(out=u1_sb[:].rearrange("p (c n) -> p c n", c=8),
                                        in_=u1.rearrange("(c p) n -> p c n", p=128)))
                p.inc("dG", g.dma_start(out=v1_sb[:].rearrange("p (c n) -> p c n", c=2),
                                        in_=v1.rearrange("(c p) n -> p c n", p=128)))
                p.inc("dG", g.dma_start(out=u2_sb[:].rearrange("p (c n) -> p c n", c=32),
                                        in_=u2.rearrange("(c p) n -> p c n", p=128)))
                p.inc("dG", g.dma_start(out=v2_sb[:].rearrange("p (c n) -> p c n", c=2),
                                        in_=v2.rearrange("(c p) n -> p c n", p=128)))

            @block.sync
            def _(y):
                p.inc("dY", y.dma_start(out=b1_sb[:], in_=b1))
                p.inc("dY", y.dma_start(out=b2r[:], in_=b2))

            @block.tensor
            def _(t_):
                p.w(t_, "dY")
                t_.wait_ge(p.sems["eT"], p.cnt["eT"])
                p.inc("eT", t_.matmul(ps_b[:, 0:512], ones_row[:], b2r[0:1, 0:512],
                                      start=True, stop=True))
                t_.wait_ge(p.sems["eT"], p.cnt["eT"])
                p.inc("eT", t_.matmul(ps_b[:, 512:1024], ones_row[:], b2r[0:1, 512:1024],
                                      start=True, stop=True))

            @block.vector
            def _(v):
                p.w(v, "eT")
                v.tensor_copy(b2b[:], ps_b[:])
                v.drain()
                p.inc("eV", v.nop())
            fence(p, block)

            for gtile in range(4):
                for u in range(4):
                    t128 = 4 * gtile + u
                    seq_ln(p, block, x_sb[:, t128 * D:(t128 + 1) * D],
                           lnb[:, u * D:(u + 1) * D], scr, p.const(1e-5))
                jobs = []
                for u in range(4):
                    for c in range(8):
                        jobs.append((lnb[:, u * D + 128 * c:u * D + 128 * (c + 1)],
                                     lnt[:, c * 512 + 128 * u:c * 512 + 128 * (u + 1)]))
                tp_sched(p, block, jobs, ps_t2, id_b[:])

                @block.tensor
                def _(t_):
                    p.w(t_, "eV")
                    for rc in range(2):
                        for c in range(8):
                            if rc == 1 and c == 7:
                                t_.wait_ge(p.sems["eT"], p.cnt["eT"])
                            ins = t_.matmul(ps_b[:, 512 * rc:512 * (rc + 1)],
                                            u1_sb[:, c * 256 + 128 * rc:c * 256 + 128 * (rc + 1)],
                                            lnt[:, c * 512:(c + 1) * 512],
                                            start=(c == 0), stop=(c == 7))
                    p.inc("eT", ins)

                @block.vector
                def _(v):
                    p.w(v, "eT")
                    v.tensor_copy(pfm[:], ps_b[:])
                    v.drain()
                    p.inc("eV", v.nop())

                for j in range(32):
                    @block.tensor
                    def _(t_, j=j):
                        p.w(t_, "eV")
                        par = j % 2
                        t_.wait_ge(p.sems["eT"], p.cnt["eT"])
                        for rc in range(2):
                            ins = t_.matmul(ps_h[:, 512 * par:512 * (par + 1)],
                                            v1_sb[:, rc * 4096 + 128 * j:rc * 4096 + 128 * (j + 1)],
                                            pfm[:, 512 * rc:512 * (rc + 1)],
                                            start=(rc == 0), stop=(rc == 1))
                        p.inc("eT", ins)

                    @block.vector
                    def _(v, j=j):
                        p.w(v, "eT")
                        par = j % 2
                        v.tensor_scalar(g0[:], ps_h[:, 512 * par:512 * (par + 1)],
                                        b1_sb[:, j:j + 1], None, ALU.add)
                        v.drain()
                        v.tensor_mul(g1s[:], g0[:], g0[:])
                        v.drain()
                        v.tensor_scalar(g1s[:], g1s[:], 0.044715, 1.0, ALU.mult, ALU.add)
                        v.drain()
                        v.tensor_mul(g1s[:], g1s[:], g0[:])
                        v.drain()
                        v.nop().then_inc(p.sems["eV"], 1)
                        p.cnt["eV"] += 1

                    @block.scalar
                    def _(s, j=j):
                        p.w(s, "eV")
                        s.activation(g2s[:], g1s[:], AF.Tanh, scale=GELU_C)
                        s.drain()
                        p.inc("eS", s.nop())

                    @block.vector
                    def _(v, j=j):
                        p.w(v, "eS")
                        v.tensor_scalar(g2s[:], g2s[:], 0.5, 0.5, ALU.mult, ALU.add)
                        v.drain()
                        v.tensor_mul(hfm2[:, 512 * j:512 * (j + 1)], g2s[:], g0[:])
                        v.drain()
                        v.nop().then_inc(p.sems["eV"], 1)
                        p.cnt["eV"] += 1

                @block.tensor
                def _(t_):
                    p.w(t_, "eV")
                    for rc in range(2):
                        for j in range(32):
                            if rc == 1 and j == 31:
                                t_.wait_ge(p.sems["eT"], p.cnt["eT"])
                            ins = t_.matmul(ps_b[:, 512 * rc:512 * (rc + 1)],
                                            u2_sb[:, j * 256 + 128 * rc:j * 256 + 128 * (rc + 1)],
                                            hfm2[:, 512 * j:512 * (j + 1)],
                                            start=(j == 0), stop=(j == 31))
                    p.inc("eT", ins)

                @block.vector
                def _(v):
                    p.w(v, "eT")
                    v.tensor_copy(qfm2[:], ps_b[:])
                    v.drain()
                    p.inc("eV", v.nop())

                for u in range(4):
                    @block.tensor
                    def _(t_, u=u):
                        p.w(t_, "eV")
                        for n in range(2):
                            for rc in range(2):
                                if n == 1 and rc == 1:
                                    t_.wait_ge(p.sems["eT"], p.cnt["eT"])
                                ins = t_.matmul(ps_h[:, 512 * n:512 * (n + 1)],
                                                qfm2[:, 512 * rc + 128 * u:512 * rc + 128 * (u + 1)],
                                                v2_sb[:, rc * D + 512 * n:rc * D + 512 * n + 512],
                                                start=(rc == 0), stop=(rc == 1))
                        p.inc("eT", ins)

                    @block.vector
                    def _(v, u=u):
                        p.w(v, "eT")
                        v.tensor_add(ot[:, u * D:(u + 1) * D], ps_h[:], b2b[:])
                        v.drain()
                        p.inc("eV", v.nop())

                for u in range(4):
                    t128 = 4 * gtile + u
                    seq_ln(p, block, ot[:, u * D:(u + 1) * D], yt2[:], scr, p.const(1e-5))

                    @block.vector
                    def _(v, t128=t128):
                        p.w(v, "eS")
                        xa = x_sb[:, t128 * D:(t128 + 1) * D]
                        v.tensor_add(xa, xa, yt2[:])
                        v.drain()
                        p.inc("eV", v.nop())

            fence(p, block)
            psBE.close()
            be.close()

            ga = contextlib.ExitStack()
            outg_sb = ga.enter_context(nc.sbuf_tensor("outg_sb", [128, NTILE * D], BF16))
            onfm_sb = ga.enter_context(nc.sbuf_tensor("onfm_sb", [128, 8 * NT], BF16))
            lng = ga.enter_context(nc.sbuf_tensor("lng", [128, D], F32))
            gg0 = ga.enter_context(nc.sbuf_tensor("gg0", [128, D], F32))
            gg1 = ga.enter_context(nc.sbuf_tensor("gg1", [128, D], F32))
            hb2 = ga.enter_context(nc.sbuf_tensor("hb2", [128, D], BF16))
            psGA = contextlib.ExitStack()
            ps_tg = psGA.enter_context(nc.psum_tensor("ps_tg", [128, 2048], BF16))

            for t in range(NTILE):
                seq_ln(p, block, x_sb[:, t * D:(t + 1) * D], lng[:], scr, p.const(1e-5))

                @block.vector
                def _(v, t=t):
                    p.w(v, "eS")
                    v.tensor_mul(gg0[:], lng[:], lng[:])
                    v.drain()
                    v.tensor_scalar(gg0[:], gg0[:], 0.044715, 1.0, ALU.mult, ALU.add)
                    v.drain()
                    v.tensor_mul(gg0[:], gg0[:], lng[:])
                    v.drain()
                    p.inc("eV", v.nop())

                @block.scalar
                def _(s, t=t):
                    p.w(s, "eV")
                    s.activation(gg1[:], gg0[:], AF.Tanh, scale=GELU_C)
                    s.drain()
                    p.inc("eS", s.nop())

                @block.vector
                def _(v, t=t):
                    p.w(v, "eS")
                    v.tensor_scalar(gg1[:], gg1[:], 0.5, 0.5, ALU.mult, ALU.add)
                    v.drain()
                    v.tensor_mul(gg0[:], gg1[:], lng[:])
                    v.drain()
                    v.tensor_copy(outg_sb[:, t * D:(t + 1) * D], gg0[:])
                    v.drain()
                    p.inc("eV", v.nop())

                @block.scalar
                def _(s, t=t):
                    p.w(s, "eV")
                    s.activation(scr["sq"][0], gg0[:], AF.Square, accum_out=scr["vv"][0])
                    s.drain()
                    s.activation(scr["vv"][0], scr["vv"][0], AF.Ln)
                    s.drain()
                    s.activation(scr["r"][0], scr["vv"][0], AF.Exp, scale=-0.5)
                    s.drain()
                    p.inc("eS", s.nop())

                @block.vector
                def _(v, t=t):
                    p.w(v, "eS")
                    v.tensor_scalar(scr["r"][0], scr["r"][0], 1e12, None, ALU.min)
                    v.drain()
                    v.tensor_scalar(hb2[:], gg0[:], scr["r"][0], None, ALU.mult)
                    v.drain()
                    p.inc("eV", v.nop())

                jobs = [(hb2[:, 128 * c:128 * (c + 1)],
                         onfm_sb[:, c * NT + 128 * t:c * NT + 128 * (t + 1)])
                        for c in range(8)]
                tp_sched(p, block, jobs, ps_tg, id_b[:])

            fence(p, block)
            psGA.close()

            @block.gpsimd
            def _(g):
                p.w(g, "eV")
                p.inc("dG", g.dma_start(out=cc_on_in.rearrange("(c p) n -> p c n", p=128),
                                        in_=onfm_sb[:].rearrange("p (c n) -> p c n", c=8)))
                p.w(g, "dG")
                p.inc("dCC", g.collective_compute(
                    "AllGather", ALU.bypass, replica_groups=RG2,
                    ins=[cc_on_in.opt()], outs=[cc_on_out.opt()]), 16)

            s1 = contextlib.ExitStack()
            kst = s1.enter_context(nc.sbuf_tensor("kst", [128, 2 * 8 * 512], BF16))
            runm = s1.enter_context(nc.sbuf_tensor("runm", [128, 16], F32))
            esim = s1.enter_context(nc.sbuf_tensor("esim", [128, 16], F32))
            eman = s1.enter_context(nc.sbuf_tensor("eman", [128, 8], BF16))
            ema_sb = s1.enter_context(nc.sbuf_tensor("ema_sb", [128, 8], F32))
            tris_sb = s1.enter_context(nc.sbuf_tensor("tris_sb", [128, 128], F32))
            mng2 = s1.enter_context(nc.sbuf_tensor("mng2", [128, 128], F32))
            m0_sb = s1.enter_context(nc.sbuf_tensor("m0_sb", [128, 2], F32))
            flagc_sb = s1.enter_context(nc.sbuf_tensor("flagc_sb", [128, 2], F32))
            seldg = s1.enter_context(nc.sbuf_tensor("seldg", [128, 128], F32))
            chm = s1.enter_context(nc.sbuf_tensor("chm", [128, 1], F32))
            chm2 = s1.enter_context(nc.sbuf_tensor("chm2", [128, 1], F32))
            scc = s1.enter_context(nc.sbuf_tensor("scc", [128, 8], F32))
            gcol = s1.enter_context(nc.sbuf_tensor("gcol", [128, 4], F32))
            rt = s1.enter_context(nc.sbuf_tensor("rt", [128, D], F32))
            yt3 = s1.enter_context(nc.sbuf_tensor("yt3", [128, D], F32))
            psS1 = contextlib.ExitStack()
            ps_s = psS1.enter_context(nc.psum_tensor("ps_s", [128, 1024], F32))
            ps_e = psS1.enter_context(nc.psum_tensor("ps_e", [128, 128], F32))

            @block.sync
            def _(y):
                p.inc("dY", y.dma_start(out=tris_sb[:], in_=tris))
                p.inc("dY", y.dma_start(out=m0_sb[:], in_=m0))
                p.inc("dY", y.dma_start(out=flagc_sb[:], in_=flagc_i))
                p.inc("dY", y.dma_start(out=ema_sb[:], in_=ema8))
                p.inc("dY", y.dma_start(out=scc[0:1, 0:3], in_=sc3))

            @block.vector
            def _(v):
                p.w(v, "dY")
                v.memset(runm[:], -2.0)
                v.drain()
                v.tensor_scalar(mng2[:], tris_sb[:], -1.0, 2.0, ALU.add, ALU.mult)
                v.drain()
                p.inc("eV", v.nop())
            fence(p, block, "vsty")

            @block.tensor
            def _(t_):
                t_.wait_ge(p.sems["eT"], p.cnt["eT"])
                p.inc("eT", t_.matmul(ps_e[:, 0:3], ones_row[:], scc[0:1, 0:3],
                                      start=True, stop=True))

            @block.scalar
            def _(s):
                p.w(s, "eT")
                s.activation(scc[:, 3:4], ps_e[:, 0:1], AF.Exp)
                s.activation(scc[:, 4:5], ps_e[:, 1:2], AF.Sigmoid)
                s.activation(scc[:, 5:6], ps_e[:, 2:3], AF.Sigmoid)
                s.drain()
                s.activation(scr["sq"][0][:, 0:8], ema_sb[:], AF.Square,
                             accum_out=scr["vv"][0])
                s.drain()
                p.inc("eS", s.nop())

            @block.vector
            def _(v):
                p.w(v, "eS")
                v.tensor_scalar(scc[:, 6:7], scc[:, 4:5], -1.0, 1.0, ALU.mult, ALU.add)
                v.drain()
                p.inc("eV", v.nop())

            @block.tensor
            def _(t_):
                p.w(t_, "eS")
                t_.wait_ge(p.sems["eT"], p.cnt["eT"])
                p.inc("eT", t_.matmul(ps_e[0:1, 4:5], ones_col[:], scr["vv"][0],
                                      start=True, stop=True))

            @block.scalar
            def _(s):
                p.w(s, "eT")
                s.activation(scr["m"][0][0:1, :], ps_e[0:1, 4:5], AF.Ln)
                s.drain()
                s.activation(scr["m"][0][0:1, :], scr["m"][0][0:1, :], AF.Exp, scale=-0.5)
                s.drain()
                p.inc("eS", s.nop())

            @block.tensor
            def _(t_):
                p.w(t_, "eS")
                t_.wait_ge(p.sems["eT"], p.cnt["eT"])
                p.inc("eT", t_.matmul(ps_e[:, 5:6], ones_row[:], scr["m"][0][0:1, :],
                                      start=True, stop=True))

            @block.vector
            def _(v):
                p.w(v, "eT")
                v.tensor_scalar(scc[:, 7:8], ps_e[:, 5:6], 1e12, None, ALU.min)
                v.drain()
                v.tensor_scalar(eman[:], ema_sb[:], scc[:, 7:8], None, ALU.mult)
                v.drain()
                p.inc("eV", v.nop())
            fence(p, block)

            @block.gpsimd
            def _(g):
                p.w(g, "dCC")
                g.engine_nop()
                p.inc("eG", g.engine_nop())

            for ks in range(4):
                par = ks % 2

                @block.sync
                def _(y, ks=ks, par=par):
                    p.w(y, "eG")
                    if ks >= 2:
                        p.w(y, "eT")
                    p.inc("dY", y.dma_start(
                        out=kst[:, par * 4096:(par + 1) * 4096].rearrange("p (c n) -> p c n", c=8),
                        in_=cc_on_out.rearrange("(c p) n -> p c n", p=128)[:, 0:8, 512 * ks:512 * (ks + 1)]))

                for qt in range(NTILE):
                    @block.tensor
                    def _(t_, ks=ks, qt=qt, par=par):
                        p.w(t_, "eV")
                        p.w(t_, "dY")
                        for c in range(8):
                            if c == 7:
                                t_.wait_ge(p.sems["eT"], p.cnt["eT"])
                            ins = t_.matmul(ps_s[:, 0:512],
                                            onfm_sb[:, c * NT + 128 * qt:c * NT + 128 * (qt + 1)],
                                            kst[:, par * 4096 + 512 * c:par * 4096 + 512 * (c + 1)],
                                            start=(c == 0), stop=(c == 7))
                        p.inc("eT", ins)

                    @block.vector
                    def _(v, qt=qt):
                        p.w(v, "eT")
                        v.reduce_max(chm[:], ps_s[:, 0:512], AX.X)
                        v.drain()
                        v.tensor_scalar(chm[:], chm[:], flagc_sb[:, 0:1], None, ALU.mult)
                        v.drain()
                        v.scalar_tensor_tensor(chm2[:], flagc_sb[:, 1:2], -1e4, chm[:],
                                               ALU.mult, ALU.add)
                        v.drain()
                        v.tensor_max(runm[:, qt:qt + 1], runm[:, qt:qt + 1], chm2[:])
                        v.drain()
                        v.nop().then_inc(p.sems["eV"], 1)
                        p.cnt["eV"] += 1

            for ks in range(4):
                for qt in range(4 * ks, NTILE):
                    qbase = 128 * qt
                    dj = (qbase - 512 * ks) // 128 if (qbase - 512 * ks) < 512 else None
                    wfull = 512 if dj is None else 128 * dj

                    @block.tensor
                    def _(t_, ks=ks, qt=qt):
                        p.w(t_, "eV")
                        for c in range(8):
                            if c == 7:
                                t_.wait_ge(p.sems["eT"], p.cnt["eT"])
                            ins = t_.matmul(ps_s[:, 0:512],
                                            onfm_sb[:, c * NT + 128 * qt:c * NT + 128 * (qt + 1)],
                                            onfm_sb[:, c * NT + 512 * ks:c * NT + 512 * (ks + 1)],
                                            start=(c == 0), stop=(c == 7))
                        p.inc("eT", ins)

                    @block.vector
                    def _(v, qt=qt, dj=dj, wfull=wfull):
                        p.w(v, "eT")
                        if dj is not None:
                            v.tensor_mul(seldg[:], ps_s[:, 128 * dj:128 * (dj + 1)], tris_sb[:])
                            v.drain()
                            v.tensor_add(seldg[:], seldg[:], mng2[:])
                            v.drain()
                            v.reduce_max(chm[:], seldg[:], AX.X)
                            v.drain()
                            if wfull > 0:
                                v.reduce_max(chm2[:], ps_s[:, 0:wfull], AX.X)
                                v.drain()
                                v.tensor_max(chm[:], chm[:], chm2[:])
                                v.drain()
                        else:
                            v.reduce_max(chm[:], ps_s[:, 0:512], AX.X)
                            v.drain()
                        v.tensor_max(runm[:, qt:qt + 1], runm[:, qt:qt + 1], chm[:])
                        v.drain()
                        v.nop().then_inc(p.sems["eV"], 1)
                        p.cnt["eV"] += 1

            fence(p, block)

            for qt in range(NTILE):
                @block.tensor
                def _(t_, qt=qt):
                    p.w(t_, "eV")
                    for c in range(8):
                        if c == 7:
                            t_.wait_ge(p.sems["eT"], p.cnt["eT"])
                        ins = t_.matmul(ps_e[:, 8 + (qt % 4):9 + (qt % 4)],
                                        onfm_sb[:, c * NT + 128 * qt:c * NT + 128 * (qt + 1)],
                                        eman[:, c:c + 1], start=(c == 0), stop=(c == 7))
                    p.inc("eT", ins)

                @block.vector
                def _(v, qt=qt):
                    p.w(v, "eT")
                    v.tensor_copy(esim[:, qt:qt + 1], ps_e[:, 8 + (qt % 4):9 + (qt % 4)])
                    v.drain()
                    p.inc("eV", v.nop())
            fence(p, block)

            for qt in range(NTILE):
                @block.vector
                def _(v, qt=qt):
                    p.w(v, "eS")
                    sq_ = gcol[:, 0:1]
                    if qt == 0:
                        v.tensor_scalar(sq_, esim[:, 0:1], m0_sb[:, 0:1], None, ALU.mult)
                        v.drain()
                        v.scalar_tensor_tensor(sq_, runm[:, 0:1], m0_sb[:, 1:2], sq_,
                                               ALU.mult, ALU.add)
                        v.drain()
                    else:
                        v.tensor_copy(sq_, runm[:, qt:qt + 1])
                        v.drain()
                    v.tensor_sub(gcol[:, 1:2], sq_, esim[:, qt:qt + 1])
                    v.drain()
                    v.tensor_mul(gcol[:, 1:2], gcol[:, 1:2], scc[:, 5:6])
                    v.drain()
                    v.tensor_add(gcol[:, 1:2], gcol[:, 1:2], esim[:, qt:qt + 1])
                    v.drain()
                    v.tensor_mul(gcol[:, 2:3], gcol[:, 1:2], scc[:, 3:4])
                    v.drain()
                    p.inc("eV", v.nop())

                @block.scalar
                def _(s, qt=qt):
                    p.w(s, "eV")
                    s.activation(gcol[:, 3:4], gcol[:, 2:3], AF.Exp, scale=-1.0)
                    s.drain()
                    p.inc("eS", s.nop())

                @block.vector
                def _(v, qt=qt):
                    p.w(v, "eS")
                    v.tensor_mul(gcol[:, 3:4], gcol[:, 3:4], scc[:, 4:5])
                    v.drain()
                    v.tensor_add(gcol[:, 3:4], gcol[:, 3:4], scc[:, 6:7])
                    v.drain()
                    v.tensor_scalar(rt[:], outg_sb[:, qt * D:(qt + 1) * D], gcol[:, 3:4],
                                    None, ALU.mult)
                    v.drain()
                    p.inc("eV", v.nop())

                seq_ln(p, block, rt[:], yt3[:], scr, p.const(1e-5))

                @block.vector
                def _(v, qt=qt):
                    p.w(v, "eS")
                    xa = x_sb[:, qt * D:(qt + 1) * D]
                    v.tensor_add(xa, xa, yt3[:])
                    v.drain()
                    p.inc("eV", v.nop())

            fence(p, block)
            psS1.close()
            s1.close()
            ga.close()

            @block.gpsimd
            def _(g):
                p.w(g, "eV")
                p.inc("dG", g.dma_start(out=x3_dram.rearrange("(t p) d -> p t d", p=128),
                                        in_=x_sb[:].rearrange("p (t d) -> p t d", t=NTILE)))
            fence(p, block, "vg")

            s3 = contextlib.ExitStack()
            wqT = s3.enter_context(nc.sbuf_tensor("wqT", [128, 8 * D], BF16))
            wkT = s3.enter_context(nc.sbuf_tensor("wkT", [128, 8 * 256], F32))
            wvT = s3.enter_context(nc.sbuf_tensor("wvT", [128, 8 * 256], F32))
            wld = s3.enter_context(nc.sbuf_tensor("wld", [128, 1024], F32))
            wq_t = s3.enter_context(nc.sbuf_tensor("wq_t", [128, 1024], BF16))
            wq_t32 = s3.enter_context(nc.sbuf_tensor("wq_t32", [128, 1024], F32))
            q_scr = {k: s3.enter_context(nc.sbuf_tensor("q_" + k, [128, 1],
                      I32 if k == "i1" else F32))[:] for k in
                     ("mx", "mn", "rng", "isc", "base", "i1", "c1", "d1", "m1", "half1")}
            q_scr["it"] = s3.enter_context(nc.sbuf_tensor("q_it", [128, 128], I32))[:]
            q_scr["ct"] = s3.enter_context(nc.sbuf_tensor("q_ct", [128, 128], F32))[:]
            q_scr["dt"] = s3.enter_context(nc.sbuf_tensor("q_dt", [128, 128], F32))[:]
            q_scr["mt"] = s3.enter_context(nc.sbuf_tensor("q_mt", [128, 128], F32))[:]
            q_scr["halft"] = s3.enter_context(nc.sbuf_tensor("q_halft", [128, 128], F32))[:]
            q_scr["obuf"] = s3.enter_context(nc.sbuf_tensor("q_obuf", [128, 1024], BF16))[:]
            ps3 = contextlib.ExitStack()
            ps_q = ps3.enter_context(nc.psum_tensor("ps_q", [128, 512], BF16))
            ps_q32 = ps3.enter_context(nc.psum_tensor("ps_q32", [128, 512], F32))

            @block.vector
            def _(v):
                v.memset(q_scr["half1"], 0.5)
                v.memset(q_scr["halft"], 0.5)
                v.drain()
                p.inc("eV", v.nop())
            fence(p, block, "vy")

            quant_stream(p, block, wq_i, D, D, wqT, False, id_b[:], q_scr, wld[:], wq_t[:], ps_q)
            quant_stream(p, block, wk_i, 256, D, wkT, False, id_f[:], q_scr, wld[:],
                         wq_t32[:], ps_q32, f32=True)
            quant_stream(p, block, wv_i, 256, D, wvT, False, id_f[:], q_scr, wld[:],
                         wq_t32[:], ps_q32, f32=True)
            quant_stream(p, block, wo_i, D, D, wo_dram, True, id_b[:], q_scr, wld[:], wq_t[:], ps_q)
            MW1 = D * 3584 // 128
            quant_stream(p, block, w1e_i, 3584, D,
                         cc_mw_in.rearrange("(a b) c -> (a b c)").rearrange("(r c) -> r c", c=3584)[0:D, :] if False
                         else cc_mw_in.rearrange("(r c) x -> r (c x)", c=28)[0:D, :],
                         True, id_b[:], q_scr, wld[:], wq_t[:], ps_q)
            quant_stream(p, block, w3e_i, 3584, D,
                         cc_mw_in.rearrange("(r c) x -> r (c x)", c=28)[D:2 * D, :],
                         True, id_b[:], q_scr, wld[:], wq_t[:], ps_q)
            quant_stream(p, block, w2e_i, D, 3584,
                         cc_mw_in.rearrange("(r c) x -> r (c x)", c=8)[2 * 3584:2 * 3584 + 3584, :],
                         True, id_b[:], q_scr, wld[:], wq_t[:], ps_q)
            fence(p, block)
            ps3.close()

            @block.gpsimd
            def _(g):
                p.w(g, "dG")
                p.inc("dCC", g.collective_compute(
                    "AllGather", ALU.bypass, replica_groups=RG8,
                    ins=[cc_mw_in.opt()], outs=[cc_mw_out.opt()]), 16)

            s2 = contextlib.ExitStack()
            xnfm = s2.enter_context(nc.sbuf_tensor("xnfm", [128, 8 * NT], BF16))
            g1b = s2.enter_context(nc.sbuf_tensor("g1b", [128, D], F32))
            g1r_sb = s2.enter_context(nc.sbuf_tensor("g1r_sb", [1, D], F32))
            ln3 = s2.enter_context(nc.sbuf_tensor("ln3", [128, D], F32))
            xn32 = s2.enter_context(nc.sbuf_tensor("xn32", [128, D], F32))
            xnb = s2.enter_context(nc.sbuf_tensor("xnb", [128, D], BF16))
            xnfm32 = s2.enter_context(nc.sbuf_tensor("xnfm32", [128, D], F32))
            qstg = s2.enter_context(nc.sbuf_tensor("qstg", [128, 2 * D], BF16))
            kvstg = s2.enter_context(nc.sbuf_tensor("kvstg", [128, 2 * 512], BF16))
            invb = s2.enter_context(nc.sbuf_tensor("invb", [128, 128], F32))
            invr = s2.enter_context(nc.sbuf_tensor("invr", [1, 128], F32))
            pos_sb = s2.enter_context(nc.sbuf_tensor("pos_sb", [128, 16], F32))
            frq = s2.enter_context(nc.sbuf_tensor("frq", [128, 128], F32))
            frq2 = s2.enter_context(nc.sbuf_tensor("frq2", [128, 128], F32))
            zz = s2.enter_context(nc.sbuf_tensor("zz", [128, 128], F32))
            fli = s2.enter_context(nc.sbuf_tensor("fli", [128, 128], I32))
            flc = s2.enter_context(nc.sbuf_tensor("flc", [128, 128], F32))
            flm = s2.enter_context(nc.sbuf_tensor("flm", [128, 128], F32))
            tinyp = s2.enter_context(nc.sbuf_tensor("tinyp", [128, 128], F32))
            ptin = s2.enter_context(nc.sbuf_tensor("ptin", [128, 256], F32))
            cost = s2.enter_context(nc.sbuf_tensor("cost", [128, 128], F32))
            sint = s2.enter_context(nc.sbuf_tensor("sint", [128, 128], F32))
            qf = s2.enter_context(nc.sbuf_tensor("qf", [128, D], F32))
            rq = s2.enter_context(nc.sbuf_tensor("rq", [128, D], F32))
            qb = s2.enter_context(nc.sbuf_tensor("qb", [128, D], BF16))
            kvf = s2.enter_context(nc.sbuf_tensor("kvf", [128, 512], F32))
            kqf = s2.enter_context(nc.sbuf_tensor("kqf", [128, 256], F32))
            krt = s2.enter_context(nc.sbuf_tensor("krt", [128, 256], F32))
            psS2 = contextlib.ExitStack()
            ps_g = psS2.enter_context(nc.psum_tensor("ps_g", [128, 1024], F32))
            ps_kv = psS2.enter_context(nc.psum_tensor("ps_kv", [128, 512], F32))
            ps_t4 = psS2.enter_context(nc.psum_tensor("ps_t4", [128, 2048], BF16))
            ps_t32 = psS2.enter_context(nc.psum_tensor("ps_t32", [128, 512], F32))

            @block.sync
            def _(y):
                p.inc("dY", y.dma_start(out=g1r_sb[:], in_=g1r))
                p.inc("dY", y.dma_start(out=pos_sb[:], in_=poscol))

            @block.tensor
            def _(t_):
                p.w(t_, "dY")
                t_.wait_ge(p.sems["eT"], p.cnt["eT"])
                t_.matmul(ps_g[:, 0:512], ones_row[:], g1r_sb[0:1, 0:512],
                          start=True, stop=True)
                p.inc("eT", t_.matmul(ps_g[:, 512:1024], ones_row[:], g1r_sb[0:1, 512:1024],
                                      start=True, stop=True))

            @block.vector
            def _(v):
                p.w(v, "eT")
                v.tensor_copy(g1b[:], ps_g[:])
                v.memset(tinyp[:], 1e-30)
                v.drain()
                p.inc("eV", v.nop())

            @block.gpsimd
            def _(g):
                p.w(g, "dY")
                g.iota(invr[0:1, 0:64], [[1, 64]], base=0, channel_multiplier=0,
                       allow_small_or_imprecise_dtypes=True)
                g.drain()
                g.tensor_copy(invr[0:1, 64:128], invr[0:1, 0:64])
                g.drain()
                p.inc("eG", g.engine_nop())

            @block.scalar
            def _(s):
                p.w(s, "eG")
                s.activation(invr[:], invr[:], AF.Exp, scale=-float(np.log(1e6)) / 64.0)
                s.drain()
                p.inc("eS", s.nop())

            @block.tensor
            def _(t_):
                p.w(t_, "eS")
                p.w(t_, "eV")
                t_.wait_ge(p.sems["eT"], p.cnt["eT"])
                p.inc("eT", t_.matmul(ps_g[:, 0:128], ones_row[:], invr[:],
                                      start=True, stop=True))

            @block.vector
            def _(v):
                p.w(v, "eT")
                v.tensor_copy(invb[:], ps_g[:, 0:128])
                v.drain()
                p.inc("eV", v.nop())
            fence(p, block)

            for t in range(NTILE):
                seq_ln(p, block, x_sb[:, t * D:(t + 1) * D], ln3[:], scr, p.const(1e-5))
                seq_ln(p, block, ln3[:], xn32[:], scr, p.const(1e-5), sub_mean=False,
                       gain_bcast=g1b[:])

                @block.vector
                def _(v, t=t):
                    p.w(v, "eS")
                    v.tensor_copy(xnb[:], xn32[:])
                    v.drain()
                    p.inc("eV", v.nop())

                jobs = [(xnb[:, 128 * c:128 * (c + 1)],
                         xnfm[:, c * NT + 128 * t:c * NT + 128 * (t + 1)]) for c in range(8)]
                tp_sched(p, block, jobs, ps_t4, id_b[:])
                for c in range(8):
                    @block.tensor
                    def _(t_, c=c):
                        p.w(t_, "eV")
                        t_.wait_ge(p.sems["eT"], p.cnt["eT"])
                        p.inc("eT", t_.transpose(ps_t32[:, 0:128] if c % 2 == 0 else ps_t32[:, 128:256],
                                                 xn32[:, 128 * c:128 * (c + 1)], id_f[:]))

                    @block.vector
                    def _(v, c=c):
                        p.w(v, "eT")
                        v.tensor_copy(xnfm32[:, 128 * c:128 * (c + 1)],
                                      ps_t32[:, 0:128] if c % 2 == 0 else ps_t32[:, 128:256])
                        v.drain()
                        p.inc("eV", v.nop())

                @block.vector
                def _(v, t=t):
                    p.w(v, "eT")
                    v.tensor_scalar(frq[:], invb[:], pos_sb[:, t:t + 1], None, ALU.mult)
                    v.drain()
                    v.tensor_scalar(frq2[:], frq[:], 1.0, PI / 2.0, ALU.mult, ALU.add)
                    v.drain()
                    for k2, fr in enumerate((frq, frq2)):
                        v.tensor_scalar(zz[:], fr[:], 1.0 / TWO_PI, 0.5, ALU.mult, ALU.add)
                        v.drain()
                        v.tensor_copy(fli[:], zz[:])
                        v.drain()
                        v.tensor_copy(flc[:], fli[:])
                        v.drain()
                        v.tensor_sub(flm[:], flc[:], zz[:])
                        v.drain()
                        v.tensor_tensor(flm[:], flm[:], tinyp[:], ALU.is_ge)
                        v.drain()
                        v.tensor_sub(flc[:], flc[:], flm[:])
                        v.drain()
                        v.scalar_tensor_tensor(ptin[:, 128 * k2:128 * (k2 + 1)], flc[:],
                                               -TWO_PI, fr[:], ALU.mult, ALU.add)
                        v.drain()
                    p.inc("eV", v.nop())

                @block.scalar
                def _(s, t=t):
                    p.w(s, "eV")
                    s.activation(sint[:], ptin[:, 0:128], AF.Sin)
                    s.activation(cost[:], ptin[:, 128:256], AF.Sin)
                    s.drain()
                    p.inc("eS", s.nop())

                @block.tensor
                def _(t_, t=t):
                    p.w(t_, "eV")
                    for n in range(2):
                        for c in range(8):
                            t_.matmul(ps_g[:, 512 * n:512 * (n + 1)],
                                      xnfm[:, c * NT + 128 * t:c * NT + 128 * (t + 1)],
                                      wqT[:, c * D + 512 * n:c * D + 512 * n + 512],
                                      start=(c == 0), stop=(c == 7))
                    for c in range(8):
                        t_.matmul(ps_kv[:, 0:256],
                                  xnfm32[:, 128 * c:128 * (c + 1)],
                                  wkT[:, c * 256:(c + 1) * 256],
                                  start=(c == 0), stop=(c == 7))
                    for c in range(8):
                        if c == 7:
                            t_.wait_ge(p.sems["eT"], p.cnt["eT"])
                        ins = t_.matmul(ps_kv[:, 256:512],
                                        xnfm32[:, 128 * c:128 * (c + 1)],
                                        wvT[:, c * 256:(c + 1) * 256],
                                        start=(c == 0), stop=(c == 7))
                    p.inc("eT", ins)

                @block.vector
                def _(v, t=t):
                    p.w(v, "eT")
                    p.w(v, "eS")
                    if t >= 2:
                        p.w(v, "dG")
                    v.tensor_copy(qf[:], ps_g[:])
                    v.drain()
                    for h in range(8):
                        v.tensor_scalar(rq[:, 128 * h:128 * h + 64],
                                        qf[:, 128 * h + 64:128 * (h + 1)], -1.0, None, ALU.mult)
                        v.tensor_copy(rq[:, 128 * h + 64:128 * (h + 1)],
                                      qf[:, 128 * h:128 * h + 64])
                    v.drain()
                    for h in range(8):
                        v.tensor_mul(qf[:, 128 * h:128 * (h + 1)],
                                     qf[:, 128 * h:128 * (h + 1)], cost[:])
                        v.tensor_mul(rq[:, 128 * h:128 * (h + 1)],
                                     rq[:, 128 * h:128 * (h + 1)], sint[:])
                    v.drain()
                    v.tensor_add(qb[:], qf[:], rq[:])
                    v.drain()
                    v.tensor_copy(kvf[:], ps_kv[:])
                    v.drain()
                    emit_quant_tile(p, v, kvf[:, 0:128], kqf[:, 0:128], q_scr)
                    emit_quant_tile(p, v, kvf[:, 128:256], kqf[:, 128:256], q_scr)
                    par_t = t % 2
                    emit_quant_tile(p, v, kvf[:, 256:384],
                                    kvstg[:, 512 * par_t + 256:512 * par_t + 384], q_scr)
                    emit_quant_tile(p, v, kvf[:, 384:512],
                                    kvstg[:, 512 * par_t + 384:512 * par_t + 512], q_scr)
                    for kvh in range(2):
                        o = 128 * kvh
                        v.tensor_scalar(krt[:, o:o + 64], kqf[:, o + 64:o + 128],
                                        -1.0, None, ALU.mult)
                        v.tensor_copy(krt[:, o + 64:o + 128], kqf[:, o:o + 64])
                        v.drain()
                        v.tensor_mul(kqf[:, o:o + 128], kqf[:, o:o + 128], cost[:])
                        v.drain()
                        v.tensor_mul(krt[:, o:o + 128], krt[:, o:o + 128], sint[:])
                        v.drain()
                        v.tensor_add(kvstg[:, 512 * par_t + o:512 * par_t + o + 128],
                                     kqf[:, o:o + 128], krt[:, o:o + 128])
                        v.drain()
                    v.nop().then_inc(p.sems["eV"], 1)
                    p.cnt["eV"] += 1

                @block.gpsimd
                def _(g, t=t):
                    p.w(g, "eV")
                    p.inc("dG", g.dma_start(out=cc_kv_in[128 * t:128 * (t + 1), :],
                                            in_=kvstg[:, 512 * (t % 2):512 * (t % 2) + 512]))

                jobs = [(qb[:, 128 * h:128 * (h + 1)],
                         qstg[:, (t % 2) * D + 128 * h:(t % 2) * D + 128 * (h + 1)])
                        for h in range(8)]
                tp_sched(p, block, jobs, ps_t4, id_b[:])

                @block.gpsimd
                def _(g, t=t):
                    p.w(g, "eV")
                    p.inc("dG", g.dma_start(
                        out=qfm_dram.rearrange("(c p) n -> p c n", p=128)[:, :, 128 * t:128 * (t + 1)],
                        in_=qstg[:, (t % 2) * D:(t % 2) * D + D].rearrange("p (c n) -> p c n", c=8)))
            fence(p, block)
            psS2.close()
            s2.close()
            s3.close()

            @block.gpsimd
            def _(g):
                p.w(g, "dG")
                p.inc("dCC", g.collective_compute(
                    "AllGather", ALU.bypass, replica_groups=RG2,
                    ins=[cc_kv_in.opt()], outs=[cc_kv_out.opt()]), 16)
                p.w(g, "dCC")
                p.inc("eG", g.engine_nop())
            fence(p, block)
            sx.close()

            ISQc = ISQ
            sH = contextlib.ExitStack()
            h_sb = sH.enter_context(nc.sbuf_tensor("h_sb", [128, NTILE * D], F32))
            e2 = contextlib.ExitStack()
            qt_sb = e2.enter_context(nc.sbuf_tensor("qt_sb", [128, 2 * D], BF16))
            kfm_r = e2.enter_context(nc.sbuf_tensor("kfm_r", [128, 2 * NT], BF16))
            kfm_o = e2.enter_context(nc.sbuf_tensor("kfm_o", [128, 2 * NT], BF16))
            v_r = e2.enter_context(nc.sbuf_tensor("v_r", [128, NTILE * 256], BF16))
            v_o = e2.enter_context(nc.sbuf_tensor("v_o", [128, NTILE * 256], BF16))
            wo_sb = e2.enter_context(nc.sbuf_tensor("wo_sb", [128, 8 * D], BF16))
            psb = e2.enter_context(nc.sbuf_tensor("psb", [128, 2048], BF16))
            rmx = e2.enter_context(nc.sbuf_tensor("rmx", [128, 4], F32))
            rsm = e2.enter_context(nc.sbuf_tensor("rsm", [128, 4], F32))
            seld = e2.enter_context(nc.sbuf_tensor("seld", [128, 128], F32))
            ofb = e2.enter_context(nc.sbuf_tensor("ofb", [128, 128], BF16))
            ofm = e2.enter_context(nc.sbuf_tensor("ofm", [128, 8 * 128], BF16))
            ptrc = e2.enter_context(nc.sbuf_tensor("ptrc", [128, 256], BF16))
            mneg = e2.enter_context(nc.sbuf_tensor("mneg_sb", [128, 128], F32))
            tri_fE = e2.enter_context(nc.sbuf_tensor("tri_fE", [128, 128], F32))
            tri_bE = e2.enter_context(nc.sbuf_tensor("tri_bE", [128, 128], BF16))
            flagr_sb = e2.enter_context(nc.sbuf_tensor("flagr_sb", [1, 512], F32))
            at_sb = e2.enter_context(nc.sbuf_tensor("at_sb", [128, D], F32))
            ln3t = e2.enter_context(nc.sbuf_tensor("ln3t", [128, D], F32))
            x3st = e2.enter_context(nc.sbuf_tensor("x3st", [128, D], F32))
            psE2 = contextlib.ExitStack()
            ps_big = psE2.enter_context(nc.psum_tensor("ps_big", [128, 2048], F32))
            ps_ptr = psE2.enter_context(nc.psum_tensor("ps_ptr", [128, 2048], BF16))
            ps_o = psE2.enter_context(nc.psum_tensor("ps_o", [128, 384], F32))

            @block.sync
            def _(y):
                p.w(y, "eG")
                p.inc("dY", y.dma_start(out=tri_fE[:], in_=tri_inc))
                p.inc("dY", y.dma_start(out=mneg[:], in_=mneg_i))
                p.inc("dY", y.dma_start(out=flagr_sb[:], in_=flagr_i))
                p.inc("dY", y.dma_start(out=v_r[:].rearrange("p (t n) -> p t n", t=NTILE),
                                        in_=cc_kv_out.rearrange("(t p) n -> p t n", p=128)[:, 0:NTILE, 256:512]))
                p.inc("dY", y.dma_start(out=v_o[:].rearrange("p (t n) -> p t n", t=NTILE),
                                        in_=cc_kv_in.rearrange("(t p) n -> p t n", p=128)[:, :, 256:512]))
                p.inc("dY", y.dma_start_transpose(kfm_r[:, 0:NT], cc_kv_out[0:NT, 0:128]))
                p.inc("dY", y.dma_start_transpose(kfm_r[:, NT:2 * NT], cc_kv_out[0:NT, 128:256]))
                p.inc("dY", y.dma_start_transpose(kfm_o[:, 0:NT], cc_kv_in[:, 0:128]))
                p.inc("dY", y.dma_start_transpose(kfm_o[:, NT:2 * NT], cc_kv_in[:, 128:256]))

            @block.gpsimd
            def _(g):
                p.inc("dG", g.dma_start(out=wo_sb[:].rearrange("p (c n) -> p c n", c=8),
                                        in_=wo_dram.rearrange("(c p) n -> p c n", p=128)))

            @block.vector
            def _(v):
                p.w(v, "dY")
                p.w(v, "dG")
                v.tensor_copy(tri_bE[:], tri_fE[:])
                v.drain()
                p.inc("eV", v.nop())
            fence(p, block)

            for qt in range(NTILE):
                par_q = qt % 2

                @block.sync
                def _(y, qt=qt, par_q=par_q):
                    p.w(y, "eS")
                    p.w(y, "eV")
                    p.w(y, "eT")
                    p.inc("dY", y.dma_start(out=x3st[:], in_=x3_dram[128 * qt:128 * (qt + 1), :]))
                    p.inc("dY", y.dma_start(
                        out=qt_sb[:, par_q * D:(par_q + 1) * D].rearrange("p (c n) -> p c n", c=8),
                        in_=qfm_dram.rearrange("(c p) n -> p c n", p=128)[:, :, 128 * qt:128 * (qt + 1)]))

                ow = 128 * (qt + 1)
                for h_ in range(8):
                    kvh = h_ // 4
                    lq = qt_sb[:, par_q * D + 128 * h_:par_q * D + 128 * (h_ + 1)]
                    segs = [("rect", 2048), ("own", ow)]
                    for pass2 in (False, True):
                        for si, (nm, w_) in enumerate(segs):
                            nmm = (w_ + 511) // 512
                            kbase = kfm_r if nm == "rect" else kfm_o

                            @block.tensor
                            def _(t_, lq=lq, w_=w_, kvh=kvh, nmm=nmm, nm=nm, kbase=kbase):
                                p.w(t_, "eV")
                                p.w(t_, "eS")
                                p.w(t_, "dY")
                                for j in range(nmm):
                                    ww = min(512, w_ - 512 * j)
                                    last = (j == nmm - 1)
                                    if nm == "rect":
                                        t_.matmul(ps_big[:, 512 * j:512 * j + ww], lq,
                                                  kbase[:, kvh * NT + 512 * j:kvh * NT + 512 * j + ww],
                                                  start=True, stop=False)
                                        if last:
                                            t_.wait_ge(p.sems["eT"], p.cnt["eT"])
                                        ins = t_.matmul(ps_big[:, 512 * j:512 * j + ww],
                                                        ones_row[:], flagr_sb[0:1, 0:ww],
                                                        start=False, stop=True)
                                    else:
                                        if last:
                                            t_.wait_ge(p.sems["eT"], p.cnt["eT"])
                                        ins = t_.matmul(ps_big[:, 512 * j:512 * j + ww], lq,
                                                        kbase[:, kvh * NT + 512 * j:kvh * NT + 512 * j + ww],
                                                        start=True, stop=True)
                                p.inc("eT", ins)

                            diag = (nm == "own")
                            if not pass2:
                                @block.vector
                                def _(v, si=si, w_=w_, diag=diag):
                                    p.w(v, "eT")
                                    if diag:
                                        v.tensor_mul(seld[:], ps_big[:, w_ - 128:w_], tri_fE[:])
                                        v.drain()
                                        v.tensor_add(seld[:], seld[:], mneg[:])
                                        v.drain()
                                        v.reduce_max(rmx[:, 1:2], seld[:], AX.X)
                                        v.drain()
                                        if w_ > 128:
                                            v.reduce_max(rmx[:, 2:3], ps_big[:, 0:w_ - 128], AX.X)
                                            v.drain()
                                            v.tensor_max(rmx[:, 1:2], rmx[:, 1:2], rmx[:, 2:3])
                                            v.drain()
                                    else:
                                        v.reduce_max(rmx[:, 1:2], ps_big[:, 0:w_], AX.X)
                                        v.drain()
                                    if si == 0:
                                        v.tensor_copy(rmx[:, 0:1], rmx[:, 1:2])
                                    else:
                                        v.tensor_max(rmx[:, 0:1], rmx[:, 0:1], rmx[:, 1:2])
                                    v.drain()
                                    v.nop().then_inc(p.sems["eV"], 1)
                                    p.cnt["eV"] += 1
                            else:
                                if si == 0:
                                    @block.vector
                                    def _(v):
                                        p.w(v, "eT")
                                        v.tensor_scalar(rmx[:, 3:4], rmx[:, 0:1], -ISQc,
                                                        None, ALU.mult)
                                        v.drain()
                                        v.nop().then_inc(p.sems["eV"], 1)
                                        p.cnt["eV"] += 1

                                @block.scalar
                                def _(s, w_=w_):
                                    p.w(s, "eV")
                                    p.w(s, "eT")
                                    s.activation(psb[:, 0:w_], ps_big[:, 0:w_], AF.Exp,
                                                 scale=ISQc, bias=rmx[:, 3:4],
                                                 accum_out=rsm[:, 1:2])
                                    s.drain()
                                    p.inc("eS", s.nop())

                                @block.vector
                                def _(v, si=si, w_=w_, diag=diag):
                                    p.w(v, "eS")
                                    if diag:
                                        v.reduce_sum(rsm[:, 2:3], psb[:, w_ - 128:w_], AX.X)
                                        v.drain()
                                        v.tensor_mul(psb[:, w_ - 128:w_], psb[:, w_ - 128:w_],
                                                     tri_bE[:])
                                        v.drain()
                                        v.reduce_sum(rsm[:, 3:4], psb[:, w_ - 128:w_], AX.X)
                                        v.drain()
                                        v.tensor_sub(rsm[:, 1:2], rsm[:, 1:2], rsm[:, 2:3])
                                        v.drain()
                                        v.tensor_add(rsm[:, 1:2], rsm[:, 1:2], rsm[:, 3:4])
                                        v.drain()
                                    if si == 0:
                                        v.tensor_copy(rsm[:, 0:1], rsm[:, 1:2])
                                    else:
                                        v.tensor_add(rsm[:, 0:1], rsm[:, 0:1], rsm[:, 1:2])
                                    v.drain()
                                    v.nop().then_inc(p.sems["eV"], 1)
                                    p.cnt["eV"] += 1

                                nch = w_ // 128
                                vsrc = v_r if nm == "rect" else v_o
                                for kc in range(nch):
                                    pslot = kc % 8
                                    first = (si == 0 and kc == 0)
                                    last = (si == 1 and kc == nch - 1)

                                    @block.tensor
                                    def _(t_, kc=kc, pslot=pslot):
                                        p.w(t_, "eV")
                                        t_.wait_ge(p.sems["eT"], p.cnt["eT"])
                                        p.inc("eT", t_.transpose(
                                            ps_ptr[:, 128 * pslot:128 * (pslot + 1)],
                                            psb[:, 128 * kc:128 * (kc + 1)], tri_bE[:] if False else id_b[:]))

                                    @block.vector
                                    def _(v, pslot=pslot):
                                        p.w(v, "eT")
                                        v.tensor_copy(ptrc[:, 128 * (pslot % 2):128 * (pslot % 2 + 1)],
                                                      ps_ptr[:, 128 * pslot:128 * (pslot + 1)])
                                        v.drain()
                                        v.nop().then_inc(p.sems["eV"], 1)
                                        p.cnt["eV"] += 1

                                    @block.tensor
                                    def _(t_, kc=kc, kvh=kvh, first=first, last=last,
                                          pslot=pslot, vsrc=vsrc):
                                        p.w(t_, "eV")
                                        if last:
                                            t_.wait_ge(p.sems["eT"], p.cnt["eT"])
                                        ins = t_.matmul(ps_o[:, 0:128],
                                                        ptrc[:, 128 * (pslot % 2):128 * (pslot % 2 + 1)],
                                                        vsrc[:, kc * 256 + 128 * kvh:kc * 256 + 128 * (kvh + 1)],
                                                        start=first, stop=last)
                                        if last:
                                            p.inc("eT", ins)

                    @block.vector
                    def _(v, h_=h_):
                        p.w(v, "eT")
                        v.reciprocal(rsm[:, 1:2], rsm[:, 0:1])
                        v.drain()
                        v.tensor_scalar(ofb[:], ps_o[:, 0:128], rsm[:, 1:2], None, ALU.mult)
                        v.drain()
                        v.nop().then_inc(p.sems["eV"], 1)
                        p.cnt["eV"] += 1

                    @block.tensor
                    def _(t_, h_=h_):
                        p.w(t_, "eV")
                        t_.wait_ge(p.sems["eT"], p.cnt["eT"])
                        p.inc("eT", t_.transpose(ps_ptr[:, 0:128], ofb[:], id_b[:]))

                    @block.vector
                    def _(v, h_=h_):
                        p.w(v, "eT")
                        v.tensor_copy(ofm[:, 128 * h_:128 * (h_ + 1)], ps_ptr[:, 0:128])
                        v.drain()
                        v.nop().then_inc(p.sems["eV"], 1)
                        p.cnt["eV"] += 1

                @block.tensor
                def _(t_, qt=qt):
                    p.w(t_, "eV")
                    for n in range(2):
                        for h2 in range(8):
                            if n == 1 and h2 == 7:
                                t_.wait_ge(p.sems["eT"], p.cnt["eT"])
                            ins = t_.matmul(ps_big[:, 512 * n:512 * (n + 1)],
                                            ofm[:, 128 * h2:128 * (h2 + 1)],
                                            wo_sb[:, h2 * D + 512 * n:h2 * D + 512 * n + 512],
                                            start=(h2 == 0), stop=(h2 == 7))
                    p.inc("eT", ins)

                @block.vector
                def _(v, qt=qt):
                    p.w(v, "eT")
                    v.tensor_copy(at_sb[:], ps_big[:, 0:1024])
                    v.drain()
                    p.inc("eV", v.nop())

                seq_ln(p, block, x3st[:], ln3t[:], scr, p.const(1e-5))

                @block.vector
                def _(v, qt=qt):
                    p.w(v, "eS")
                    v.tensor_add(h_sb[:, qt * D:(qt + 1) * D], ln3t[:], at_sb[:])
                    v.drain()
                    p.inc("eV", v.nop())
            fence(p, block)
            psE2.close()
            e2.close()

            e3 = contextlib.ExitStack()
            hnfm = e3.enter_context(nc.sbuf_tensor("hnfm", [128, 8 * NT], BF16))
            g2b = e3.enter_context(nc.sbuf_tensor("g2b", [128, D], F32))
            g2r_sb = e3.enter_context(nc.sbuf_tensor("g2r_sb", [1, D], F32))
            wg_sb = e3.enter_context(nc.sbuf_tensor("wg_sb", [128, 64], F32))
            wgb = e3.enter_context(nc.sbuf_tensor("wgb", [128, 64], BF16))
            hnb = e3.enter_context(nc.sbuf_tensor("hnb", [128, D], BF16))
            rwT = e3.enter_context(nc.sbuf_tensor("rwT", [8, NT], BF16))
            rwbb = e3.enter_context(nc.sbuf_tensor("rwbb", [128, 8], BF16))
            lg = e3.enter_context(nc.sbuf_tensor("lg", [128, 8], F32))
            exl = e3.enter_context(nc.sbuf_tensor("exl", [128, 8], F32))
            exm = e3.enter_context(nc.sbuf_tensor("exm", [128, 8], F32))
            mk2 = e3.enter_context(nc.sbuf_tensor("mk2", [128, 8], F32))
            tcol = e3.enter_context(nc.sbuf_tensor("tcol", [128, 6], F32))
            neg1 = e3.enter_context(nc.sbuf_tensor("neg1", [128, 8], F32))
            rw_sb = e3.enter_context(nc.sbuf_tensor("rw_sb", [128, 8], F32))
            moe_acc = e3.enter_context(nc.sbuf_tensor("moe_acc", [128, 8 * NT], BF16))
            psE3 = contextlib.ExitStack()
            ps_l = psE3.enter_context(nc.psum_tensor("ps_l", [128, 1024], F32))
            ps_t5 = psE3.enter_context(nc.psum_tensor("ps_t5", [128, 2048], BF16))

            @block.sync
            def _(y):
                p.inc("dY", y.dma_start(out=g2r_sb[:], in_=g2r))
                p.inc("dY", y.dma_start(out=wg_sb[:].rearrange("p (c n) -> p c n", c=8),
                                        in_=wgT_i.rearrange("(c p) n -> p c n", p=128)))

            @block.tensor
            def _(t_):
                p.w(t_, "dY")
                t_.wait_ge(p.sems["eT"], p.cnt["eT"])
                t_.matmul(ps_l[:, 0:512], ones_row[:], g2r_sb[0:1, 0:512],
                          start=True, stop=True)
                p.inc("eT", t_.matmul(ps_l[:, 512:1024], ones_row[:], g2r_sb[0:1, 512:1024],
                                      start=True, stop=True))

            @block.vector
            def _(v):
                p.w(v, "eT")
                p.w(v, "dY")
                v.tensor_copy(g2b[:], ps_l[:])
                v.tensor_copy(wgb[:], wg_sb[:])
                v.memset(neg1[:], -1.0)
                v.drain()
                p.inc("eV", v.nop())
            fence(p, block)

            for t in range(NTILE):
                seq_ln(p, block, h_sb[:, t * D:(t + 1) * D], hnb[:], scr, p.const(1e-5),
                       sub_mean=False, gain_bcast=g2b[:])
                jobs = [(hnb[:, 128 * c:128 * (c + 1)],
                         hnfm[:, c * NT + 128 * t:c * NT + 128 * (t + 1)]) for c in range(8)]
                tp_sched(p, block, jobs, ps_t5, id_b[:])

                @block.tensor
                def _(t_, t=t):
                    p.w(t_, "eV")
                    for c in range(8):
                        if c == 7:
                            t_.wait_ge(p.sems["eT"], p.cnt["eT"])
                        ins = t_.matmul(ps_l[:, 0:8],
                                        hnfm[:, c * NT + 128 * t:c * NT + 128 * (t + 1)],
                                        wgb[:, 8 * c:8 * (c + 1)],
                                        start=(c == 0), stop=(c == 7))
                    p.inc("eT", ins)

                @block.vector
                def _(v, t=t):
                    p.w(v, "eT")
                    v.tensor_copy(lg[:], ps_l[:, 0:8])
                    v.drain()
                    v.reduce_max(tcol[:, 0:1], lg[:], AX.X)
                    v.drain()
                    v.tensor_scalar(tcol[:, 1:2], tcol[:, 0:1], -1.0, None, ALU.mult)
                    v.drain()
                    p.inc("eV", v.nop())

                @block.scalar
                def _(s, t=t):
                    p.w(s, "eV")
                    s.activation(exl[:], lg[:], AF.Exp, bias=tcol[:, 1:2])
                    s.drain()
                    p.inc("eS", s.nop())

                @block.vector
                def _(v, t=t):
                    p.w(v, "eS")
                    v.reduce_max(tcol[:, 2:3], exl[:], AX.X)
                    v.drain()
                    v.tensor_scalar(exm[:], ones8[:], tcol[:, 2:3], None, ALU.mult)
                    v.drain()
                    v.tensor_tensor(mk2[:], exl[:], exm[:], ALU.is_ge)
                    v.drain()
                    v.tensor_scalar(exm[:], exl[:], 1.0, None, ALU.add)
                    v.drain()
                    v.tensor_mul(exm[:], exm[:], mk2[:])
                    v.drain()
                    v.tensor_sub(exm[:], exl[:], exm[:])
                    v.drain()
                    v.reduce_max(tcol[:, 3:4], exm[:], AX.X)
                    v.drain()
                    v.tensor_scalar(exm[:], ones8[:], tcol[:, 3:4], None, ALU.mult)
                    v.drain()
                    v.tensor_tensor(mk2[:], exl[:], exm[:], ALU.is_ge)
                    v.drain()
                    v.tensor_add(tcol[:, 4:5], tcol[:, 2:3], tcol[:, 3:4])
                    v.drain()
                    v.reciprocal(tcol[:, 5:6], tcol[:, 4:5])
                    v.drain()
                    v.tensor_mul(rw_sb[:], exl[:], mk2[:])
                    v.drain()
                    v.tensor_scalar(rwbb[:], rw_sb[:], tcol[:, 5:6], None, ALU.mult)
                    v.drain()
                    p.inc("eV", v.nop())

                @block.tensor
                def _(t_, t=t):
                    p.w(t_, "eV")
                    t_.wait_ge(p.sems["eT"], p.cnt["eT"])
                    p.inc("eT", t_.transpose(ps_t5[0:8, 0:128], rwbb[:], id_b[:]))

                @block.vector
                def _(v, t=t):
                    p.w(v, "eT")
                    v.tensor_copy(rwT[0:8, 128 * t:128 * (t + 1)], ps_t5[0:8, 0:128])
                    v.drain()
                    p.inc("eV", v.nop())
            fence(p, block)

            w1tv = cc_mw_out.rearrange("(r c) x -> r (c x)", c=28)
            w2tv = cc_mw_out.rearrange("(r c) x -> r (c x)", c=8)
            R1 = 3 * D
            R2 = 3 * 3584 // 1 if False else (MWR * 128) // 1024
            e4 = contextlib.ExitStack()
            up_sb = e4.enter_context(nc.sbuf_tensor("up_sb", [128, 28 * 512], BF16))
            pan1 = e4.enter_context(nc.sbuf_tensor("pan1", [128, 2 * 1024], BF16))
            pan3 = e4.enter_context(nc.sbuf_tensor("pan3", [128, 2 * 1024], BF16))
            pan2 = e4.enter_context(nc.sbuf_tensor("pan2", [128, 3584], BF16))
            rwb = e4.enter_context(nc.sbuf_tensor("rwb", [128, 512], BF16))
            oneh_sb = e4.enter_context(nc.sbuf_tensor("oneh_sb", [8, 1024], BF16))
            u1f = e4.enter_context(nc.sbuf_tensor("u1f", [128, 512], F32))
            psE4 = contextlib.ExitStack()
            ps_up = psE4.enter_context(nc.psum_tensor("ps_up", [128, 1024], F32))
            ps_m = psE4.enter_context(nc.psum_tensor("ps_m", [128, 1024], F32))

            @block.gpsimd
            def _(g):
                p.w(g, "dCC")
                p.inc("dG", g.dma_start(out=oneh_sb[:], in_=oneh_i))

            @block.vector
            def _(v):
                p.w(v, "dG")
                v.memset(moe_acc[:], 0.0)
                v.drain()
                p.inc("eV", v.nop())
            fence(p, block, "vtgy")

            for e in range(8):
                for mt in range(4):
                    tok0 = 512 * mt

                    @block.tensor
                    def _(t_, e=e, mt=mt, tok0=tok0):
                        p.w(t_, "eV")
                        t_.wait_ge(p.sems["eT"], p.cnt["eT"])
                        p.inc("eT", t_.matmul(ps_up[:, 512:1024],
                                              oneh_sb[0:8, 128 * e:128 * (e + 1)],
                                              rwT[0:8, tok0:tok0 + 512],
                                              start=True, stop=True))

                    @block.vector
                    def _(v, e=e, mt=mt):
                        p.w(v, "eT")
                        v.tensor_copy(rwb[:], ps_up[:, 512:1024])
                        v.drain()
                        p.inc("eV", v.nop())

                    for ffc in range(28):
                        par = ffc % 2

                        @block.sync
                        def _(y, e=e, ffc=ffc, par=par):
                            p.w(y, "eT")
                            p.inc("dY", y.dma_start(
                                out=pan1[:, 1024 * par:1024 * (par + 1)].rearrange("p (c n) -> p c n", c=8),
                                in_=w1tv[3 * D * e + 0:3 * D * e + D, :].rearrange("(c p) n -> p c n", p=128)[:, :, 128 * ffc:128 * (ffc + 1)]))
                            p.inc("dY", y.dma_start(
                                out=pan3[:, 1024 * par:1024 * (par + 1)].rearrange("p (c n) -> p c n", c=8),
                                in_=w1tv[3 * D * e + D:3 * D * e + 2 * D, :].rearrange("(c p) n -> p c n", p=128)[:, :, 128 * ffc:128 * (ffc + 1)]))

                        @block.tensor
                        def _(t_, e=e, mt=mt, ffc=ffc, par=par, tok0=tok0):
                            p.w(t_, "dY")
                            p.w(t_, "eS")
                            p.w(t_, "eV")
                            for c in range(8):
                                t_.matmul(ps_up[:, 0:512],
                                          pan1[:, 1024 * par + 128 * c:1024 * par + 128 * (c + 1)],
                                          hnfm[:, c * NT + tok0:c * NT + tok0 + 512],
                                          start=(c == 0), stop=(c == 7))
                            for c in range(8):
                                if c == 7:
                                    t_.wait_ge(p.sems["eT"], p.cnt["eT"])
                                ins = t_.matmul(ps_up[:, 512:1024],
                                                pan3[:, 1024 * par + 128 * c:1024 * par + 128 * (c + 1)],
                                                hnfm[:, c * NT + tok0:c * NT + tok0 + 512],
                                                start=(c == 0), stop=(c == 7))
                            p.inc("eT", ins)

                        @block.scalar
                        def _(s, ffc=ffc):
                            p.w(s, "eT")
                            s.activation(u1f[:], ps_up[:, 0:512], AF.Silu)
                            s.drain()
                            p.inc("eS", s.nop())

                        @block.vector
                        def _(v, ffc=ffc):
                            p.w(v, "eS")
                            v.tensor_mul(u1f[:], u1f[:], ps_up[:, 512:1024])
                            v.drain()
                            v.tensor_mul(up_sb[:, 512 * ffc:512 * (ffc + 1)], u1f[:], rwb[:])
                            v.drain()
                            v.nop().then_inc(p.sems["eV"], 1)
                            p.cnt["eV"] += 1

                    for dc in range(8):
                        @block.sync
                        def _(y, e=e, dc=dc):
                            p.w(y, "eV")
                            p.w(y, "eT")
                            p.inc("dY", y.dma_start(
                                out=pan2[:].rearrange("p (c n) -> p c n", c=28),
                                in_=w2tv[10752 * e + 7168:10752 * e + 10752, :].rearrange("(c p) n -> p c n", p=128)[:, :, 128 * dc:128 * (dc + 1)]))

                        @block.tensor
                        def _(t_, e=e, mt=mt, dc=dc, tok0=tok0):
                            p.w(t_, "dY")
                            p.w(t_, "eV")
                            for ffc in range(28):
                                if ffc == 27:
                                    t_.wait_ge(p.sems["eT"], p.cnt["eT"])
                                ins = t_.matmul(ps_m[:, 512 * (dc % 2):512 * (dc % 2) + 512],
                                                pan2[:, 128 * ffc:128 * (ffc + 1)],
                                                up_sb[:, 512 * ffc:512 * (ffc + 1)],
                                                start=(ffc == 0), stop=(ffc == 27))
                            p.inc("eT", ins)

                        @block.vector
                        def _(v, e=e, dc=dc, tok0=tok0):
                            p.w(v, "eT")
                            acc = moe_acc[:, dc * NT + tok0:dc * NT + tok0 + 512]
                            v.tensor_add(acc, acc, ps_m[:, 512 * (dc % 2):512 * (dc % 2) + 512])
                            v.drain()
                            v.nop().then_inc(p.sems["eV"], 1)
                            p.cnt["eV"] += 1
            fence(p, block)
            psE4.close()
            e4.close()

            fin = contextlib.ExitStack()
            moet = fin.enter_context(nc.sbuf_tensor("moet", [128, D], BF16))
            hmo = fin.enter_context(nc.sbuf_tensor("hmo", [128, D], F32))
            yt4 = fin.enter_context(nc.sbuf_tensor("yt4", [128, D], F32))
            outt = fin.enter_context(nc.sbuf_tensor("outt", [128, 2 * D], F32))
            x3f = fin.enter_context(nc.sbuf_tensor("x3f", [128, D], F32))
            psF = contextlib.ExitStack()
            ps_t7 = psF.enter_context(nc.psum_tensor("ps_t7", [128, 2048], BF16))

            for t in range(NTILE):
                @block.sync
                def _(y, t=t):
                    p.w(y, "eV")
                    p.w(y, "eS")
                    p.inc("dY", y.dma_start(out=x3f[:], in_=x3_dram[128 * t:128 * (t + 1), :]))

                jobs = [(moe_acc[:, c * NT + 128 * t:c * NT + 128 * (t + 1)],
                         moet[:, 128 * c:128 * (c + 1)]) for c in range(8)]
                tp_sched(p, block, jobs, ps_t7, id_b[:])

                @block.vector
                def _(v, t=t):
                    p.w(v, "eV")
                    v.tensor_add(hmo[:], h_sb[:, t * D:(t + 1) * D], moet[:])
                    v.drain()
                    p.inc("eV", v.nop())

                seq_ln(p, block, hmo[:], yt4[:], scr, p.const(1e-5))

                @block.vector
                def _(v, t=t):
                    p.w(v, "eS")
                    p.w(v, "dY")
                    if t >= 2:
                        p.w(v, "dG")
                    v.tensor_add(outt[:, (t % 2) * D:(t % 2) * D + D], x3f[:], yt4[:])
                    v.drain()
                    p.inc("eV", v.nop())

                @block.gpsimd
                def _(g, t=t):
                    p.w(g, "eV")
                    p.inc("dG", g.dma_start(out=out_o[128 * t:128 * (t + 1), :],
                                            in_=outt[:, (t % 2) * D:(t % 2) * D + D]))

            @block.gpsimd
            def _(g):
                p.w(g, "dG")
            psF.close()
            fin.close()
            psE3.close()
            e3.close()
            sH.close()
    return p.nc


def _full_inmaps(L0_attn_query, L0_attn_W_Q, L0_attn_W_K, L0_attn_W_V, L0_attn_W_O,
                 L0_ffn_U1, L0_ffn_V1, L0_ffn_b1, L0_ffn_U2, L0_ffn_V2, L0_ffn_b2,
                 L1_attn_ema_out, L1_attn_logit_decay, L1_attn_log_tau, L1_attn_log_blend,
                 L1_attn_logit_blend_seq, L1_ffn_w_q, L1_ffn_w_k, L1_ffn_w_v, L1_ffn_w_o,
                 L1_ffn_g1, L1_ffn_g2, L1_ffn_w_gate, L1_ffn_w1, L1_ffn_w3, L1_ffn_w2):
    f32 = np.float32
    X = np.asarray(L0_attn_query, f32)
    iden = np.eye(128, dtype=f32)
    tric = np.triu(np.ones((128, 128), dtype=f32))
    tris_ = np.tril(np.ones((128, 128), dtype=f32), -1)
    tri_inc = np.tril(np.ones((128, 128), dtype=f32))
    mneg = ((1.0 - tri_inc) * np.float32(-1e30)).astype(f32)
    oneh = np.zeros((8, 1024), dtype=f32)
    for e in range(8):
        oneh[e, 128 * e:128 * (e + 1)] = 1.0
    wqk = np.ascontiguousarray(np.concatenate(
        [np.asarray(L0_attn_W_Q, f32).reshape(16, D).T,
         np.asarray(L0_attn_W_K, f32).reshape(16, D).T], axis=1))
    wvl0 = np.ascontiguousarray(np.asarray(L0_attn_W_V, f32).transpose(1, 0, 2).reshape(D, D))
    mo = np.ascontiguousarray(np.asarray(L0_attn_W_O, f32).transpose(2, 1, 0).reshape(D, D))
    b1h = np.ascontiguousarray(np.asarray(L0_ffn_b1, f32).reshape(32, 128).T)
    ema8 = np.ascontiguousarray(np.asarray(L1_attn_ema_out, f32).reshape(8, 128).T)
    sc3 = np.array([[float(np.asarray(L1_attn_log_tau)), float(np.asarray(L1_attn_log_blend)),
                     float(np.asarray(L1_attn_logit_blend_seq))]], dtype=f32)
    w1 = np.asarray(L1_ffn_w1, f32)
    w3 = np.asarray(L1_ffn_w3, f32)
    w2 = np.asarray(L1_ffn_w2, f32)
    wgT = np.ascontiguousarray(np.asarray(L1_ffn_w_gate, f32).T)
    in_maps = []
    for c in range(8):
        b, h = c // 2, c % 2
        m0c = ((np.arange(128) == 0).astype(f32)[:, None]
               if h == 0 else np.zeros((128, 1), dtype=f32))
        flag = np.float32(1.0 if h == 1 else 0.0)
        in_maps.append({
            "xs": np.ascontiguousarray(X[b, h * NT:(h + 1) * NT]),
            "wqk": wqk, "wvl0": wvl0, "iden": iden, "tric": tric, "tris": tris_,
            "tri_inc": tri_inc, "mneg": mneg, "mo": mo,
            "u1": np.asarray(L0_ffn_U1, f32), "v1": np.asarray(L0_ffn_V1, f32),
            "b1": b1h, "u2": np.asarray(L0_ffn_U2, f32), "v2": np.asarray(L0_ffn_V2, f32),
            "b2": np.asarray(L0_ffn_b2, f32)[None],
            "ema8": ema8, "sc3": sc3,
            "m0": np.ascontiguousarray(np.concatenate([m0c, 1.0 - m0c], axis=1)),
            "poscol": np.ascontiguousarray((h * NT + np.arange(NT).reshape(16, 128).T).astype(f32)),
            "g1r": np.asarray(L1_ffn_g1, f32)[None], "g2r": np.asarray(L1_ffn_g2, f32)[None],
            "wgT": wgT, "oneh": oneh,
            "wq": np.asarray(L1_ffn_w_q, f32), "wk": np.asarray(L1_ffn_w_k, f32),
            "wv": np.asarray(L1_ffn_w_v, f32), "wo": np.asarray(L1_ffn_w_o, f32),
            "w1e": w1[c], "w3e": w3[c], "w2e": w2[c],
            "mh": np.full((1, 2080), flag, dtype=f32),
            "flagc": np.ascontiguousarray(np.concatenate(
                [np.full((128, 1), flag, f32), np.full((128, 1), 1.0 - flag, f32)], axis=1)),
            "flagr": np.full((1, 512), 0.0 if h == 1 else -1e30, dtype=f32),
        })
    return in_maps


def kernel_fused_cc(**inputs):
    in_maps = _full_inmaps(**inputs)
    nc = build_full()
    res = run_bass_kernel_spmd(nc, in_maps, core_ids=list(range(8)))
    out = np.empty((4, 4096, D), dtype=np.float32)
    for c in range(8):
        b, h = c // 2, c % 2
        out[b, h * NT:(h + 1) * NT] = np.asarray(res.results[c]["out"], np.float32)
    return out


def build_C2():
    p = P()
    nc = p.nc
    f = nc.dram_tensor
    x2_i = f("x2", [NT, D], F32, kind="ExternalInput").ap()
    outg_i = f("outg", [NT, D], BF16, kind="ExternalInput").ap()
    onfm_i = f("onfm", [D, NT], BF16, kind="ExternalInput").ap()
    onprev = f("onprev", [D, NT], BF16, kind="ExternalInput").ap()
    tris = f("tris", [128, 128], F32, kind="ExternalInput").ap()
    iden = f("iden", [128, 128], F32, kind="ExternalInput").ap()
    ema8 = f("ema8", [128, 8], F32, kind="ExternalInput").ap()
    sc3 = f("sc3", [1, 3], F32, kind="ExternalInput").ap()
    m0 = f("m0", [128, 2], F32, kind="ExternalInput").ap()
    poscol = f("poscol", [128, 16], F32, kind="ExternalInput").ap()
    g1r = f("g1r", [1, D], F32, kind="ExternalInput").ap()
    wq_i = f("wq", [D, D], F32, kind="ExternalInput").ap()
    wk_i = f("wk", [256, D], F32, kind="ExternalInput").ap()
    wv_i = f("wv", [256, D], F32, kind="ExternalInput").ap()
    wo_i = f("wo", [D, D], F32, kind="ExternalInput").ap()
    w1e_i = f("w1e", [3584, D], F32, kind="ExternalInput").ap()
    w3e_i = f("w3e", [3584, D], F32, kind="ExternalInput").ap()
    w2e_i = f("w2e", [D, 3584], F32, kind="ExternalInput").ap()
    flagc_i = f("flagc", [128, 2], F32, kind="ExternalInput").ap()
    x3_dram = f("x3", [NT, D], F32, kind="ExternalOutput").ap()
    qfm_dram = f("qfm", [D, NT], BF16, kind="ExternalOutput").ap()
    cc_kv_in = f("kv", [NT, 512], BF16, kind="ExternalOutput").ap()
    wo_dram = f("woqt", [D, D], BF16, kind="ExternalOutput").ap()
    MWR = (D * 3584 * 2 + 3584 * D) // 128
    cc_mw_in = f("mw", [MWR, 128], BF16, kind="ExternalOutput").ap()
    cc_on_out = onprev

    for s_ in ("dY", "dG", "dCC", "eV", "eS", "eT", "eG", "eY"):
        p.sem(s_)

    id_f = p.sb("id_f", [128, 128], F32)
    id_b = p.sb("id_b", [128, 128], BF16)
    ones_row = p.sb("ones_row", [1, 128], F32)
    ones_col = p.sb("ones_col", [128, 1], F32)
    ones8 = p.sb("ones8", [128, 8], F32)
    _sq = p.sb("ssq0", [128, D], F32)[:]
    scr = {"sq": [_sq, _sq],
           "m": [p.sb("sm0", [128, 1], F32)[:], p.sb("sm1", [128, 1], F32)[:]],
           "vv": [p.sb("sv0", [128, 1], F32)[:], p.sb("sv1", [128, 1], F32)[:]],
           "r": [p.sb("sr0", [128, 1], F32)[:], p.sb("sr1", [128, 1], F32)[:]]}
    ISQ = 1.0 / float(np.sqrt(128.0))
    TWO_PI = 2.0 * PI

    with p.st:
        with nc.Block() as block:
            p.add_consts(block, [1e-5])

            sx = contextlib.ExitStack()
            x_sb = sx.enter_context(nc.sbuf_tensor("x_sb", [128, NTILE * D], F32))
            ga = contextlib.ExitStack()
            outg_sb = ga.enter_context(nc.sbuf_tensor("outg_sb", [128, NTILE * D], BF16))
            onfm_sb = ga.enter_context(nc.sbuf_tensor("onfm_sb", [128, 8 * NT], BF16))

            @block.sync
            def _(y):
                p.inc("dY", y.dma_start(out=x_sb[:].rearrange("p (t d) -> p t d", t=NTILE),
                                        in_=x2_i.rearrange("(t p) d -> p t d", p=128)))
                p.inc("dY", y.dma_start(out=outg_sb[:].rearrange("p (t d) -> p t d", t=NTILE),
                                        in_=outg_i.rearrange("(t p) d -> p t d", p=128)))
                p.inc("dY", y.dma_start(out=onfm_sb[:].rearrange("p (c n) -> p c n", c=8),
                                        in_=onfm_i.rearrange("(c p) n -> p c n", p=128)))
                p.inc("dY", y.dma_start(out=id_f[:], in_=iden))

            @block.vector
            def _(v):
                p.w(v, "dY")
                v.tensor_copy(id_b[:], id_f[:])
                v.memset(ones_col[:], 1.0)
                v.memset(ones_row[:], 1.0)
                v.memset(ones8[:], 1.0)
                v.drain()
                p.inc("eV", v.nop())

            @block.gpsimd
            def _(g):
                p.inc("eG", g.engine_nop())
            fence(p, block)

            s1 = contextlib.ExitStack()
            kst = s1.enter_context(nc.sbuf_tensor("kst", [128, 2 * 8 * 512], BF16))
            runm = s1.enter_context(nc.sbuf_tensor("runm", [128, 16], F32))
            esim = s1.enter_context(nc.sbuf_tensor("esim", [128, 16], F32))
            eman = s1.enter_context(nc.sbuf_tensor("eman", [128, 8], BF16))
            ema_sb = s1.enter_context(nc.sbuf_tensor("ema_sb", [128, 8], F32))
            tris_sb = s1.enter_context(nc.sbuf_tensor("tris_sb", [128, 128], F32))
            mng2 = s1.enter_context(nc.sbuf_tensor("mng2", [128, 128], F32))
            m0_sb = s1.enter_context(nc.sbuf_tensor("m0_sb", [128, 2], F32))
            flagc_sb = s1.enter_context(nc.sbuf_tensor("flagc_sb", [128, 2], F32))
            seldg = s1.enter_context(nc.sbuf_tensor("seldg", [128, 128], F32))
            chm = s1.enter_context(nc.sbuf_tensor("chm", [128, 1], F32))
            chm2 = s1.enter_context(nc.sbuf_tensor("chm2", [128, 1], F32))
            scc = s1.enter_context(nc.sbuf_tensor("scc", [128, 8], F32))
            gcol = s1.enter_context(nc.sbuf_tensor("gcol", [128, 4], F32))
            rt = s1.enter_context(nc.sbuf_tensor("rt", [128, D], F32))
            yt3 = s1.enter_context(nc.sbuf_tensor("yt3", [128, D], F32))
            psS1 = contextlib.ExitStack()
            ps_s = psS1.enter_context(nc.psum_tensor("ps_s", [128, 1024], F32))
            ps_e = psS1.enter_context(nc.psum_tensor("ps_e", [128, 128], F32))

            @block.sync
            def _(y):
                p.inc("dY", y.dma_start(out=tris_sb[:], in_=tris))
                p.inc("dY", y.dma_start(out=m0_sb[:], in_=m0))
                p.inc("dY", y.dma_start(out=flagc_sb[:], in_=flagc_i))
                p.inc("dY", y.dma_start(out=ema_sb[:], in_=ema8))
                p.inc("dY", y.dma_start(out=scc[0:1, 0:3], in_=sc3))

            @block.vector
            def _(v):
                p.w(v, "dY")
                v.memset(runm[:], -2.0)
                v.drain()
                v.tensor_scalar(mng2[:], tris_sb[:], -1.0, 2.0, ALU.add, ALU.mult)
                v.drain()
                p.inc("eV", v.nop())
            fence(p, block, "vsty")

            @block.tensor
            def _(t_):
                t_.wait_ge(p.sems["eT"], p.cnt["eT"])
                p.inc("eT", t_.matmul(ps_e[:, 0:3], ones_row[:], scc[0:1, 0:3],
                                      start=True, stop=True))

            @block.scalar
            def _(s):
                p.w(s, "eT")
                s.activation(scc[:, 3:4], ps_e[:, 0:1], AF.Exp)
                s.activation(scc[:, 4:5], ps_e[:, 1:2], AF.Sigmoid)
                s.activation(scc[:, 5:6], ps_e[:, 2:3], AF.Sigmoid)
                s.drain()
                s.activation(scr["sq"][0][:, 0:8], ema_sb[:], AF.Square,
                             accum_out=scr["vv"][0])
                s.drain()
                p.inc("eS", s.nop())

            @block.vector
            def _(v):
                p.w(v, "eS")
                v.tensor_scalar(scc[:, 6:7], scc[:, 4:5], -1.0, 1.0, ALU.mult, ALU.add)
                v.drain()
                p.inc("eV", v.nop())

            @block.tensor
            def _(t_):
                p.w(t_, "eS")
                t_.wait_ge(p.sems["eT"], p.cnt["eT"])
                p.inc("eT", t_.matmul(ps_e[0:1, 4:5], ones_col[:], scr["vv"][0],
                                      start=True, stop=True))

            @block.scalar
            def _(s):
                p.w(s, "eT")
                s.activation(scr["m"][0][0:1, :], ps_e[0:1, 4:5], AF.Ln)
                s.drain()
                s.activation(scr["m"][0][0:1, :], scr["m"][0][0:1, :], AF.Exp, scale=-0.5)
                s.drain()
                p.inc("eS", s.nop())

            @block.tensor
            def _(t_):
                p.w(t_, "eS")
                t_.wait_ge(p.sems["eT"], p.cnt["eT"])
                p.inc("eT", t_.matmul(ps_e[:, 5:6], ones_row[:], scr["m"][0][0:1, :],
                                      start=True, stop=True))

            @block.vector
            def _(v):
                p.w(v, "eT")
                v.tensor_scalar(scc[:, 7:8], ps_e[:, 5:6], 1e12, None, ALU.min)
                v.drain()
                v.tensor_scalar(eman[:], ema_sb[:], scc[:, 7:8], None, ALU.mult)
                v.drain()
                p.inc("eV", v.nop())
            fence(p, block)

            @block.gpsimd
            def _(g):
                p.w(g, "dCC")
                g.engine_nop()
                p.inc("eG", g.engine_nop())

            for ks in range(4):
                par = ks % 2

                @block.sync
                def _(y, ks=ks, par=par):
                    p.w(y, "eG")
                    if ks >= 2:
                        p.w(y, "eT")
                    p.inc("dY", y.dma_start(
                        out=kst[:, par * 4096:(par + 1) * 4096].rearrange("p (c n) -> p c n", c=8),
                        in_=cc_on_out.rearrange("(c p) n -> p c n", p=128)[:, 0:8, 512 * ks:512 * (ks + 1)]))

                for qt in range(NTILE):
                    @block.tensor
                    def _(t_, ks=ks, qt=qt, par=par):
                        p.w(t_, "eV")
                        p.w(t_, "dY")
                        for c in range(8):
                            if c == 7:
                                t_.wait_ge(p.sems["eT"], p.cnt["eT"])
                            ins = t_.matmul(ps_s[:, 0:512],
                                            onfm_sb[:, c * NT + 128 * qt:c * NT + 128 * (qt + 1)],
                                            kst[:, par * 4096 + 512 * c:par * 4096 + 512 * (c + 1)],
                                            start=(c == 0), stop=(c == 7))
                        p.inc("eT", ins)

                    @block.vector
                    def _(v, qt=qt):
                        p.w(v, "eT")
                        v.reduce_max(chm[:], ps_s[:, 0:512], AX.X)
                        v.drain()
                        v.tensor_scalar(chm[:], chm[:], flagc_sb[:, 0:1], None, ALU.mult)
                        v.drain()
                        v.scalar_tensor_tensor(chm2[:], flagc_sb[:, 1:2], -1e4, chm[:],
                                               ALU.mult, ALU.add)
                        v.drain()
                        v.tensor_max(runm[:, qt:qt + 1], runm[:, qt:qt + 1], chm2[:])
                        v.drain()
                        v.nop().then_inc(p.sems["eV"], 1)
                        p.cnt["eV"] += 1

            for ks in range(4):
                for qt in range(4 * ks, NTILE):
                    qbase = 128 * qt
                    dj = (qbase - 512 * ks) // 128 if (qbase - 512 * ks) < 512 else None
                    wfull = 512 if dj is None else 128 * dj

                    @block.tensor
                    def _(t_, ks=ks, qt=qt):
                        p.w(t_, "eV")
                        for c in range(8):
                            if c == 7:
                                t_.wait_ge(p.sems["eT"], p.cnt["eT"])
                            ins = t_.matmul(ps_s[:, 0:512],
                                            onfm_sb[:, c * NT + 128 * qt:c * NT + 128 * (qt + 1)],
                                            onfm_sb[:, c * NT + 512 * ks:c * NT + 512 * (ks + 1)],
                                            start=(c == 0), stop=(c == 7))
                        p.inc("eT", ins)

                    @block.vector
                    def _(v, qt=qt, dj=dj, wfull=wfull):
                        p.w(v, "eT")
                        if dj is not None:
                            v.tensor_mul(seldg[:], ps_s[:, 128 * dj:128 * (dj + 1)], tris_sb[:])
                            v.drain()
                            v.tensor_add(seldg[:], seldg[:], mng2[:])
                            v.drain()
                            v.reduce_max(chm[:], seldg[:], AX.X)
                            v.drain()
                            if wfull > 0:
                                v.reduce_max(chm2[:], ps_s[:, 0:wfull], AX.X)
                                v.drain()
                                v.tensor_max(chm[:], chm[:], chm2[:])
                                v.drain()
                        else:
                            v.reduce_max(chm[:], ps_s[:, 0:512], AX.X)
                            v.drain()
                        v.tensor_max(runm[:, qt:qt + 1], runm[:, qt:qt + 1], chm[:])
                        v.drain()
                        v.nop().then_inc(p.sems["eV"], 1)
                        p.cnt["eV"] += 1

            fence(p, block)

            for qt in range(NTILE):
                @block.tensor
                def _(t_, qt=qt):
                    p.w(t_, "eV")
                    for c in range(8):
                        if c == 7:
                            t_.wait_ge(p.sems["eT"], p.cnt["eT"])
                        ins = t_.matmul(ps_e[:, 8 + (qt % 4):9 + (qt % 4)],
                                        onfm_sb[:, c * NT + 128 * qt:c * NT + 128 * (qt + 1)],
                                        eman[:, c:c + 1], start=(c == 0), stop=(c == 7))
                    p.inc("eT", ins)

                @block.vector
                def _(v, qt=qt):
                    p.w(v, "eT")
                    v.tensor_copy(esim[:, qt:qt + 1], ps_e[:, 8 + (qt % 4):9 + (qt % 4)])
                    v.drain()
                    p.inc("eV", v.nop())
            fence(p, block)

            for qt in range(NTILE):
                @block.vector
                def _(v, qt=qt):
                    p.w(v, "eS")
                    sq_ = gcol[:, 0:1]
                    if qt == 0:
                        v.tensor_scalar(sq_, esim[:, 0:1], m0_sb[:, 0:1], None, ALU.mult)
                        v.drain()
                        v.scalar_tensor_tensor(sq_, runm[:, 0:1], m0_sb[:, 1:2], sq_,
                                               ALU.mult, ALU.add)
                        v.drain()
                    else:
                        v.tensor_copy(sq_, runm[:, qt:qt + 1])
                        v.drain()
                    v.tensor_sub(gcol[:, 1:2], sq_, esim[:, qt:qt + 1])
                    v.drain()
                    v.tensor_mul(gcol[:, 1:2], gcol[:, 1:2], scc[:, 5:6])
                    v.drain()
                    v.tensor_add(gcol[:, 1:2], gcol[:, 1:2], esim[:, qt:qt + 1])
                    v.drain()
                    v.tensor_mul(gcol[:, 2:3], gcol[:, 1:2], scc[:, 3:4])
                    v.drain()
                    p.inc("eV", v.nop())

                @block.scalar
                def _(s, qt=qt):
                    p.w(s, "eV")
                    s.activation(gcol[:, 3:4], gcol[:, 2:3], AF.Exp, scale=-1.0)
                    s.drain()
                    p.inc("eS", s.nop())

                @block.vector
                def _(v, qt=qt):
                    p.w(v, "eS")
                    v.tensor_mul(gcol[:, 3:4], gcol[:, 3:4], scc[:, 4:5])
                    v.drain()
                    v.tensor_add(gcol[:, 3:4], gcol[:, 3:4], scc[:, 6:7])
                    v.drain()
                    v.tensor_scalar(rt[:], outg_sb[:, qt * D:(qt + 1) * D], gcol[:, 3:4],
                                    None, ALU.mult)
                    v.drain()
                    p.inc("eV", v.nop())

                seq_ln(p, block, rt[:], yt3[:], scr, p.const(1e-5))

                @block.vector
                def _(v, qt=qt):
                    p.w(v, "eS")
                    xa = x_sb[:, qt * D:(qt + 1) * D]
                    v.tensor_add(xa, xa, yt3[:])
                    v.drain()
                    p.inc("eV", v.nop())

            fence(p, block)
            psS1.close()
            s1.close()
            ga.close()

            @block.gpsimd
            def _(g):
                p.w(g, "eV")
                p.inc("dG", g.dma_start(out=x3_dram.rearrange("(t p) d -> p t d", p=128),
                                        in_=x_sb[:].rearrange("p (t d) -> p t d", t=NTILE)))
            fence(p, block, "vg")

            s3 = contextlib.ExitStack()
            wqT = s3.enter_context(nc.sbuf_tensor("wqT", [128, 8 * D], BF16))
            wkT = s3.enter_context(nc.sbuf_tensor("wkT", [128, 8 * 256], F32))
            wvT = s3.enter_context(nc.sbuf_tensor("wvT", [128, 8 * 256], F32))
            wld = s3.enter_context(nc.sbuf_tensor("wld", [128, 1024], F32))
            wq_t = s3.enter_context(nc.sbuf_tensor("wq_t", [128, 1024], BF16))
            wq_t32 = s3.enter_context(nc.sbuf_tensor("wq_t32", [128, 1024], F32))
            q_scr = {k: s3.enter_context(nc.sbuf_tensor("q_" + k, [128, 1],
                      I32 if k == "i1" else F32))[:] for k in
                     ("mx", "mn", "rng", "isc", "base", "i1", "c1", "d1", "m1", "half1")}
            q_scr["it"] = s3.enter_context(nc.sbuf_tensor("q_it", [128, 128], I32))[:]
            q_scr["ct"] = s3.enter_context(nc.sbuf_tensor("q_ct", [128, 128], F32))[:]
            q_scr["dt"] = s3.enter_context(nc.sbuf_tensor("q_dt", [128, 128], F32))[:]
            q_scr["mt"] = s3.enter_context(nc.sbuf_tensor("q_mt", [128, 128], F32))[:]
            q_scr["halft"] = s3.enter_context(nc.sbuf_tensor("q_halft", [128, 128], F32))[:]
            q_scr["obuf"] = s3.enter_context(nc.sbuf_tensor("q_obuf", [128, 1024], BF16))[:]
            ps3 = contextlib.ExitStack()
            ps_q = ps3.enter_context(nc.psum_tensor("ps_q", [128, 512], BF16))
            ps_q32 = ps3.enter_context(nc.psum_tensor("ps_q32", [128, 512], F32))

            @block.vector
            def _(v):
                v.memset(q_scr["half1"], 0.5)
                v.memset(q_scr["halft"], 0.5)
                v.drain()
                p.inc("eV", v.nop())
            fence(p, block, "vy")

            quant_stream(p, block, wq_i, D, D, wqT, False, id_b[:], q_scr, wld[:], wq_t[:], ps_q)
            quant_stream(p, block, wk_i, 256, D, wkT, False, id_f[:], q_scr, wld[:],
                         wq_t32[:], ps_q32, f32=True)
            quant_stream(p, block, wv_i, 256, D, wvT, False, id_f[:], q_scr, wld[:],
                         wq_t32[:], ps_q32, f32=True)
            quant_stream(p, block, wo_i, D, D, wo_dram, True, id_b[:], q_scr, wld[:], wq_t[:], ps_q)
            MW1 = D * 3584 // 128
            quant_stream(p, block, w1e_i, 3584, D,
                         cc_mw_in.rearrange("(a b) c -> (a b c)").rearrange("(r c) -> r c", c=3584)[0:D, :] if False
                         else cc_mw_in.rearrange("(r c) x -> r (c x)", c=28)[0:D, :],
                         True, id_b[:], q_scr, wld[:], wq_t[:], ps_q)
            quant_stream(p, block, w3e_i, 3584, D,
                         cc_mw_in.rearrange("(r c) x -> r (c x)", c=28)[D:2 * D, :],
                         True, id_b[:], q_scr, wld[:], wq_t[:], ps_q)
            quant_stream(p, block, w2e_i, D, 3584,
                         cc_mw_in.rearrange("(r c) x -> r (c x)", c=8)[2 * 3584:2 * 3584 + 3584, :],
                         True, id_b[:], q_scr, wld[:], wq_t[:], ps_q)
            fence(p, block)
            ps3.close()

            s2 = contextlib.ExitStack()
            xnfm = s2.enter_context(nc.sbuf_tensor("xnfm", [128, 8 * NT], BF16))
            g1b = s2.enter_context(nc.sbuf_tensor("g1b", [128, D], F32))
            g1r_sb = s2.enter_context(nc.sbuf_tensor("g1r_sb", [1, D], F32))
            ln3 = s2.enter_context(nc.sbuf_tensor("ln3", [128, D], F32))
            xn32 = s2.enter_context(nc.sbuf_tensor("xn32", [128, D], F32))
            xnb = s2.enter_context(nc.sbuf_tensor("xnb", [128, D], BF16))
            xnfm32 = s2.enter_context(nc.sbuf_tensor("xnfm32", [128, D], F32))
            qstg = s2.enter_context(nc.sbuf_tensor("qstg", [128, 2 * D], BF16))
            kvstg = s2.enter_context(nc.sbuf_tensor("kvstg", [128, 2 * 512], BF16))
            invb = s2.enter_context(nc.sbuf_tensor("invb", [128, 128], F32))
            invr = s2.enter_context(nc.sbuf_tensor("invr", [1, 128], F32))
            pos_sb = s2.enter_context(nc.sbuf_tensor("pos_sb", [128, 16], F32))
            frq = s2.enter_context(nc.sbuf_tensor("frq", [128, 128], F32))
            frq2 = s2.enter_context(nc.sbuf_tensor("frq2", [128, 128], F32))
            zz = s2.enter_context(nc.sbuf_tensor("zz", [128, 128], F32))
            fli = s2.enter_context(nc.sbuf_tensor("fli", [128, 128], I32))
            flc = s2.enter_context(nc.sbuf_tensor("flc", [128, 128], F32))
            flm = s2.enter_context(nc.sbuf_tensor("flm", [128, 128], F32))
            tinyp = s2.enter_context(nc.sbuf_tensor("tinyp", [128, 128], F32))
            ptin = s2.enter_context(nc.sbuf_tensor("ptin", [128, 256], F32))
            cost = s2.enter_context(nc.sbuf_tensor("cost", [128, 128], F32))
            sint = s2.enter_context(nc.sbuf_tensor("sint", [128, 128], F32))
            qf = s2.enter_context(nc.sbuf_tensor("qf", [128, D], F32))
            rq = s2.enter_context(nc.sbuf_tensor("rq", [128, D], F32))
            qb = s2.enter_context(nc.sbuf_tensor("qb", [128, D], BF16))
            kvf = s2.enter_context(nc.sbuf_tensor("kvf", [128, 512], F32))
            kqf = s2.enter_context(nc.sbuf_tensor("kqf", [128, 256], F32))
            krt = s2.enter_context(nc.sbuf_tensor("krt", [128, 256], F32))
            psS2 = contextlib.ExitStack()
            ps_g = psS2.enter_context(nc.psum_tensor("ps_g", [128, 1024], F32))
            ps_kv = psS2.enter_context(nc.psum_tensor("ps_kv", [128, 512], F32))
            ps_t4 = psS2.enter_context(nc.psum_tensor("ps_t4", [128, 2048], BF16))
            ps_t32 = psS2.enter_context(nc.psum_tensor("ps_t32", [128, 512], F32))

            @block.sync
            def _(y):
                p.inc("dY", y.dma_start(out=g1r_sb[:], in_=g1r))
                p.inc("dY", y.dma_start(out=pos_sb[:], in_=poscol))

            @block.tensor
            def _(t_):
                p.w(t_, "dY")
                t_.wait_ge(p.sems["eT"], p.cnt["eT"])
                t_.matmul(ps_g[:, 0:512], ones_row[:], g1r_sb[0:1, 0:512],
                          start=True, stop=True)
                p.inc("eT", t_.matmul(ps_g[:, 512:1024], ones_row[:], g1r_sb[0:1, 512:1024],
                                      start=True, stop=True))

            @block.vector
            def _(v):
                p.w(v, "eT")
                v.tensor_copy(g1b[:], ps_g[:])
                v.memset(tinyp[:], 1e-30)
                v.drain()
                p.inc("eV", v.nop())

            @block.gpsimd
            def _(g):
                p.w(g, "dY")
                g.iota(invr[0:1, 0:64], [[1, 64]], base=0, channel_multiplier=0,
                       allow_small_or_imprecise_dtypes=True)
                g.drain()
                g.tensor_copy(invr[0:1, 64:128], invr[0:1, 0:64])
                g.drain()
                p.inc("eG", g.engine_nop())

            @block.scalar
            def _(s):
                p.w(s, "eG")
                s.activation(invr[:], invr[:], AF.Exp, scale=-float(np.log(1e6)) / 64.0)
                s.drain()
                p.inc("eS", s.nop())

            @block.tensor
            def _(t_):
                p.w(t_, "eS")
                p.w(t_, "eV")
                t_.wait_ge(p.sems["eT"], p.cnt["eT"])
                p.inc("eT", t_.matmul(ps_g[:, 0:128], ones_row[:], invr[:],
                                      start=True, stop=True))

            @block.vector
            def _(v):
                p.w(v, "eT")
                v.tensor_copy(invb[:], ps_g[:, 0:128])
                v.drain()
                p.inc("eV", v.nop())
            fence(p, block)

            for t in range(NTILE):
                seq_ln(p, block, x_sb[:, t * D:(t + 1) * D], ln3[:], scr, p.const(1e-5))
                seq_ln(p, block, ln3[:], xn32[:], scr, p.const(1e-5), sub_mean=False,
                       gain_bcast=g1b[:])

                @block.vector
                def _(v, t=t):
                    p.w(v, "eS")
                    v.tensor_copy(xnb[:], xn32[:])
                    v.drain()
                    p.inc("eV", v.nop())

                jobs = [(xnb[:, 128 * c:128 * (c + 1)],
                         xnfm[:, c * NT + 128 * t:c * NT + 128 * (t + 1)]) for c in range(8)]
                tp_sched(p, block, jobs, ps_t4, id_b[:])
                for c in range(8):
                    @block.tensor
                    def _(t_, c=c):
                        p.w(t_, "eV")
                        t_.wait_ge(p.sems["eT"], p.cnt["eT"])
                        p.inc("eT", t_.transpose(ps_t32[:, 0:128] if c % 2 == 0 else ps_t32[:, 128:256],
                                                 xn32[:, 128 * c:128 * (c + 1)], id_f[:]))

                    @block.vector
                    def _(v, c=c):
                        p.w(v, "eT")
                        v.tensor_copy(xnfm32[:, 128 * c:128 * (c + 1)],
                                      ps_t32[:, 0:128] if c % 2 == 0 else ps_t32[:, 128:256])
                        v.drain()
                        p.inc("eV", v.nop())

                @block.vector
                def _(v, t=t):
                    p.w(v, "eT")
                    v.tensor_scalar(frq[:], invb[:], pos_sb[:, t:t + 1], None, ALU.mult)
                    v.drain()
                    v.tensor_scalar(frq2[:], frq[:], 1.0, PI / 2.0, ALU.mult, ALU.add)
                    v.drain()
                    for k2, fr in enumerate((frq, frq2)):
                        v.tensor_scalar(zz[:], fr[:], 1.0 / TWO_PI, 0.5, ALU.mult, ALU.add)
                        v.drain()
                        v.tensor_copy(fli[:], zz[:])
                        v.drain()
                        v.tensor_copy(flc[:], fli[:])
                        v.drain()
                        v.tensor_sub(flm[:], flc[:], zz[:])
                        v.drain()
                        v.tensor_tensor(flm[:], flm[:], tinyp[:], ALU.is_ge)
                        v.drain()
                        v.tensor_sub(flc[:], flc[:], flm[:])
                        v.drain()
                        v.scalar_tensor_tensor(ptin[:, 128 * k2:128 * (k2 + 1)], flc[:],
                                               -TWO_PI, fr[:], ALU.mult, ALU.add)
                        v.drain()
                    p.inc("eV", v.nop())

                @block.scalar
                def _(s, t=t):
                    p.w(s, "eV")
                    s.activation(sint[:], ptin[:, 0:128], AF.Sin)
                    s.activation(cost[:], ptin[:, 128:256], AF.Sin)
                    s.drain()
                    p.inc("eS", s.nop())

                @block.tensor
                def _(t_, t=t):
                    p.w(t_, "eV")
                    for n in range(2):
                        for c in range(8):
                            t_.matmul(ps_g[:, 512 * n:512 * (n + 1)],
                                      xnfm[:, c * NT + 128 * t:c * NT + 128 * (t + 1)],
                                      wqT[:, c * D + 512 * n:c * D + 512 * n + 512],
                                      start=(c == 0), stop=(c == 7))
                    for c in range(8):
                        t_.matmul(ps_kv[:, 0:256],
                                  xnfm32[:, 128 * c:128 * (c + 1)],
                                  wkT[:, c * 256:(c + 1) * 256],
                                  start=(c == 0), stop=(c == 7))
                    for c in range(8):
                        if c == 7:
                            t_.wait_ge(p.sems["eT"], p.cnt["eT"])
                        ins = t_.matmul(ps_kv[:, 256:512],
                                        xnfm32[:, 128 * c:128 * (c + 1)],
                                        wvT[:, c * 256:(c + 1) * 256],
                                        start=(c == 0), stop=(c == 7))
                    p.inc("eT", ins)

                @block.vector
                def _(v, t=t):
                    p.w(v, "eT")
                    p.w(v, "eS")
                    if t >= 2:
                        p.w(v, "dG")
                    v.tensor_copy(qf[:], ps_g[:])
                    v.drain()
                    for h in range(8):
                        v.tensor_scalar(rq[:, 128 * h:128 * h + 64],
                                        qf[:, 128 * h + 64:128 * (h + 1)], -1.0, None, ALU.mult)
                        v.tensor_copy(rq[:, 128 * h + 64:128 * (h + 1)],
                                      qf[:, 128 * h:128 * h + 64])
                    v.drain()
                    for h in range(8):
                        v.tensor_mul(qf[:, 128 * h:128 * (h + 1)],
                                     qf[:, 128 * h:128 * (h + 1)], cost[:])
                        v.tensor_mul(rq[:, 128 * h:128 * (h + 1)],
                                     rq[:, 128 * h:128 * (h + 1)], sint[:])
                    v.drain()
                    v.tensor_add(qb[:], qf[:], rq[:])
                    v.drain()
                    v.tensor_copy(kvf[:], ps_kv[:])
                    v.drain()
                    emit_quant_tile(p, v, kvf[:, 0:128], kqf[:, 0:128], q_scr)
                    emit_quant_tile(p, v, kvf[:, 128:256], kqf[:, 128:256], q_scr)
                    par_t = t % 2
                    emit_quant_tile(p, v, kvf[:, 256:384],
                                    kvstg[:, 512 * par_t + 256:512 * par_t + 384], q_scr)
                    emit_quant_tile(p, v, kvf[:, 384:512],
                                    kvstg[:, 512 * par_t + 384:512 * par_t + 512], q_scr)
                    for kvh in range(2):
                        o = 128 * kvh
                        v.tensor_scalar(krt[:, o:o + 64], kqf[:, o + 64:o + 128],
                                        -1.0, None, ALU.mult)
                        v.tensor_copy(krt[:, o + 64:o + 128], kqf[:, o:o + 64])
                        v.drain()
                        v.tensor_mul(kqf[:, o:o + 128], kqf[:, o:o + 128], cost[:])
                        v.drain()
                        v.tensor_mul(krt[:, o:o + 128], krt[:, o:o + 128], sint[:])
                        v.drain()
                        v.tensor_add(kvstg[:, 512 * par_t + o:512 * par_t + o + 128],
                                     kqf[:, o:o + 128], krt[:, o:o + 128])
                        v.drain()
                    v.nop().then_inc(p.sems["eV"], 1)
                    p.cnt["eV"] += 1

                @block.gpsimd
                def _(g, t=t):
                    p.w(g, "eV")
                    p.inc("dG", g.dma_start(out=cc_kv_in[128 * t:128 * (t + 1), :],
                                            in_=kvstg[:, 512 * (t % 2):512 * (t % 2) + 512]))

                jobs = [(qb[:, 128 * h:128 * (h + 1)],
                         qstg[:, (t % 2) * D + 128 * h:(t % 2) * D + 128 * (h + 1)])
                        for h in range(8)]
                tp_sched(p, block, jobs, ps_t4, id_b[:])

                @block.gpsimd
                def _(g, t=t):
                    p.w(g, "eV")
                    p.inc("dG", g.dma_start(
                        out=qfm_dram.rearrange("(c p) n -> p c n", p=128)[:, :, 128 * t:128 * (t + 1)],
                        in_=qstg[:, (t % 2) * D:(t % 2) * D + D].rearrange("p (c n) -> p c n", c=8)))
            fence(p, block)
            psS2.close()
            s2.close()
            s3.close()

            @block.gpsimd
            def _(g):
                p.w(g, "dG")
                p.w(g, "dY")
            sx.close()
    return p.nc


def build_E2():
    p = P()
    nc = p.nc
    f = nc.dram_tensor
    x3_dram = f("x3", [NT, D], F32, kind="ExternalInput").ap()
    qfm_dram = f("qfm", [D, NT], BF16, kind="ExternalInput").ap()
    cc_kv_in = f("kvown", [NT, 512], BF16, kind="ExternalInput").ap()
    cc_kv_out = f("kvprev", [NT, 512], BF16, kind="ExternalInput").ap()
    wo_dram = f("woqt", [D, D], BF16, kind="ExternalInput").ap()
    g2r = f("g2r", [1, D], F32, kind="ExternalInput").ap()
    wgT_i = f("wgT", [D, 8], F32, kind="ExternalInput").ap()
    tri_inc = f("tri_inc", [128, 128], F32, kind="ExternalInput").ap()
    mneg_i = f("mneg", [128, 128], F32, kind="ExternalInput").ap()
    iden = f("iden", [128, 128], F32, kind="ExternalInput").ap()
    oneh_i = f("oneh", [8, 1024], F32, kind="ExternalInput").ap()
    MWR = (D * 3584 * 2 + 3584 * D) // 128
    cc_mw_out = f("mwall", [8 * MWR, 128], BF16, kind="ExternalInput").ap()
    flagr_i = f("flagr", [1, 512], F32, kind="ExternalInput").ap()
    flagc_i = f("flagc", [128, 2], F32, kind="ExternalInput").ap()
    out_o = f("out", [NT, D], F32, kind="ExternalOutput").ap()

    for s_ in ("dY", "dY2", "dG", "dCC", "eV", "eS", "eT", "eG", "eY"):
        p.sem(s_)

    id_f = p.sb("id_f", [128, 128], F32)
    id_b = p.sb("id_b", [128, 128], BF16)
    ones_row = p.sb("ones_row", [1, 128], F32)
    ones_col = p.sb("ones_col", [128, 1], F32)
    ones8 = p.sb("ones8", [128, 8], F32)
    _sq = p.sb("ssq0", [128, D], F32)[:]
    scr = {"sq": [_sq, _sq],
           "m": [p.sb("sm0", [128, 1], F32)[:], p.sb("sm1", [128, 1], F32)[:]],
           "vv": [p.sb("sv0", [128, 1], F32)[:], p.sb("sv1", [128, 1], F32)[:]],
           "r": [p.sb("sr0", [128, 1], F32)[:], p.sb("sr1", [128, 1], F32)[:]]}
    ISQ = 1.0 / float(np.sqrt(128.0))
    TWO_PI = 2.0 * PI

    with p.st:
        with nc.Block() as block:
            p.add_consts(block, [1e-5])

            @block.sync
            def _(y):
                p.inc("dY", y.dma_start(out=id_f[:], in_=iden))

            @block.vector
            def _(v):
                p.w(v, "dY")
                v.tensor_copy(id_b[:], id_f[:])
                v.memset(ones_col[:], 1.0)
                v.memset(ones_row[:], 1.0)
                v.memset(ones8[:], 1.0)
                v.drain()
                p.inc("eV", v.nop())

            @block.gpsimd
            def _(g):
                p.inc("dCC", g.engine_nop(), 16)
            fence(p, block, "vgy")

            ISQc = ISQ
            sH = contextlib.ExitStack()
            h_sb = sH.enter_context(nc.sbuf_tensor("h_sb", [128, NTILE * D], F32))
            e2 = contextlib.ExitStack()
            qt_sb = e2.enter_context(nc.sbuf_tensor("qt_sb", [128, 2 * D], BF16))
            kfm_r = e2.enter_context(nc.sbuf_tensor("kfm_r", [128, 2 * NT], BF16))
            kfm_o = e2.enter_context(nc.sbuf_tensor("kfm_o", [128, 2 * NT], BF16))
            v_r = e2.enter_context(nc.sbuf_tensor("v_r", [128, NTILE * 256], BF16))
            v_o = e2.enter_context(nc.sbuf_tensor("v_o", [128, NTILE * 256], BF16))
            wo_sb = e2.enter_context(nc.sbuf_tensor("wo_sb", [128, 8 * D], BF16))
            psb = e2.enter_context(nc.sbuf_tensor("psb", [128, 2048], BF16))
            rmx = e2.enter_context(nc.sbuf_tensor("rmx", [128, 4], F32))
            rsm = e2.enter_context(nc.sbuf_tensor("rsm", [128, 4], F32))
            seld = e2.enter_context(nc.sbuf_tensor("seld", [128, 128], F32))
            ofb = e2.enter_context(nc.sbuf_tensor("ofb", [128, 128], BF16))
            ofm = e2.enter_context(nc.sbuf_tensor("ofm", [128, 8 * 128], BF16))
            ptrc = e2.enter_context(nc.sbuf_tensor("ptrc", [128, 256], BF16))
            mneg = e2.enter_context(nc.sbuf_tensor("mneg_sb", [128, 128], F32))
            tri_fE = e2.enter_context(nc.sbuf_tensor("tri_fE", [128, 128], F32))
            tri_bE = e2.enter_context(nc.sbuf_tensor("tri_bE", [128, 128], BF16))
            flagr_sb = e2.enter_context(nc.sbuf_tensor("flagr_sb", [1, 512], F32))
            flagc_sb = e2.enter_context(nc.sbuf_tensor("flagc_sb", [128, 2], F32))
            at_sb = e2.enter_context(nc.sbuf_tensor("at_sb", [128, D], F32))
            ln3t = e2.enter_context(nc.sbuf_tensor("ln3t", [128, D], F32))
            x3st = e2.enter_context(nc.sbuf_tensor("x3st", [128, D], F32))
            psE2 = contextlib.ExitStack()
            ps_big = psE2.enter_context(nc.psum_tensor("ps_big", [128, 2048], F32))
            ps_ptr = psE2.enter_context(nc.psum_tensor("ps_ptr", [128, 2048], BF16))
            ps_o = psE2.enter_context(nc.psum_tensor("ps_o", [128, 384], F32))

            @block.sync
            def _(y):
                p.w(y, "eG")
                p.inc("dY", y.dma_start(out=tri_fE[:], in_=tri_inc))
                p.inc("dY", y.dma_start(out=mneg[:], in_=mneg_i))
                p.inc("dY", y.dma_start(out=flagr_sb[:], in_=flagr_i))
                p.inc("dY", y.dma_start(out=flagc_sb[:], in_=flagc_i))
                p.inc("dY", y.dma_start(out=v_r[:].rearrange("p (t n) -> p t n", t=NTILE),
                                        in_=cc_kv_out.rearrange("(t p) n -> p t n", p=128)[:, 0:NTILE, 256:512]))
                p.inc("dY", y.dma_start(out=v_o[:].rearrange("p (t n) -> p t n", t=NTILE),
                                        in_=cc_kv_in.rearrange("(t p) n -> p t n", p=128)[:, :, 256:512]))
                p.inc("dY", y.dma_start_transpose(kfm_r[:, 0:NT], cc_kv_out[0:NT, 0:128]))
                p.inc("dY", y.dma_start_transpose(kfm_r[:, NT:2 * NT], cc_kv_out[0:NT, 128:256]))
                p.inc("dY", y.dma_start_transpose(kfm_o[:, 0:NT], cc_kv_in[:, 0:128]))
                p.inc("dY", y.dma_start_transpose(kfm_o[:, NT:2 * NT], cc_kv_in[:, 128:256]))

            @block.gpsimd
            def _(g):
                p.inc("dG", g.dma_start(out=wo_sb[:].rearrange("p (c n) -> p c n", c=8),
                                        in_=wo_dram.rearrange("(c p) n -> p c n", p=128)))

            @block.vector
            def _(v):
                p.w(v, "dY")
                p.w(v, "dG")
                v.tensor_copy(tri_bE[:], tri_fE[:])
                v.drain()
                p.inc("eV", v.nop())
            fence(p, block)

            for qt in range(NTILE):
                par_q = qt % 2

                @block.sync
                def _(y, qt=qt, par_q=par_q):
                    p.w(y, "eS")
                    p.w(y, "eV")
                    p.w(y, "eT")
                    p.inc("dY", y.dma_start(out=x3st[:], in_=x3_dram[128 * qt:128 * (qt + 1), :]))
                    p.inc("dY", y.dma_start(
                        out=qt_sb[:, par_q * D:(par_q + 1) * D].rearrange("p (c n) -> p c n", c=8),
                        in_=qfm_dram.rearrange("(c p) n -> p c n", p=128)[:, :, 128 * qt:128 * (qt + 1)]))

                ow = 128 * (qt + 1)
                for h_ in range(8):
                    kvh = h_ // 4
                    lq = qt_sb[:, par_q * D + 128 * h_:par_q * D + 128 * (h_ + 1)]
                    segs = [("rect", 2048), ("own", ow)]
                    for pass2 in (False, True):
                        for si, (nm, w_) in enumerate(segs):
                            nmm = (w_ + 511) // 512
                            kbase = kfm_r if nm == "rect" else kfm_o

                            @block.tensor
                            def _(t_, lq=lq, w_=w_, kvh=kvh, nmm=nmm, nm=nm, kbase=kbase):
                                p.w(t_, "eV")
                                p.w(t_, "eS")
                                p.w(t_, "dY")
                                for j in range(nmm):
                                    ww = min(512, w_ - 512 * j)
                                    last = (j == nmm - 1)
                                    if last:
                                        t_.wait_ge(p.sems["eT"], p.cnt["eT"])
                                    ins = t_.matmul(ps_big[:, 512 * j:512 * j + ww], lq,
                                                    kbase[:, kvh * NT + 512 * j:kvh * NT + 512 * j + ww],
                                                    start=True, stop=True)
                                p.inc("eT", ins)

                            diag = (nm == "own")
                            if not pass2:
                                @block.vector
                                def _(v, si=si, w_=w_, diag=diag):
                                    p.w(v, "eT")
                                    if diag:
                                        v.tensor_mul(seld[:], ps_big[:, w_ - 128:w_], tri_fE[:])
                                        v.drain()
                                        v.tensor_add(seld[:], seld[:], mneg[:])
                                        v.drain()
                                        v.reduce_max(rmx[:, 1:2], seld[:], AX.X)
                                        v.drain()
                                        if w_ > 128:
                                            v.reduce_max(rmx[:, 2:3], ps_big[:, 0:w_ - 128], AX.X)
                                            v.drain()
                                            v.tensor_max(rmx[:, 1:2], rmx[:, 1:2], rmx[:, 2:3])
                                            v.drain()
                                    else:
                                        v.reduce_max(rmx[:, 1:2], ps_big[:, 0:w_], AX.X)
                                        v.drain()
                                        v.tensor_scalar(rmx[:, 1:2], rmx[:, 1:2],
                                                        flagc_sb[:, 0:1], None, ALU.mult)
                                        v.drain()
                                        v.scalar_tensor_tensor(rmx[:, 1:2], flagc_sb[:, 1:2],
                                                               -1e30, rmx[:, 1:2],
                                                               ALU.mult, ALU.add)
                                        v.drain()
                                    if si == 0:
                                        v.tensor_copy(rmx[:, 0:1], rmx[:, 1:2])
                                    else:
                                        v.tensor_max(rmx[:, 0:1], rmx[:, 0:1], rmx[:, 1:2])
                                    v.drain()
                                    v.nop().then_inc(p.sems["eV"], 1)
                                    p.cnt["eV"] += 1
                            else:
                                if si == 0:
                                    @block.vector
                                    def _(v):
                                        p.w(v, "eT")
                                        v.tensor_scalar(rmx[:, 3:4], rmx[:, 0:1], -ISQc,
                                                        None, ALU.mult)
                                        v.drain()
                                        v.nop().then_inc(p.sems["eV"], 1)
                                        p.cnt["eV"] += 1

                                @block.scalar
                                def _(s, w_=w_):
                                    p.w(s, "eV")
                                    p.w(s, "eT")
                                    s.activation(psb[:, 0:w_], ps_big[:, 0:w_], AF.Exp,
                                                 scale=ISQc, bias=rmx[:, 3:4],
                                                 accum_out=rsm[:, 1:2])
                                    s.drain()
                                    p.inc("eS", s.nop())

                                @block.vector
                                def _(v, si=si, w_=w_, diag=diag):
                                    p.w(v, "eS")
                                    if si == 0:
                                        v.tensor_scalar(psb[:, 0:w_], psb[:, 0:w_],
                                                        flagc_sb[:, 0:1], None, ALU.mult)
                                        v.drain()
                                        v.tensor_scalar(rsm[:, 1:2], rsm[:, 1:2],
                                                        flagc_sb[:, 0:1], None, ALU.mult)
                                        v.drain()
                                    if diag:
                                        v.reduce_sum(rsm[:, 2:3], psb[:, w_ - 128:w_], AX.X)
                                        v.drain()
                                        v.tensor_mul(psb[:, w_ - 128:w_], psb[:, w_ - 128:w_],
                                                     tri_bE[:])
                                        v.drain()
                                        v.reduce_sum(rsm[:, 3:4], psb[:, w_ - 128:w_], AX.X)
                                        v.drain()
                                        v.tensor_sub(rsm[:, 1:2], rsm[:, 1:2], rsm[:, 2:3])
                                        v.drain()
                                        v.tensor_add(rsm[:, 1:2], rsm[:, 1:2], rsm[:, 3:4])
                                        v.drain()
                                    if si == 0:
                                        v.tensor_copy(rsm[:, 0:1], rsm[:, 1:2])
                                    else:
                                        v.tensor_add(rsm[:, 0:1], rsm[:, 0:1], rsm[:, 1:2])
                                    v.drain()
                                    v.nop().then_inc(p.sems["eV"], 1)
                                    p.cnt["eV"] += 1

                                nch = w_ // 128
                                vsrc = v_r if nm == "rect" else v_o
                                for kc in range(nch):
                                    pslot = kc % 8
                                    first = (si == 0 and kc == 0)
                                    last = (si == 1 and kc == nch - 1)

                                    @block.tensor
                                    def _(t_, kc=kc, pslot=pslot):
                                        p.w(t_, "eV")
                                        t_.wait_ge(p.sems["eT"], p.cnt["eT"])
                                        p.inc("eT", t_.transpose(
                                            ps_ptr[:, 128 * pslot:128 * (pslot + 1)],
                                            psb[:, 128 * kc:128 * (kc + 1)], tri_bE[:] if False else id_b[:]))

                                    @block.vector
                                    def _(v, pslot=pslot):
                                        p.w(v, "eT")
                                        v.tensor_copy(ptrc[:, 128 * (pslot % 2):128 * (pslot % 2 + 1)],
                                                      ps_ptr[:, 128 * pslot:128 * (pslot + 1)])
                                        v.drain()
                                        v.nop().then_inc(p.sems["eV"], 1)
                                        p.cnt["eV"] += 1

                                    @block.tensor
                                    def _(t_, kc=kc, kvh=kvh, first=first, last=last,
                                          pslot=pslot, vsrc=vsrc):
                                        p.w(t_, "eV")
                                        if last:
                                            t_.wait_ge(p.sems["eT"], p.cnt["eT"])
                                        ins = t_.matmul(ps_o[:, 0:128],
                                                        ptrc[:, 128 * (pslot % 2):128 * (pslot % 2 + 1)],
                                                        vsrc[:, kc * 256 + 128 * kvh:kc * 256 + 128 * (kvh + 1)],
                                                        start=first, stop=last)
                                        if last:
                                            p.inc("eT", ins)

                    @block.vector
                    def _(v, h_=h_):
                        p.w(v, "eT")
                        v.reciprocal(rsm[:, 1:2], rsm[:, 0:1])
                        v.drain()
                        v.tensor_scalar(ofb[:], ps_o[:, 0:128], rsm[:, 1:2], None, ALU.mult)
                        v.drain()
                        v.nop().then_inc(p.sems["eV"], 1)
                        p.cnt["eV"] += 1

                    @block.tensor
                    def _(t_, h_=h_):
                        p.w(t_, "eV")
                        t_.wait_ge(p.sems["eT"], p.cnt["eT"])
                        p.inc("eT", t_.transpose(ps_ptr[:, 0:128], ofb[:], id_b[:]))

                    @block.vector
                    def _(v, h_=h_):
                        p.w(v, "eT")
                        v.tensor_copy(ofm[:, 128 * h_:128 * (h_ + 1)], ps_ptr[:, 0:128])
                        v.drain()
                        v.nop().then_inc(p.sems["eV"], 1)
                        p.cnt["eV"] += 1

                @block.tensor
                def _(t_, qt=qt):
                    p.w(t_, "eV")
                    for n in range(2):
                        for h2 in range(8):
                            if n == 1 and h2 == 7:
                                t_.wait_ge(p.sems["eT"], p.cnt["eT"])
                            ins = t_.matmul(ps_big[:, 512 * n:512 * (n + 1)],
                                            ofm[:, 128 * h2:128 * (h2 + 1)],
                                            wo_sb[:, h2 * D + 512 * n:h2 * D + 512 * n + 512],
                                            start=(h2 == 0), stop=(h2 == 7))
                    p.inc("eT", ins)

                @block.vector
                def _(v, qt=qt):
                    p.w(v, "eT")
                    v.tensor_copy(at_sb[:], ps_big[:, 0:1024])
                    v.drain()
                    p.inc("eV", v.nop())

                seq_ln(p, block, x3st[:], ln3t[:], scr, p.const(1e-5))

                @block.vector
                def _(v, qt=qt):
                    p.w(v, "eS")
                    v.tensor_add(h_sb[:, qt * D:(qt + 1) * D], ln3t[:], at_sb[:])
                    v.drain()
                    p.inc("eV", v.nop())
            fence(p, block)
            psE2.close()
            e2.close()

            e3 = contextlib.ExitStack()
            hnfm = e3.enter_context(nc.sbuf_tensor("hnfm", [128, 8 * NT], BF16))
            g2b = e3.enter_context(nc.sbuf_tensor("g2b", [128, D], F32))
            g2r_sb = e3.enter_context(nc.sbuf_tensor("g2r_sb", [1, D], F32))
            wg_sb = e3.enter_context(nc.sbuf_tensor("wg_sb", [128, 64], F32))
            wgb = e3.enter_context(nc.sbuf_tensor("wgb", [128, 64], BF16))
            hnb = e3.enter_context(nc.sbuf_tensor("hnb", [128, D], BF16))
            rwT = e3.enter_context(nc.sbuf_tensor("rwT", [8, NT], BF16))
            rwbb = e3.enter_context(nc.sbuf_tensor("rwbb", [128, 8], BF16))
            lg = e3.enter_context(nc.sbuf_tensor("lg", [128, 8], F32))
            exl = e3.enter_context(nc.sbuf_tensor("exl", [128, 8], F32))
            exm = e3.enter_context(nc.sbuf_tensor("exm", [128, 8], F32))
            mk2 = e3.enter_context(nc.sbuf_tensor("mk2", [128, 8], F32))
            tcol = e3.enter_context(nc.sbuf_tensor("tcol", [128, 6], F32))
            neg1 = e3.enter_context(nc.sbuf_tensor("neg1", [128, 8], F32))
            rw_sb = e3.enter_context(nc.sbuf_tensor("rw_sb", [128, 8], F32))
            moe_acc = e3.enter_context(nc.sbuf_tensor("moe_acc", [128, 8 * NT], BF16))
            psE3 = contextlib.ExitStack()
            ps_l = psE3.enter_context(nc.psum_tensor("ps_l", [128, 1024], F32))
            ps_t5 = psE3.enter_context(nc.psum_tensor("ps_t5", [128, 2048], BF16))

            @block.sync
            def _(y):
                p.inc("dY", y.dma_start(out=g2r_sb[:], in_=g2r))
                p.inc("dY", y.dma_start(out=wg_sb[:].rearrange("p (c n) -> p c n", c=8),
                                        in_=wgT_i.rearrange("(c p) n -> p c n", p=128)))

            @block.tensor
            def _(t_):
                p.w(t_, "dY")
                t_.wait_ge(p.sems["eT"], p.cnt["eT"])
                t_.matmul(ps_l[:, 0:512], ones_row[:], g2r_sb[0:1, 0:512],
                          start=True, stop=True)
                p.inc("eT", t_.matmul(ps_l[:, 512:1024], ones_row[:], g2r_sb[0:1, 512:1024],
                                      start=True, stop=True))

            @block.vector
            def _(v):
                p.w(v, "eT")
                p.w(v, "dY")
                v.tensor_copy(g2b[:], ps_l[:])
                v.tensor_copy(wgb[:], wg_sb[:])
                v.memset(neg1[:], -1.0)
                v.drain()
                p.inc("eV", v.nop())
            fence(p, block)

            for t in range(NTILE):
                seq_ln(p, block, h_sb[:, t * D:(t + 1) * D], hnb[:], scr, p.const(1e-5),
                       sub_mean=False, gain_bcast=g2b[:])
                jobs = [(hnb[:, 128 * c:128 * (c + 1)],
                         hnfm[:, c * NT + 128 * t:c * NT + 128 * (t + 1)]) for c in range(8)]
                tp_sched(p, block, jobs, ps_t5, id_b[:])

                @block.tensor
                def _(t_, t=t):
                    p.w(t_, "eV")
                    for c in range(8):
                        if c == 7:
                            t_.wait_ge(p.sems["eT"], p.cnt["eT"])
                        ins = t_.matmul(ps_l[:, 0:8],
                                        hnfm[:, c * NT + 128 * t:c * NT + 128 * (t + 1)],
                                        wgb[:, 8 * c:8 * (c + 1)],
                                        start=(c == 0), stop=(c == 7))
                    p.inc("eT", ins)

                @block.vector
                def _(v, t=t):
                    p.w(v, "eT")
                    v.tensor_copy(lg[:], ps_l[:, 0:8])
                    v.drain()
                    v.reduce_max(tcol[:, 0:1], lg[:], AX.X)
                    v.drain()
                    v.tensor_scalar(tcol[:, 1:2], tcol[:, 0:1], -1.0, None, ALU.mult)
                    v.drain()
                    p.inc("eV", v.nop())

                @block.scalar
                def _(s, t=t):
                    p.w(s, "eV")
                    s.activation(exl[:], lg[:], AF.Exp, bias=tcol[:, 1:2])
                    s.drain()
                    p.inc("eS", s.nop())

                @block.vector
                def _(v, t=t):
                    p.w(v, "eS")
                    v.reduce_max(tcol[:, 2:3], exl[:], AX.X)
                    v.drain()
                    v.tensor_scalar(exm[:], ones8[:], tcol[:, 2:3], None, ALU.mult)
                    v.drain()
                    v.tensor_tensor(mk2[:], exl[:], exm[:], ALU.is_ge)
                    v.drain()
                    v.tensor_scalar(exm[:], exl[:], 1.0, None, ALU.add)
                    v.drain()
                    v.tensor_mul(exm[:], exm[:], mk2[:])
                    v.drain()
                    v.tensor_sub(exm[:], exl[:], exm[:])
                    v.drain()
                    v.reduce_max(tcol[:, 3:4], exm[:], AX.X)
                    v.drain()
                    v.tensor_scalar(exm[:], ones8[:], tcol[:, 3:4], None, ALU.mult)
                    v.drain()
                    v.tensor_tensor(mk2[:], exl[:], exm[:], ALU.is_ge)
                    v.drain()
                    v.tensor_add(tcol[:, 4:5], tcol[:, 2:3], tcol[:, 3:4])
                    v.drain()
                    v.reciprocal(tcol[:, 5:6], tcol[:, 4:5])
                    v.drain()
                    v.tensor_mul(rw_sb[:], exl[:], mk2[:])
                    v.drain()
                    v.tensor_scalar(rwbb[:], rw_sb[:], tcol[:, 5:6], None, ALU.mult)
                    v.drain()
                    p.inc("eV", v.nop())

                @block.tensor
                def _(t_, t=t):
                    p.w(t_, "eV")
                    t_.wait_ge(p.sems["eT"], p.cnt["eT"])
                    p.inc("eT", t_.transpose(ps_t5[0:8, 0:128], rwbb[:], id_b[:]))

                @block.vector
                def _(v, t=t):
                    p.w(v, "eT")
                    v.tensor_copy(rwT[0:8, 128 * t:128 * (t + 1)], ps_t5[0:8, 0:128])
                    v.drain()
                    p.inc("eV", v.nop())
            fence(p, block)

            psE3.close()
            w1tv = cc_mw_out.rearrange("(r c) x -> r (c x)", c=28)
            w2tv = cc_mw_out.rearrange("(r c) x -> r (c x)", c=8)
            R1 = 3 * D
            R2 = 3 * 3584 // 1 if False else (MWR * 128) // 1024
            e4 = contextlib.ExitStack()
            up_sb = e4.enter_context(nc.sbuf_tensor("up_sb", [128, 28 * 512], BF16))
            pan1 = e4.enter_context(nc.sbuf_tensor("pan1", [128, 2 * 1024], BF16))
            pan3 = e4.enter_context(nc.sbuf_tensor("pan3", [128, 2 * 1024], BF16))
            pan2 = e4.enter_context(nc.sbuf_tensor("pan2", [128, 2 * 3584], BF16))
            rwb = e4.enter_context(nc.sbuf_tensor("rwb", [128, 512], BF16))
            oneh_sb = e4.enter_context(nc.sbuf_tensor("oneh_sb", [8, 1024], BF16))
            u1f = e4.enter_context(nc.sbuf_tensor("u1f", [128, 2 * 512], F32))
            psE4 = contextlib.ExitStack()
            ps_up = psE4.enter_context(nc.psum_tensor("ps_up", [128, 2048], F32))
            ps_m = psE4.enter_context(nc.psum_tensor("ps_m", [128, 1024], F32))

            @block.gpsimd
            def _(g):
                p.w(g, "dCC")
                p.inc("dG", g.dma_start(out=oneh_sb[:], in_=oneh_i))

            @block.vector
            def _(v):
                p.w(v, "dG")
                v.memset(moe_acc[:], 0.0)
                v.drain()
                p.inc("eV", v.nop())
            fence(p, block, "vtgy")

            for e in range(8):
                for mt in range(4):
                    tok0 = 512 * mt

                    @block.tensor
                    def _(t_, e=e, mt=mt, tok0=tok0):
                        p.w(t_, "eV")
                        t_.wait_ge(p.sems["eT"], p.cnt["eT"])
                        p.inc("eT", t_.matmul(ps_up[:, 512:1024],
                                              oneh_sb[0:8, 128 * e:128 * (e + 1)],
                                              rwT[0:8, tok0:tok0 + 512],
                                              start=True, stop=True))

                    @block.vector
                    def _(v, e=e, mt=mt):
                        p.w(v, "eV")
                        v.wait_ge(p.sems["eT"], p.cnt["eT"])
                        v.tensor_copy(rwb[:], ps_up[:, 512:1024])
                        v.drain()
                        v.nop().then_inc(p.sems["eV"], 1)
                        p.cnt["eV"] += 1

                    up_t = {}
                    up_v = {}
                    up_s = {}
                    ld_c = {}
                    for ffc in range(28):
                        par = ffc % 2
                        dsem = "dY" if par == 0 else "dY2"

                        @block.sync
                        def _(y, e=e, ffc=ffc, par=par, dsem=dsem):
                            if ffc >= 2:
                                p.w(y, "eT", up_t[ffc - 2])
                            p.inc(dsem, y.dma_start(
                                out=pan1[:, 1024 * par:1024 * (par + 1)].rearrange("p (c n) -> p c n", c=8),
                                in_=w1tv[3 * D * e + 0:3 * D * e + D, :].rearrange("(c p) n -> p c n", p=128)[:, :, 128 * ffc:128 * (ffc + 1)]))
                            p.inc(dsem, y.dma_start(
                                out=pan3[:, 1024 * par:1024 * (par + 1)].rearrange("p (c n) -> p c n", c=8),
                                in_=w1tv[3 * D * e + D:3 * D * e + 2 * D, :].rearrange("(c p) n -> p c n", p=128)[:, :, 128 * ffc:128 * (ffc + 1)]))
                            ld_c[ffc] = p.cnt[dsem]

                        base = 1024 * (ffc % 2)

                        @block.tensor
                        def _(t_, e=e, mt=mt, ffc=ffc, par=par, tok0=tok0, dsem=dsem,
                              base=base):
                            t_.wait_ge(p.sems[dsem], ld_c[ffc])
                            if ffc >= 2:
                                t_.wait_ge(p.sems["eS"], up_s[ffc - 2])
                                t_.wait_ge(p.sems["eV"], up_v[ffc - 2])
                            elif ffc == 0:
                                t_.wait_ge(p.sems["eV"], p.cnt["eV"])
                            for c in range(8):
                                t_.matmul(ps_up[:, base:base + 512],
                                          pan1[:, 1024 * par + 128 * c:1024 * par + 128 * (c + 1)],
                                          hnfm[:, c * NT + tok0:c * NT + tok0 + 512],
                                          start=(c == 0), stop=(c == 7))
                            for c in range(8):
                                if c == 7:
                                    t_.wait_ge(p.sems["eT"], p.cnt["eT"])
                                ins = t_.matmul(ps_up[:, base + 512:base + 1024],
                                                pan3[:, 1024 * par + 128 * c:1024 * par + 128 * (c + 1)],
                                                hnfm[:, c * NT + tok0:c * NT + tok0 + 512],
                                                start=(c == 0), stop=(c == 7))
                            p.inc("eT", ins)
                            up_t[ffc] = p.cnt["eT"]

                        @block.scalar
                        def _(s, ffc=ffc, base=base):
                            s.wait_ge(p.sems["eT"], up_t[ffc])
                            if ffc >= 2:
                                s.wait_ge(p.sems["eV"], up_v[ffc - 2])
                            s.activation(u1f[:, 512 * (ffc % 2):512 * (ffc % 2) + 512],
                                         ps_up[:, base:base + 512], AF.Silu)
                            s.drain()
                            s.nop().then_inc(p.sems["eS"], 1)
                            p.cnt["eS"] += 1
                            up_s[ffc] = p.cnt["eS"]

                        @block.vector
                        def _(v, ffc=ffc, base=base):
                            v.wait_ge(p.sems["eS"], up_s[ffc])
                            uf = u1f[:, 512 * (ffc % 2):512 * (ffc % 2) + 512]
                            v.tensor_mul(uf, uf, ps_up[:, base + 512:base + 1024])
                            v.drain()
                            v.tensor_mul(up_sb[:, 512 * ffc:512 * (ffc + 1)], uf, rwb[:])
                            v.drain()
                            v.nop().then_inc(p.sems["eV"], 1)
                            p.cnt["eV"] += 1
                            up_v[ffc] = p.cnt["eV"]

                    w2_t = {}
                    w2_v = {}
                    ld2_c = {}
                    for dc in range(8):
                        par = dc % 2
                        dsem = "dY" if par == 0 else "dY2"

                        @block.sync
                        def _(y, e=e, dc=dc, par=par, dsem=dsem):
                            if dc >= 2:
                                p.w(y, "eT", w2_t[dc - 2])
                            p.inc(dsem, y.dma_start(
                                out=pan2[:, 3584 * par:3584 * (par + 1)].rearrange("p (c n) -> p c n", c=28),
                                in_=w2tv[10752 * e + 7168:10752 * e + 10752, :].rearrange("(c p) n -> p c n", p=128)[:, :, 128 * dc:128 * (dc + 1)]))
                            ld2_c[dc] = p.cnt[dsem]

                        @block.tensor
                        def _(t_, e=e, mt=mt, dc=dc, tok0=tok0, par=par, dsem=dsem):
                            t_.wait_ge(p.sems[dsem], ld2_c[dc])
                            if dc == 0:
                                t_.wait_ge(p.sems["eV"], up_v[27])
                            if dc >= 2:
                                t_.wait_ge(p.sems["eV"], w2_v[dc - 2])
                            for ffc in range(28):
                                if ffc == 27:
                                    t_.wait_ge(p.sems["eT"], p.cnt["eT"])
                                ins = t_.matmul(ps_m[:, 512 * (dc % 2):512 * (dc % 2) + 512],
                                                pan2[:, 3584 * par + 128 * ffc:3584 * par + 128 * (ffc + 1)],
                                                up_sb[:, 512 * ffc:512 * (ffc + 1)],
                                                start=(ffc == 0), stop=(ffc == 27))
                            p.inc("eT", ins)
                            w2_t[dc] = p.cnt["eT"]

                        @block.vector
                        def _(v, e=e, dc=dc, tok0=tok0):
                            v.wait_ge(p.sems["eT"], w2_t[dc])
                            acc = moe_acc[:, dc * NT + tok0:dc * NT + tok0 + 512]
                            v.tensor_add(acc, acc, ps_m[:, 512 * (dc % 2):512 * (dc % 2) + 512])
                            v.drain()
                            v.nop().then_inc(p.sems["eV"], 1)
                            p.cnt["eV"] += 1
                            w2_v[dc] = p.cnt["eV"]
            fence(p, block)
            psE4.close()
            e4.close()

            fin = contextlib.ExitStack()
            moet = fin.enter_context(nc.sbuf_tensor("moet", [128, D], BF16))
            hmo = fin.enter_context(nc.sbuf_tensor("hmo", [128, D], F32))
            yt4 = fin.enter_context(nc.sbuf_tensor("yt4", [128, D], F32))
            outt = fin.enter_context(nc.sbuf_tensor("outt", [128, 2 * D], F32))
            x3f = fin.enter_context(nc.sbuf_tensor("x3f", [128, D], F32))
            psF = contextlib.ExitStack()
            ps_t7 = psF.enter_context(nc.psum_tensor("ps_t7", [128, 2048], BF16))

            for t in range(NTILE):
                @block.sync
                def _(y, t=t):
                    p.w(y, "eV")
                    p.w(y, "eS")
                    p.inc("dY", y.dma_start(out=x3f[:], in_=x3_dram[128 * t:128 * (t + 1), :]))

                jobs = [(moe_acc[:, c * NT + 128 * t:c * NT + 128 * (t + 1)],
                         moet[:, 128 * c:128 * (c + 1)]) for c in range(8)]
                tp_sched(p, block, jobs, ps_t7, id_b[:])

                @block.vector
                def _(v, t=t):
                    p.w(v, "eV")
                    v.tensor_add(hmo[:], h_sb[:, t * D:(t + 1) * D], moet[:])
                    v.drain()
                    p.inc("eV", v.nop())

                seq_ln(p, block, hmo[:], yt4[:], scr, p.const(1e-5))

                @block.vector
                def _(v, t=t):
                    p.w(v, "eS")
                    p.w(v, "dY")
                    if t >= 2:
                        p.w(v, "dG")
                    v.tensor_add(outt[:, (t % 2) * D:(t % 2) * D + D], x3f[:], yt4[:])
                    v.drain()
                    p.inc("eV", v.nop())

                @block.gpsimd
                def _(g, t=t):
                    p.w(g, "eV")
                    p.inc("dG", g.dma_start(out=out_o[128 * t:128 * (t + 1), :],
                                            in_=outt[:, (t % 2) * D:(t % 2) * D + D]))

            @block.gpsimd
            def _(g):
                p.w(g, "dG")
            psF.close()
            fin.close()
            e3.close()
            sH.close()
    return p.nc


def kernel(**inputs):
    in_maps = _full_inmaps(**inputs)
    f32 = np.float32
    X = np.asarray(inputs["L0_attn_query"], f32)

    ncA = build_A()
    inA = [{"xs": m["xs"], "wqk": m["wqk"], "wv": m["wvl0"], "iden": m["iden"]}
           for m in in_maps]
    rA = run_bass_kernel_spmd(ncA, inA, core_ids=list(range(8))).results

    ncB = build_B()
    zeros_tot = np.zeros((1, 2080), dtype=f32)
    inB = []
    for c in range(8):
        m = in_maps[c]
        b, h = c // 2, c % 2
        totp = np.asarray(rA[2 * b]["tot"], f32) if h == 1 else zeros_tot
        inB.append({"xs": m["xs"], "trig": np.asarray(rA[c]["trig"], f32),
                    "vtok": rA[c]["vtok"], "totp": totp, "tric": m["tric"],
                    "iden": m["iden"], "mo": m["mo"], "u1": m["u1"], "v1": m["v1"],
                    "b1": m["b1"], "u2": m["u2"], "v2": m["v2"], "b2": m["b2"]})
    rB = run_bass_kernel_spmd(ncB, inB, core_ids=list(range(8))).results

    ncC = build_C2()
    inC = []
    for c in range(8):
        m = in_maps[c]
        b = c // 2
        inC.append({"x2": np.asarray(rB[c]["x2"], f32), "outg": rB[c]["outg"],
                    "onfm": rB[c]["onfm"], "onprev": rB[2 * b]["onfm"],
                    "tris": m["tris"], "iden": m["iden"], "ema8": m["ema8"],
                    "sc3": m["sc3"], "m0": m["m0"], "poscol": m["poscol"],
                    "g1r": m["g1r"], "wq": m["wq"], "wk": m["wk"], "wv": m["wv"],
                    "wo": m["wo"], "w1e": m["w1e"], "w3e": m["w3e"], "w2e": m["w2e"],
                    "flagc": m["flagc"]})
    rC = run_bass_kernel_spmd(ncC, inC, core_ids=list(range(8))).results

    mwall = np.ascontiguousarray(np.concatenate(
        [np.asarray(rC[e]["mw"]) for e in range(8)], axis=0))
    ncE = build_E2()
    inE = []
    for c in range(8):
        m = in_maps[c]
        b = c // 2
        inE.append({"x3": np.asarray(rC[c]["x3"], f32), "qfm": rC[c]["qfm"],
                    "kvown": rC[c]["kv"], "kvprev": rC[2 * b]["kv"],
                    "woqt": rC[c]["woqt"], "g2r": m["g2r"], "wgT": m["wgT"],
                    "tri_inc": m["tri_inc"], "mneg": m["mneg"], "iden": m["iden"],
                    "oneh": m["oneh"], "mwall": mwall, "flagr": m["flagr"],
                    "flagc": m["flagc"]})
    rE = run_bass_kernel_spmd(ncE, inE, core_ids=list(range(8))).results

    out = np.empty((4, 4096, D), dtype=f32)
    for c in range(8):
        b, h = c // 2, c % 2
        out[b, h * NT:(h + 1) * NT] = np.asarray(rE[c]["out"], f32)
    return out
